# version 08
# baseline (speedup 1.0000x reference)
import math
from contextlib import ExitStack

import numpy as np
import ml_dtypes
import concourse.bass as bass
import concourse.mybir as mybir
import concourse.bass_utils as _BU
from concourse.bass_utils import run_bass_kernel_spmd

if not getattr(_BU, "_jobs_patched", False):
    _BU._jobs_patched = True
    _orig_run_command = _BU.run_command

    def _run_command_j2(argv, **kw):
        argv = list(argv)
        for i, a in enumerate(argv):
            if a == "--jobs" and i + 1 < len(argv):
                argv[i + 1] = "2"
        return _orig_run_command(argv, **kw)

    _BU.run_command = _run_command_j2

F32 = mybir.dt.float32
BF16 = mybir.dt.bfloat16
F16 = mybir.dt.float16
I32 = mybir.dt.int32
AF = mybir.ActivationFunctionType
ALU = mybir.AluOpType
AX = mybir.AxisListType

NCORES = 8
TPC = 1024
HID = 2048
HKC = HID // 128

BF = ml_dtypes.bfloat16


def km(w):
    K, M = w.shape
    assert K % 128 == 0, (K, M)
    return np.ascontiguousarray(
        w.reshape(K // 128, 128, M).transpose(1, 0, 2).reshape(128, -1))


def col16(v):
    return np.ascontiguousarray(v.reshape(-1, 128).T)


ENGINES = ("sync", "scalar", "vector", "tensor", "gpsimd")


class P:
    def __init__(self, nc):
        self.nc = nc
        self.items = []

    def op(self, eng, fn):
        self.items.append(("op", eng, fn))

    def dma(self, eng, fn):
        self.items.append(("dma", eng, fn))

    def bar(self):
        self.items.append(("bar",))

    def finalize(self, ctx):
        nc = self.nc
        BAR = ctx.enter_context(nc.semaphore("BAR"))
        DMAC = ctx.enter_context(nc.semaphore("DMAC"))
        block = ctx.enter_context(nc.Block())

        phases = []
        cur = []
        for it in self.items:
            if it[0] == "bar":
                phases.append(cur)
                cur = []
            else:
                cur.append(it)
        if cur:
            phases.append(cur)

        plan = {e: [] for e in ENGINES}
        cum_bar = 0
        cum_dma = 0
        for ph in phases:
            by_eng = {e: [] for e in ENGINES}
            for kind, eng, fn in ph:
                by_eng[eng].append((kind, fn))
            wait_bar = cum_bar
            wait_dma = cum_dma
            for e in ENGINES:
                ops = by_eng[e]
                if not ops:
                    continue
                last_compute = None
                for i, (kind, fn) in enumerate(ops):
                    if kind == "op":
                        last_compute = i
                first = True
                for i, (kind, fn) in enumerate(ops):
                    plan[e].append((
                        kind, fn,
                        wait_bar if first else None,
                        wait_dma if first else None,
                        (kind == "op" and i == last_compute),
                    ))
                    first = False
                if last_compute is not None:
                    cum_bar += 1
            cum_dma += sum(1 for kind, _, _ in ph if kind == "dma")

        total_dma = cum_dma
        total_bar = cum_bar

        def emit(e, seq):
            for kind, fn, wb, wd, incb in plan[e]:
                if wb:
                    seq.wait_ge(BAR, wb)
                if wd:
                    seq.wait_ge(DMAC, 16 * wd)
                inst = fn(seq)
                if kind == "dma":
                    inst.then_inc(DMAC, 16)
                elif incb:
                    inst.then_inc(BAR, 1)

        for e in ENGINES:
            if not plan[e]:
                continue
            deco = getattr(block, e)
            if e == "sync":
                @deco
                def _(seq, e=e):
                    emit(e, seq)
                    seq.wait_ge(BAR, total_bar)
                    seq.wait_ge(DMAC, 16 * total_dma)
            else:
                @deco
                def _(seq, e=e):
                    emit(e, seq)



_FUSE = None

INTERMEDIATES = {
    "mod0", "mod1", "ca0", "sa0", "ga0", "sb0", "cb0", "gb0",
    "ca1", "sa1", "ga1", "sb1", "cb1", "gb1",
    "gate", "xtok", "bfeat", "cfeat", "btok", "dtp", "adt",
    "yloc", "sout", "alast", "sdo", "alasto", "alastp", "sout_g",
    "ygf", "x1", "x2", "x3", "hres", "opart", "qg", "gtok", "soutr",
    "soutr_g", "ogf",
    "wgtc0", "wgtc1", "wgtc2", "wgtc3", "wdtc0", "wdtc1", "wdtc2", "wdtc3",
    "wgt_g0", "wgt_g1", "wgt_g2", "wgt_g3", "wdt_g0", "wdt_g1", "wdt_g2", "wdt_g3",
    "mod0_g", "mod1_g", "mrow_d", "alast_g",
}


class _NCProxy:
    def __init__(self, nc):
        self._nc = nc
        self.stage = ""
        self.rename = {}
        self.registry = {}

    def dram_tensor(self, name, shape, dtype, kind=None, **kw):
        name = self.rename.get(name, name)
        key = name
        if key in self.registry:
            return self.registry[key]
        if name in INTERMEDIATES:
            h = self._nc.dram_tensor(name, shape, dtype, **kw)
        elif name == "xout":
            h = self._nc.dram_tensor(name, shape, dtype, kind="ExternalOutput")
        else:
            h = self._nc.dram_tensor(name, shape, dtype, kind="ExternalInput")
        self.registry[key] = h
        return h

    def sbuf_tensor(self, name, *a, **kw):
        return self._nc.sbuf_tensor(self.stage + "_" + name, *a, **kw)

    def psum_tensor(self, name, *a, **kw):
        return self._nc.psum_tensor(self.stage + "_" + name, *a, **kw)

    def __getattr__(self, k):
        return getattr(self._nc, k)


class _Fuse:
    def __init__(self):
        self.real_nc = bass.Bass("TRN2", target_bir_lowering=False)
        self.nc_proxy = _NCProxy(self.real_nc)
        self.p = P(self.nc_proxy)


def _mknc():
    if _FUSE is not None:
        return _FUSE.nc_proxy
    return bass.Bass("TRN2", target_bir_lowering=False)


def _mkp(nc):
    if _FUSE is not None:
        return _FUSE.p
    return P(nc)


def _fin(p, ctx):
    if _FUSE is None:
        p.finalize(ctx)


def _cleanup(cms):
    for n in reversed(list(cms)):
        cms.pop(n).__exit__(None, None, None)



def build_st0():
    nc = _mknc()
    C0 = nc.dram_tensor("c0", [128, HKC * 2], F32, kind="ExternalInput").ap()
    C1 = nc.dram_tensor("c1", [128, HKC * 2], F32, kind="ExternalInput").ap()
    WM0 = nc.dram_tensor("wm0", [128, HKC * 1536], F32, kind="ExternalInput").ap()
    WM1 = nc.dram_tensor("wm1", [128, HKC * 1536], F32, kind="ExternalInput").ap()
    MSK = nc.dram_tensor("clampmask", [2, 1536], F32, kind="ExternalInput").ap()
    O0 = nc.dram_tensor("mod0", [2, 1536], F32, kind="ExternalOutput").ap()
    O1 = nc.dram_tensor("mod1", [2, 1536], F32, kind="ExternalOutput").ap()

    with ExitStack() as ctx:
        cms = {}
        sb = lambda n, s, d=F32: ctx.enter_context(nc.sbuf_tensor(n, s, d))
        ps = lambda n, s, d=F32: ctx.enter_context(nc.psum_tensor(n, s, d))
        c_sb = [sb(f"c{l}_sb", [128, HKC * 2]) for l in range(2)]
        c_bf = [sb(f"c{l}_bf", [128, HKC * 2]) for l in range(2)]
        wm_sb = sb("wm_sb", [128, HKC * 1536])
        msk_sb = sb("msk_sb", [2, 1536])
        m_sb = [sb(f"m{l}_sb", [2, 1536]) for l in range(2)]
        ta = sb("ta", [2, 1536]); tb = sb("tb", [2, 1536])
        ti = sb("ti", [2, 1536]); td = sb("td", [2, 1536])
        tii = sb("tii", [2, 1536], I32)
        PS = [ps(f"PS{l}", [2, 1536]) for l in range(2)]

        p = _mkp(nc)
        for l, C in enumerate([C0, C1]):
            p.dma("sync", lambda s, C=C, l=l: s.dma_start(out=c_sb[l][:], in_=C))
        p.dma("sync", lambda s: s.dma_start(out=msk_sb[:], in_=MSK))
        p.bar()
        for l in range(2):
            p.op("scalar", lambda s, l=l: s.activation(c_bf[l][:], c_sb[l][:], AF.Silu))
        p.bar()
        for l, WM in enumerate([WM0, WM1]):
            p.dma("gpsimd", lambda g, WM=WM: g.dma_start(out=wm_sb[:], in_=WM))
            p.bar()
            c3 = c_bf[l][:].rearrange("p (c m) -> p c m", m=2)
            w3 = wm_sb[:].rearrange("p (c m) -> p c m", m=1536)
            for f in range(3):
                for kc in range(HKC):
                    p.op("tensor", lambda t, l=l, f=f, kc=kc, c3=c3, w3=w3:
                         t.matmul(PS[l][:, f * 512:(f + 1) * 512],
                                  c3[:, kc, :], w3[:, kc, f * 512:(f + 1) * 512],
                                  start=(kc == 0), stop=(kc == HKC - 1)))
            p.bar()
        for l in range(2):
            v_ = lambda fn: p.op("vector", fn)
            v_(lambda v, l=l: v.tensor_scalar(ta[:], PS[l][:], 0.25, None, op0=ALU.max))
            v_(lambda v, l=l: v.tensor_scalar(tb[:], PS[l][:], -0.25, None, op0=ALU.min))
            v_(lambda v, l=l: v.tensor_scalar(tii[:], PS[l][:], 0.0, None, op0=ALU.is_ge))
            v_(lambda v: v.tensor_copy(ti[:], tii[:]))
            v_(lambda v: v.tensor_tensor(td[:], ta[:], tb[:], op=ALU.subtract))
            v_(lambda v: v.tensor_tensor(td[:], ti[:], td[:], op=ALU.mult))
            v_(lambda v: v.tensor_tensor(td[:], td[:], tb[:], op=ALU.add))
            v_(lambda v, l=l: v.tensor_tensor(td[:], td[:], PS[l][:], op=ALU.subtract))
            v_(lambda v: v.tensor_tensor(td[:], td[:], msk_sb[:], op=ALU.mult))
            v_(lambda v, l=l: v.tensor_tensor(m_sb[l][:], td[:], PS[l][:], op=ALU.add))
            p.bar()
        p.dma("sync", lambda s: s.dma_start(out=O0, in_=m_sb[0][:]))
        p.dma("sync", lambda s: s.dma_start(out=O1, in_=m_sb[1][:]))
        _cleanup(cms)
        _fin(p, ctx)
    return nc


def stage_st0(inputs):
    maps = []
    msk = np.zeros((2, 12288), np.float32)
    msk[:, 2 * 2048:3 * 2048] = 1.0
    msk[:, 5 * 2048:6 * 2048] = 1.0
    for core in range(NCORES):
        c = core
        sl = slice(1536 * c, 1536 * (c + 1))
        maps.append(dict(
            c0=km(np.asarray(inputs["L0_c"]).T.copy()),
            c1=km(np.asarray(inputs["L1_c"]).T.copy()),
            wm0=km(np.asarray(inputs["L0_wmod"])[:, sl]),
            wm1=km(np.asarray(inputs["L1_wmod"])[:, sl]),
            clampmask=np.ascontiguousarray(msk[:, sl]),
        ))
    return maps


def post_st0(results):
    out = {}
    for l in range(2):
        full = np.concatenate([np.asarray(r[f"mod{l}"]) for r in results], axis=1)
        assert full.shape == (2, 12288)
        per_b = []
        for b in range(2):
            six = full[b].reshape(6, 2048)
            per_b.append({k: col16(six[i]) for i, k in
                          enumerate(["sa", "ca", "ga", "sb", "cb", "gb"])})
        out[l] = per_b
    return out



def emit_ln_mod(p, sb, ps, nc, X3, OUT3, ca1_col, sa_col,
                ones_col_bf, ones_col_f, ones_row_f, KC, T, eps_ap=None, tag=""):
    D = KC * 128
    fts = [(f0, min(T, f0 + 512)) for f0 in range(0, T, 512)]
    sq = sb(f"ln_sq{tag}", [128, KC * T], BF16)
    sq3 = sq[:].rearrange("p (c t) -> p c t", t=T)
    SS = ps(f"ln_ss{tag}", [1, 1024])
    rows = sb(f"ln_rows{tag}", [1, 3 * T])
    rb_ps = ps(f"ln_rb{tag}", [128, 1024])
    RB = sb(f"ln_RB{tag}", [128, T])
    MRB = sb(f"ln_MRB{tag}", [128, T])
    t1 = sb(f"ln_t1{tag}", [128, T])
    m2 = sb(f"ln_m2{tag}", [128, T])

    p.op("scalar", lambda s: s.activation(sq[:], X3.rearrange("p c t -> p (c t)"),
                                          AF.Square))
    p.bar()
    for f0, f1 in fts:
        n = f1 - f0
        for kc in range(KC):
            p.op("tensor", lambda t, kc=kc, f0=f0, f1=f1, n=n:
                 t.matmul(SS[:, 0:n], ones_col_f[:, 0:1], X3[:, kc, f0:f1],
                          start=(kc == 0), stop=(kc == KC - 1)))
        for kc in range(KC):
            p.op("tensor", lambda t, kc=kc, f0=f0, f1=f1, n=n:
                 t.matmul(SS[:, 512:512 + n], ones_col_bf[:, 0:1], sq3[:, kc, f0:f1],
                          start=(kc == 0), stop=(kc == KC - 1)))
        p.bar()
        mr = slice(2 * T + f0, 2 * T + f1)
        r0 = slice(f0, f1)
        r1 = slice(T + f0, T + f1)
        p.op("vector", lambda v, n=n, mr=mr: v.tensor_scalar(
            rows[:, mr], SS[:, 0:n], 1.0 / D, None, op0=ALU.mult))
        p.bar()
        p.op("scalar", lambda s, mr=mr, r0=r0: s.activation(rows[:, r0], rows[:, mr], AF.Square))
        p.bar()
        p.op("vector", lambda v, n=n, r0=r0, r1=r1: v.scalar_tensor_tensor(
            rows[:, r1], SS[:, 512:512 + n], 1.0 / D, rows[:, r0],
            op0=ALU.mult, op1=ALU.subtract))
        p.bar()
        p.op("scalar", lambda s, r0=r0, r1=r1: s.activation(rows[:, r0], rows[:, r1],
                                                            AF.Ln, bias=eps_ap))
        p.bar()
        p.op("scalar", lambda s, r0=r0: s.activation(rows[:, r0], rows[:, r0],
                                                     AF.Exp, scale=-0.5))
        p.bar()
        p.op("vector", lambda v, mr=mr, r0=r0, r1=r1: v.tensor_tensor(
            rows[:, r1], rows[:, mr], rows[:, r0], op=ALU.mult))
        p.bar()
        p.op("tensor", lambda t, n=n, r0=r0: t.matmul(
            rb_ps[:, 0:n], ones_row_f[0:1, :], rows[0:1, r0], start=True, stop=True))
        p.op("tensor", lambda t, n=n, r1=r1: t.matmul(
            rb_ps[:, 512:512 + n], ones_row_f[0:1, :], rows[0:1, r1], start=True, stop=True))
        p.bar()
        p.op("vector", lambda v, f0=f0, f1=f1, n=n: v.tensor_copy(RB[:, f0:f1], rb_ps[:, 0:n]))
        p.op("vector", lambda v, f0=f0, f1=f1, n=n: v.tensor_copy(MRB[:, f0:f1], rb_ps[:, 512:512 + n]))
        p.bar()
    for kc in range(KC):
        p.op("vector", lambda v, kc=kc: v.scalar_tensor_tensor(
            t1[:], X3[:, kc, :], ca1_col[:, kc:kc + 1], RB[:], op0=ALU.mult, op1=ALU.mult))
        p.op("vector", lambda v, kc=kc: v.tensor_scalar(
            m2[:], MRB[:], ca1_col[:, kc:kc + 1], None, op0=ALU.mult))
        p.op("vector", lambda v: v.tensor_tensor(t1[:], t1[:], m2[:], op=ALU.subtract))
        p.op("vector", lambda v, kc=kc: v.tensor_scalar(
            OUT3[:, kc, :], t1[:], sa_col[:, kc:kc + 1], None, op0=ALU.add))


TH = 1028
NXBC = 34


def build_st1():
    nc = _mknc()
    X = nc.dram_tensor("xh", [128, HKC * TH], F32, kind="ExternalInput").ap()
    W = nc.dram_tensor("w_in", [128, HKC * 8512], F32, kind="ExternalInput").ap()
    CA = nc.dram_tensor("ca0", [128, HKC], F32, kind="ExternalInput").ap()
    SA = nc.dram_tensor("sa0", [128, HKC], F32, kind="ExternalInput").ap()
    CW = nc.dram_tensor("convw", [128, NXBC * 4], F32, kind="ExternalInput").ap()
    CB = nc.dram_tensor("convb", [128, NXBC], F32, kind="ExternalInput").ap()
    DTB = nc.dram_tensor("dtb", [1, 64], F32, kind="ExternalInput").ap()
    ALOG = nc.dram_tensor("alog", [1, 64], F32, kind="ExternalInput").ap()
    HALO = nc.dram_tensor("halo", [128, 1], F32, kind="ExternalInput").ap()
    IDB = nc.dram_tensor("identbf", [128, 128], BF16, kind="ExternalInput").ap()
    OGATE = nc.dram_tensor("gate", [128, 8 * 4096], BF16, kind="ExternalOutput").ap()
    OXTOK = nc.dram_tensor("xtok", [128, 8 * 4096], BF16, kind="ExternalOutput").ap()
    OBF = nc.dram_tensor("bfeat", [128, 1024], BF16, kind="ExternalOutput").ap()
    OCF = nc.dram_tensor("cfeat", [128, 1024], BF16, kind="ExternalOutput").ap()
    OBT = nc.dram_tensor("btok", [128, 8 * 128], BF16, kind="ExternalOutput").ap()
    ODTP = nc.dram_tensor("dtp", [128, 8 * 64], F32, kind="ExternalOutput").ap()
    OADT = nc.dram_tensor("adt", [128, 8 * 64], F32, kind="ExternalOutput").ap()

    W3 = W.rearrange("p (c m) -> p c m", m=8512)
    X3v = X.rearrange("p (c t) -> p c t", t=TH)

    with ExitStack() as ctx:
        cms = {}
        def sb(n, s, d=F32, side=None):
            cm = nc.sbuf_tensor(n, s, d, side=side)
            h = cm.__enter__()
            cms[n] = cm
            return h
        def sbr(n, s, d=F32):
            return sb(n, s, d, side="right")
        def ps(n, s, d=F32):
            cm = nc.psum_tensor(n, s, d)
            h = cm.__enter__()
            cms[n] = cm
            return h
        def free(*names):
            for n in names:
                cms.pop(n).__exit__(None, None, None)

        ocb = sb("ocb", [128, 1], BF16)
        ocf = sb("ocf", [128, 1]); orf = sb("orf", [1, 128])
        epsc = sb("epsc", [1, 1])
        ca_c = sb("ca_c", [128, HKC]); sa_c = sb("sa_c", [128, HKC])
        cw_sb = sb("cw_sb", [128, NXBC * 4]); cb_sb = sb("cb_sb", [128, NXBC])
        dtb_sb = sb("dtb_sb", [1, 64]); alog_sb = sb("alog_sb", [1, 64])
        halo_sb = sb("halo_sb", [128, 1]); idb = sb("idb", [128, 128], BF16)
        x_sb = sbr("x_sb", [128, HKC * TH])
        h0 = sb("h0", [128, HKC * TH], BF16)
        h03 = h0[:].rearrange("p (c t) -> p c t", t=TH)
        x3 = x_sb[:].rearrange("p (c t) -> p c t", t=TH)

        p = _mkp(nc)
        p.dma("sync", lambda s: s.dma_start(out=x_sb[:], in_=X))
        p.dma("sync", lambda s: s.dma_start(out=ca_c[:], in_=CA))
        p.dma("sync", lambda s: s.dma_start(out=sa_c[:], in_=SA))
        p.dma("sync", lambda s: s.dma_start(out=cw_sb[:], in_=CW))
        p.dma("sync", lambda s: s.dma_start(out=cb_sb[:], in_=CB))
        p.dma("sync", lambda s: s.dma_start(out=dtb_sb[:], in_=DTB))
        p.dma("sync", lambda s: s.dma_start(out=alog_sb[:], in_=ALOG))
        p.dma("sync", lambda s: s.dma_start(out=halo_sb[:], in_=HALO))
        p.dma("sync", lambda s: s.dma_start(out=idb[:], in_=IDB))
        p.op("gpsimd", lambda g: g.memset(ocb[:], 1.0))
        p.op("gpsimd", lambda g: g.memset(ocf[:], 1.0))
        p.op("gpsimd", lambda g: g.memset(orf[:], 1.0))
        p.op("gpsimd", lambda g: g.memset(epsc[:], 1e-5))
        p.bar()
        p.op("vector", lambda v: v.tensor_scalar(ca_c[:], ca_c[:], 1.0, None, op0=ALU.add))
        p.bar()
        emit_ln_mod(p, sbr, ps, nc, x3, h03, ca_c, sa_c, ocb, ocf, orf, HKC, TH, eps_ap=epsc[0:1, 0:1])
        p.bar()
        free("ln_m2", "ln_t1", "ln_MRB", "ln_RB", "ln_rows", "ln_sq", "x_sb",
             "ln_rb", "ln_ss")

        PS8 = [ps(f"PS{i}", [128, 512]) for i in range(8)]
        wslab = sbr("wslab", [128, HKC * 512], BF16)
        ws3 = wslab[:].rearrange("p (c m) -> p c m", m=512)
        gslab = sbr("gslab", [128, 8 * 512], BF16)
        g3 = gslab[:].rearrange("p (a b) -> p a b", b=512)
        OG3 = OGATE.rearrange("p (a b) -> p a b", b=4096)

        for fs in range(8):
            p.dma("gpsimd", lambda g, fs=fs: g.dma_start(
                out=wslab[:], in_=W3[:, :, fs * 512:(fs + 1) * 512]))
            p.bar()
            for tt in range(8):
                for kc in range(HKC):
                    p.op("tensor", lambda t, tt=tt, kc=kc:
                         t.matmul(PS8[tt][:], h03[:, kc, 4 + 128 * tt:4 + 128 * (tt + 1)],
                                  ws3[:, kc, :], start=(kc == 0), stop=(kc == HKC - 1)))
            p.bar()
            for tt in range(8):
                p.op("vector", lambda v, tt=tt: v.tensor_copy(g3[:, tt, :], PS8[tt][:]))
            p.bar()
            p.dma("sync", lambda s, fs=fs: s.dma_start(
                out=OG3[:, :, fs * 512:(fs + 1) * 512], in_=g3[:, :, :]))
        xbc = sb("xbc", [128, NXBC * TH], BF16)
        xb3 = xbc[:].rearrange("p (c t) -> p c t", t=TH)
        for ms in range(17):
            p.dma("gpsimd", lambda g, ms=ms: g.dma_start(
                out=wslab[:, 0:HKC * 256], in_=W3[:, :, 4096 + ms * 256:4096 + (ms + 1) * 256]))
            p.bar()
            ws2 = wslab[:].rearrange("p (c m) -> p c m", m=256)
            for mc in range(2):
                fc = ms * 2 + mc
                for fi, (f0, f1) in enumerate([(0, 512), (512, 1024), (1024, 1028)]):
                    pnum = mc * 3 + fi
                    for kc in range(HKC):
                        p.op("tensor", lambda t, mc=mc, f0=f0, f1=f1, kc=kc, pnum=pnum, ws2=ws2:
                             t.matmul(PS8[pnum][:, 0:f1 - f0],
                                      ws2[:, kc, mc * 128:(mc + 1) * 128],
                                      h03[:, kc, f0:f1],
                                      start=(kc == 0), stop=(kc == HKC - 1)))
            p.bar()
            for mc in range(2):
                fc = ms * 2 + mc
                for fi, (f0, f1) in enumerate([(0, 512), (512, 1024), (1024, 1028)]):
                    p.op("vector", lambda v, fc=fc, f0=f0, f1=f1, fi=fi, mc=mc:
                         v.tensor_copy(xb3[:, fc, f0:f1], PS8[mc * 3 + fi][:, 0:f1 - f0]))
            p.bar()
        p.dma("gpsimd", lambda g: g.dma_start(
            out=wslab[:, 0:HKC * 64], in_=W3[:, :, 8448:8512]))
        p.bar()
        wsd = wslab[:].rearrange("p (c m) -> p c m", m=64)
        for tt in range(8):
            for kc in range(HKC):
                p.op("tensor", lambda t, tt=tt, kc=kc:
                     t.matmul(PS8[tt][:, 0:64], h03[:, kc, 4 + 128 * tt:4 + 128 * (tt + 1)],
                              wsd[:, kc, :], start=(kc == 0), stop=(kc == HKC - 1)))
        p.bar()
        dtraw = sb("dtraw", [128, 8 * 64])
        dr3 = dtraw[:].rearrange("p (a b) -> p a b", b=64)
        for tt in range(8):
            p.op("vector", lambda v, tt=tt: v.tensor_copy(dr3[:, tt, :], PS8[tt][:, 0:64]))
        p.bar()
        free("gslab", "wslab")

        for c in range(NXBC):
            p.op("vector", lambda v, c=c: v.tensor_scalar(
                xb3[:, c, 0:4], xb3[:, c, 0:4], halo_sb[:, 0:1], None, op0=ALU.mult))
        p.bar()
        acc = [sb(f"acc{i}", [128, 1024]) for i in range(2)]
        for c in range(NXBC):
            a = acc[c % 2][:]
            p.op("vector", lambda v, c=c, a=a: v.tensor_scalar(
                a, xb3[:, c, 1:1025], cw_sb[:, 4 * c:4 * c + 1], None, op0=ALU.mult))
            for k in range(1, 4):
                p.op("vector", lambda v, c=c, k=k, a=a: v.scalar_tensor_tensor(
                    a, xb3[:, c, 1 + k:1025 + k], cw_sb[:, 4 * c + k:4 * c + k + 1], a,
                    op0=ALU.mult, op1=ALU.add))
            p.bar()
            p.op("scalar", lambda s, c=c, a=a: s.activation(
                xb3[:, c, 0:1024], a, AF.Silu, bias=cb_sb[:, c:c + 1]))
            if c % 2 == 1 or c == NXBC - 1:
                p.bar()
        p.bar()
        p.dma("sync", lambda s: s.dma_start(out=OBF, in_=xb3[:, 32, 0:1024]))
        p.dma("sync", lambda s: s.dma_start(out=OCF, in_=xb3[:, 33, 0:1024]))

        dscr = sb("dscr", [128, 8 * 64]); dscr2 = sb("dscr2", [128, 8 * 64])
        arow = sb("arow", [1, 64])
        PB = PS8[1]
        p.op("scalar", lambda s: s.activation(arow[:], alog_sb[:], AF.Exp))
        p.bar()
        p.op("vector", lambda v: v.tensor_scalar(arow[:], arow[:], -1.0, None, op0=ALU.mult))
        p.bar()
        p.op("tensor", lambda t: t.matmul(PB[:, 0:64], orf[0:1, :], dtb_sb[0:1, :],
                                          start=True, stop=True))
        p.op("tensor", lambda t: t.matmul(PB[:, 64:128], orf[0:1, :], arow[0:1, :],
                                          start=True, stop=True))
        p.bar()
        dtbb = sb("dtbb", [128, 64]); ab = sb("ab", [128, 64])
        p.op("vector", lambda v: v.tensor_copy(dtbb[:], PB[:, 0:64]))
        p.op("vector", lambda v: v.tensor_copy(ab[:], PB[:, 64:128]))
        p.bar()
        d3 = dscr[:].rearrange("p (a b) -> p a b", b=64)
        for tt in range(8):
            p.op("vector", lambda v, tt=tt: v.tensor_tensor(
                d3[:, tt, :], dr3[:, tt, :], dtbb[:], op=ALU.add))
        p.bar()
        p.op("scalar", lambda s: s.activation(dscr2[:], dscr[:], AF.Abs))
        p.bar()
        p.op("scalar", lambda s: s.activation(dscr2[:], dscr2[:], AF.Exp, scale=-1.0))
        p.bar()
        p.op("scalar", lambda s: s.activation(dscr2[:], dscr2[:], AF.Ln, bias=ocf[:, 0:1]))
        p.bar()
        dtp_sb = sb("dtp_sb", [128, 8 * 64]); adt_sb = sb("adt_sb", [128, 8 * 64])
        p.op("vector", lambda v: v.tensor_scalar(dscr[:], dscr[:], 0.0, None, op0=ALU.max))
        p.bar()
        p.op("vector", lambda v: v.tensor_tensor(dtp_sb[:], dscr[:], dscr2[:], op=ALU.add))
        p.bar()
        a3 = adt_sb[:].rearrange("p (a b) -> p a b", b=64)
        dt3 = dtp_sb[:].rearrange("p (a b) -> p a b", b=64)
        for tt in range(8):
            p.op("vector", lambda v, tt=tt: v.tensor_tensor(
                a3[:, tt, :], dt3[:, tt, :], ab[:], op=ALU.mult))
        p.bar()
        p.dma("sync", lambda s: s.dma_start(out=ODTP, in_=dtp_sb[:]))
        p.dma("sync", lambda s: s.dma_start(out=OADT, in_=adt_sb[:]))

        PT = PS8[0][:].bitcast(BF16)
        xslab = sb("xslab", [128, 8 * 128], BF16)
        xs3 = xslab[:].rearrange("p (a b) -> p a b", b=128)
        OX3 = OXTOK.rearrange("p (a b) -> p a b", b=4096)
        OBT3 = OBT.rearrange("p (a b) -> p a b", b=128)
        for fc in range(33):
            for tt in range(8):
                p.op("tensor", lambda t, fc=fc, tt=tt: t.transpose(
                    PT[:, tt * 128:(tt + 1) * 128],
                    xb3[:, fc, 128 * tt:128 * (tt + 1)], idb[:]))
            p.bar()
            p.op("vector", lambda v: v.tensor_copy(xslab[:], PT[:]))
            p.bar()
            if fc < 32:
                p.dma("sync", lambda s, fc=fc: s.dma_start(
                    out=OX3[:, :, fc * 128:(fc + 1) * 128], in_=xs3[:, :, :]))
            else:
                p.dma("sync", lambda s: s.dma_start(out=OBT3[:, :, :], in_=xs3[:, :, :]))
            p.bar()
        _cleanup(cms)
        _fin(p, ctx)
    return nc


def stage_st1(inputs, mods):
    modget = (lambda l, b, k: mods[l][b][k]) if mods is not None else (
        lambda l, b, k: np.zeros((128, HKC), np.float32))
    x = np.asarray(inputs["L0_attn_hidden_states"])
    W = km(np.asarray(inputs["L0_attn_in_proj_w"]))
    cw = np.asarray(inputs["L0_attn_conv_w"])
    cwk = np.ascontiguousarray(cw.reshape(NXBC, 128, 4).transpose(1, 0, 2).reshape(128, -1))
    cbk = col16x(np.asarray(inputs["L0_attn_conv_b"]), NXBC)
    dtb = np.asarray(inputs["L0_attn_dt_bias"]).reshape(1, 64)
    alog = np.asarray(inputs["L0_attn_A_log"]).reshape(1, 64)
    identbf = np.eye(128).astype(BF)
    maps = []
    for core in range(NCORES):
        b, q = divmod(core, 4)
        t0 = q * TPC
        xh = np.zeros((TH, HID), np.float32)
        xh[4:] = x[b, t0:t0 + TPC]
        if q > 0:
            xh[0:4] = x[b, t0 - 4:t0]
        xf = np.ascontiguousarray(xh.T)
        maps.append(dict(
            xh=km2(xf, TH),
            w_in=W,
            ca0=modget(0, b, "ca"), sa0=modget(0, b, "sa"),
            convw=cwk, convb=cbk, dtb=dtb, alog=alog,
            halo=np.full((128, 1), 1.0 if q > 0 else 0.0, np.float32),
            identbf=identbf,
        ))
    return maps


def col16x(v, n):
    return np.ascontiguousarray(v.reshape(n, 128).T)


def km2(w, M):
    return km(w)



def build_st2a():
    nc = _mknc()
    XT = nc.dram_tensor("xtok", [128, 8 * 4096], BF16, kind="ExternalInput").ap()
    BFT = nc.dram_tensor("bfeat", [128, 1024], BF16, kind="ExternalInput").ap()
    CFT = nc.dram_tensor("cfeat", [128, 1024], BF16, kind="ExternalInput").ap()
    BTK = nc.dram_tensor("btok", [128, 8 * 128], BF16, kind="ExternalInput").ap()
    DTP = nc.dram_tensor("dtp", [128, 8 * 64], F32, kind="ExternalInput").ap()
    ADT = nc.dram_tensor("adt", [128, 8 * 64], F32, kind="ExternalInput").ap()
    DROW = nc.dram_tensor("drow", [1, 64], F32, kind="ExternalInput").ap()
    IDF = nc.dram_tensor("identf", [128, 128], F32, kind="ExternalInput").ap()
    IDB = nc.dram_tensor("identbf", [128, 128], BF16, kind="ExternalInput").ap()
    TRIU = nc.dram_tensor("maskneg", [128, 128], F32, kind="ExternalInput").ap()
    OH4 = nc.dram_tensor("oh4", [4, 4 * 128], BF16, kind="ExternalInput").ap()
    OH64 = nc.dram_tensor("oh64", [64, 64 * 128], F32, kind="ExternalInput").ap()
    OYL = nc.dram_tensor("yloc", [128, 8 * 4096], F32, kind="ExternalOutput").ap()
    OSOUT = nc.dram_tensor("sout", [128, 64 * 64], BF16, kind="ExternalOutput").ap()
    OALAST = nc.dram_tensor("alast", [4, 64], F32, kind="ExternalOutput").ap()
    OSDO = nc.dram_tensor("sdo", [128, 8 * 64], F32, kind="ExternalOutput").ap()

    with ExitStack() as ctx:
        cms = {}
        def sb(n, s, d=F32, side=None):
            n = n + "_s"
            cm = nc.sbuf_tensor(n, s, d, side=side)
            h = cm.__enter__(); cms[n] = cm; return h
        def ps(n, s, d=F32):
            n = n + "_s"
            cm = nc.psum_tensor(n, s, d)
            h = cm.__enter__(); cms[n] = cm; return h

        xt = sb("xt", [128, 8 * 4096], BF16)
        xt3 = xt[:].rearrange("p (a b) -> p a b", b=4096)
        bft = sb("bft", [128, 1024], BF16)
        cft = sb("cft", [128, 1024], BF16)
        btk = sb("btk", [128, 8 * 128], BF16)
        btk3 = btk[:].rearrange("p (a b) -> p a b", b=128)
        dtps = sb("dtps", [128, 8 * 64])
        dtp3 = dtps[:].rearrange("p (a b) -> p a b", b=64)
        adts = sb("adts", [128, 8 * 64])
        adt3 = adts[:].rearrange("p (a b) -> p a b", b=64)
        drows = sb("drows", [1, 64]); idf = sb("idf", [128, 128])
        idb = sb("idb", [128, 128], BF16); mneg = sb("mneg", [128, 128])
        oh4 = sb("oh4", [4, 4 * 128], BF16)
        oh64 = sb("oh64", [64, 64 * 128])
        orf = sb("orf", [1, 128])
        ones64 = sb("ones64", [64, 256])
        adtF = sb("adtF", [64, 8 * 128])
        acu = sb("acu", [64, 1024])
        acub = sb("acub", [64, 1024], BF16)
        natok = sb("natok", [128, 8 * 64])
        natok3 = natok[:].rearrange("p (a b) -> p a b", b=64)
        sdo = sb("sdo", [128, 8 * 64])
        sdo3 = sdo[:].rearrange("p (a b) -> p a b", b=64)
        dsF = sb("dsF", [64, 1024], BF16)
        dstok = sb("dstok", [128, 8 * 64])
        dstok3 = dstok[:].rearrange("p (a b) -> p a b", b=64)
        alast_sb = sb("alast_sb", [4, 64])
        e4bf = sb("e4bf", [4, 64], BF16)
        eac = sb("eac", [128, 4 * 64])
        eac3 = eac[:].rearrange("p (a b) -> p a b", b=64)
        dcols = sb("dcols", [128, 64])
        sloc = sb("sloc", [128, 64 * 64], BF16)
        sloc3 = sloc[:].rearrange("p (a b) -> p a b", b=64)
        spa = sb("spa", [128, 256], BF16)
        spb = sb("spb", [128, 128], BF16)
        argb = sb("argb", [128, 8 * 128])
        ag3 = argb[:].rearrange("p (a b) -> p a b", b=128)
        ebuf = sb("ebuf", [128, 8 * 384], BF16)
        eb3 = ebuf[:].rearrange("p (a b) -> p a b", b=384)
        mbuf = sb("mbuf", [128, 8 * 384], BF16)
        mb3 = mbuf[:].rearrange("p (a b) -> p a b", b=384)
        bdb = sb("bdb", [128, 8 * 256], BF16)
        bdb3 = bdb[:].rearrange("p (a b) -> p a b", b=128)
        xsb = sb("xsb", [128, 8 * 128], BF16)
        xsb3 = xsb[:].rearrange("p (a b) -> p a b", b=64)
        yls = [sb(f"yls{i}", [128, 4096]) for i in range(2)]
        PS8 = [ps(f"PS{i}", [128, 512]) for i in range(8)]
        OYL3 = OYL.rearrange("p (a b) -> p a b", b=4096)

        p = _mkp(nc)
        for dst, src in [(xt, XT), (bft, BFT), (cft, CFT), (btk, BTK), (dtps, DTP),
                         (adts, ADT), (drows, DROW), (idf, IDF), (idb, IDB),
                         (mneg, TRIU), (oh4, OH4), (oh64, OH64)]:
            p.dma("sync", lambda s, dst=dst, src=src: s.dma_start(out=dst[:], in_=src))
        p.op("gpsimd", lambda g: g.memset(orf[:], 1.0))
        p.op("gpsimd", lambda g: g.memset(ones64[:], 1.0))
        p.op("gpsimd", lambda g: g.memset(sloc[:], 0.0))
        p.bar()
        for tt in range(8):
            p.op("tensor", lambda t, tt=tt: t.transpose(
                PS8[tt % 4][0:64, 0:128], adt3[:, tt, :], idf[:]))
            if tt % 4 == 3:
                p.bar()
                for u in range(tt - 3, tt + 1):
                    p.op("vector", lambda v, u=u: v.tensor_copy(
                        adtF[:, u * 128:(u + 1) * 128], PS8[u % 4][0:64, 0:128]))
                p.bar()
        for c in range(4):
            p.op("vector", lambda v, c=c: v.tensor_tensor_scan(
                acu[:, c * 256:(c + 1) * 256], ones64[:, :],
                adtF[:, c * 256:(c + 1) * 256], 0.0, op0=ALU.mult, op1=ALU.add))
        p.bar()
        p.op("tensor", lambda t: t.transpose(PS8[0][0:4, 0:64],
                                             acu[:, 255:1024:256], idf[0:64, 0:64]))
        p.bar()
        p.op("vector", lambda v: v.tensor_copy(alast_sb[:], PS8[0][0:4, 0:64]))
        p.bar()
        p.dma("sync", lambda s: s.dma_start(out=OALAST, in_=alast_sb[:]))
        p.op("scalar", lambda s: s.activation(e4bf[:], alast_sb[:], AF.Exp))
        p.bar()
        for c in range(4):
            p.op("tensor", lambda t, c=c: t.matmul(
                PS8[c][:, 0:64], oh4[:, c * 128:(c + 1) * 128], e4bf[:],
                start=True, stop=True))
        p.bar()
        for c in range(4):
            p.op("vector", lambda v, c=c: v.tensor_copy(eac3[:, c, :], PS8[c][:, 0:64]))
        p.bar()
        p.op("tensor", lambda t: t.matmul(PS8[4][:, 0:64], orf[0:1, :], drows[0:1, :],
                                          start=True, stop=True))
        p.bar()
        p.op("vector", lambda v: v.tensor_copy(dcols[:], PS8[4][:, 0:64]))
        p.bar()
        for c in range(4):
            p.op("scalar", lambda s, c=c: s.activation(
                dsF[:, c * 256:(c + 1) * 256], acu[:, c * 256:(c + 1) * 256],
                AF.Exp, scale=-1.0, bias=acu[:, c * 256 + 255:c * 256 + 256]))
        p.bar()
        for tt in range(8):
            p.op("tensor", lambda t, tt=tt: t.transpose(
                PS8[tt % 2][:, 0:64], acu[:, tt * 128:(tt + 1) * 128], idf[0:64, 0:64]))
            p.op("tensor", lambda t, tt=tt: t.transpose(
                PS8[2 + tt % 2][:, 0:32].bitcast(BF16),
                dsF[:, tt * 128:(tt + 1) * 128], idb[0:64, 0:64]))
            p.bar()
            p.op("vector", lambda v, tt=tt: v.tensor_scalar(
                natok3[:, tt, :], PS8[tt % 2][:, 0:64], -1.0, None, op0=ALU.mult))
            p.op("vector", lambda v, tt=tt: v.tensor_copy(
                dstok3[:, tt, :], PS8[2 + tt % 2][:, 0:32].bitcast(BF16)))
            p.bar()
        p.op("scalar", lambda s: s.activation(sdo[:], natok[:], AF.Exp, scale=-1.0))
        p.bar()
        p.dma("sync", lambda s: s.dma_start(out=OSDO, in_=sdo[:]))

        for c in range(4):
            cs = slice(c * 256, (c + 1) * 256)
            p.op("tensor", lambda t, c=c: t.matmul(
                PS8[5][:, 0:256], bft[:, c * 256:c * 256 + 128], cft[:, c * 256:(c + 1) * 256],
                start=True, stop=True))
            p.op("tensor", lambda t, c=c: t.matmul(
                PS8[6][:, 0:128], bft[:, c * 256 + 128:(c + 1) * 256],
                cft[:, c * 256 + 128:(c + 1) * 256], start=True, stop=True))
            p.bar()
            p.op("vector", lambda v: v.tensor_copy(spa[:], PS8[5][:, 0:256]))
            p.op("vector", lambda v: v.tensor_copy(spb[:], PS8[6][:, 0:128]))
            p.bar()
            for hg in range(16):
                hs = range(hg * 4, hg * 4 + 4)
                for i, h in enumerate(hs):
                    p.op("tensor", lambda t, h=h, i=i, c=c: t.matmul(
                        PS8[i // 2][:, (i % 2) * 256:(i % 2) * 256 + 256],
                        oh64[:, h * 128:(h + 1) * 128], acu[:, c * 256:(c + 1) * 256],
                        start=True, stop=True))
                p.bar()
                for i, h in enumerate(hs):
                    pcb = PS8[i // 2][:, (i % 2) * 256:(i % 2) * 256 + 256]
                    p.op("vector", lambda v, i=i, pcb=pcb: v.tensor_tensor(
                        ag3[:, 2 * i, :], pcb[:, 0:128], mneg[:], op=ALU.add))
                    p.op("vector", lambda v, i=i, pcb=pcb: v.tensor_tensor(
                        ag3[:, 2 * i + 1, :], pcb[:, 128:256], mneg[:], op=ALU.add))
                p.bar()
                for i, h in enumerate(hs):
                    pcb = PS8[i // 2][:, (i % 2) * 256:(i % 2) * 256 + 256]
                    p.op("scalar", lambda s, i=i, h=h, c=c: s.activation(
                        eb3[:, i, 0:128], ag3[:, 2 * i, :], AF.Exp,
                        bias=natok3[:, 2 * c, h:h + 1]))
                    p.op("scalar", lambda s, i=i, h=h, c=c, pcb=pcb: s.activation(
                        eb3[:, i, 128:256], pcb[:, 128:256], AF.Exp,
                        bias=natok3[:, 2 * c, h:h + 1]))
                    p.op("scalar", lambda s, i=i, h=h, c=c: s.activation(
                        eb3[:, i, 256:384], ag3[:, 2 * i + 1, :], AF.Exp,
                        bias=natok3[:, 2 * c + 1, h:h + 1]))
                p.bar()
                for i, h in enumerate(hs):
                    p.op("vector", lambda v, i=i: v.tensor_tensor(
                        mb3[:, i, 0:256], eb3[:, i, 0:256], spa[:], op=ALU.mult))
                    p.op("vector", lambda v, i=i: v.tensor_tensor(
                        mb3[:, i, 256:384], eb3[:, i, 256:384], spb[:], op=ALU.mult))
                    for sc in range(2):
                        p.op("vector", lambda v, i=i, h=h, c=c, sc=sc: v.tensor_scalar(
                            bdb3[:, 2 * i + sc, :], btk3[:, 2 * c + sc, :],
                            dstok3[:, 2 * c + sc, h:h + 1], None, op0=ALU.mult))
                        p.op("vector", lambda v, i=i, h=h, c=c, sc=sc: v.tensor_scalar(
                            xsb3[:, 2 * i + sc, :], xt3[:, 2 * c + sc, h * 64:(h + 1) * 64],
                            dtp3[:, 2 * c + sc, h:h + 1], None, op0=ALU.mult))
                p.bar()
                for i, h in enumerate(hs):
                    o = i * 64
                    p.op("tensor", lambda t, i=i, o=o: t.matmul(
                        PS8[2][:, o:o + 64], mb3[:, i, 0:128], xsb3[:, 2 * i, :],
                        start=True, stop=True))
                    p.op("tensor", lambda t, i=i, o=o: t.matmul(
                        PS8[2][:, 256 + o:256 + o + 64], mb3[:, i, 128:256],
                        xsb3[:, 2 * i, :], start=True, stop=False))
                    p.op("tensor", lambda t, i=i, o=o: t.matmul(
                        PS8[2][:, 256 + o:256 + o + 64], mb3[:, i, 256:384],
                        xsb3[:, 2 * i + 1, :], start=False, stop=True))
                    p.op("tensor", lambda t, i=i, h=h, c=c, o=o: t.matmul(
                        PS8[3][:, o:o + 64], cft[:, c * 256:c * 256 + 128],
                        sloc3[:, h, :], start=True, stop=True))
                    p.op("tensor", lambda t, i=i, h=h, c=c, o=o: t.matmul(
                        PS8[3][:, 256 + o:256 + o + 64], cft[:, c * 256 + 128:(c + 1) * 256],
                        sloc3[:, h, :], start=True, stop=True))
                    p.op("tensor", lambda t, i=i, o=o: t.matmul(
                        PS8[4][:, o:o + 64], bdb3[:, 2 * i, :], xsb3[:, 2 * i, :],
                        start=True, stop=False))
                    p.op("tensor", lambda t, i=i, o=o: t.matmul(
                        PS8[4][:, o:o + 64], bdb3[:, 2 * i + 1, :], xsb3[:, 2 * i + 1, :],
                        start=False, stop=True))
                p.bar()
                for i, h in enumerate(hs):
                    o = i * 64
                    hc = slice(h * 64, (h + 1) * 64)
                    for sc in range(2):
                        yd = PS8[2][:, sc * 256 + o:sc * 256 + o + 64]
                        tgt = yls[sc][:]
                        p.op("vector", lambda v, i=i, h=h, c=c, sc=sc, yd=yd, tgt=tgt, hc=hc:
                             v.scalar_tensor_tensor(tgt[:, hc], xt3[:, 2 * c + sc, hc],
                                                    dcols[:, h:h + 1], yd,
                                                    op0=ALU.mult, op1=ALU.add))
                p.bar()
                for i, h in enumerate(hs):
                    o = i * 64
                    hc = slice(h * 64, (h + 1) * 64)
                    for sc in range(2):
                        yo = PS8[3][:, sc * 256 + o:sc * 256 + o + 64]
                        tgt = yls[sc][:]
                        p.op("vector", lambda v, i=i, h=h, c=c, sc=sc, yo=yo, tgt=tgt, hc=hc:
                             v.scalar_tensor_tensor(tgt[:, hc], yo,
                                                    sdo3[:, 2 * c + sc, h:h + 1], tgt[:, hc],
                                                    op0=ALU.mult, op1=ALU.add))
                    p.op("vector", lambda v, i=i, h=h, c=c:
                         v.scalar_tensor_tensor(sloc3[:, h, :], sloc3[:, h, :],
                                                eac3[:, c, h:h + 1], PS8[4][:, i * 64:i * 64 + 64],
                                                op0=ALU.mult, op1=ALU.add))
                p.bar()
            for sc in range(2):
                p.dma("sync", lambda s, c=c, sc=sc: s.dma_start(
                    out=OYL3[:, 2 * c + sc, :], in_=yls[sc][:]))
            p.bar()
        p.dma("sync", lambda s: s.dma_start(out=OSOUT, in_=sloc[:]))
        _cleanup(cms)
        _fin(p, ctx)
    return nc


def stage_st2a(inputs, r1):
    if r1 is None:
        z = np.zeros((1, 1), np.float32)
        r1 = [dict(xtok=z, bfeat=z, cfeat=z, btok=z, dtp=z, adt=z)] * NCORES
    identf = np.eye(128, dtype=np.float32)
    identbf = np.eye(128).astype(BF)
    maskneg = np.where(np.triu(np.ones((128, 128), bool)), 0.0, -3e38).astype(np.float32)
    oh4 = np.zeros((4, 4 * 128), np.float32)
    for c in range(4):
        oh4[c, c * 128:(c + 1) * 128] = 1.0
    oh64 = np.zeros((64, 64 * 128), np.float32)
    for h in range(64):
        oh64[h, h * 128:(h + 1) * 128] = 1.0
    drow = np.asarray(inputs["L0_attn_D"]).reshape(1, 64)
    maps = []
    for core in range(NCORES):
        r = r1[core]
        maps.append(dict(
            xtok=np.asarray(r["xtok"]), bfeat=np.asarray(r["bfeat"]),
            cfeat=np.asarray(r["cfeat"]), btok=np.asarray(r["btok"]),
            dtp=np.asarray(r["dtp"]), adt=np.asarray(r["adt"]),
            drow=drow, identf=identf, identbf=identbf,
            maskneg=maskneg,
            oh4=oh4.astype(BF), oh64=oh64,
        ))
    return maps



def build_st2b():
    nc = _mknc()
    YL = nc.dram_tensor("yloc", [128, 8 * 4096], F32, kind="ExternalInput").ap()
    SALL = nc.dram_tensor("sout_g", [1024, 4096], BF16, kind="ExternalInput").ap()
    ALO = nc.dram_tensor("alasto", [64, 4], F32, kind="ExternalInput").ap()
    ALP = nc.dram_tensor("alastp", [64, 32], F32, kind="ExternalInput").ap()
    SELM = nc.dram_tensor("selm", [8, 8], F32, kind="ExternalInput").ap()
    SELV = nc.dram_tensor("selv", [8, 1], F32, kind="ExternalInput").ap()
    SDO = nc.dram_tensor("sdo", [128, 8 * 64], F32, kind="ExternalInput").ap()
    CFT = nc.dram_tensor("cfeat", [128, 1024], BF16, kind="ExternalInput").ap()
    GATE = nc.dram_tensor("gate", [128, 8 * 4096], BF16, kind="ExternalInput").ap()
    NW = nc.dram_tensor("nw", [1, 4096], F32, kind="ExternalInput").ap()
    IDF = nc.dram_tensor("identf", [128, 128], F32, kind="ExternalInput").ap()
    IDB = nc.dram_tensor("identbf", [128, 128], BF16, kind="ExternalInput").ap()
    OH4 = nc.dram_tensor("oh4", [4, 4 * 128], BF16, kind="ExternalInput").ap()
    OH8 = nc.dram_tensor("oh8", [8, 8 * 128], BF16, kind="ExternalInput").ap()
    OYGF = nc.dram_tensor("ygf", [128, 32 * 1024], BF16, kind="ExternalOutput").ap()

    YL3 = YL.rearrange("p (a b) -> p a b", b=4096)
    G3 = GATE.rearrange("p (a b) -> p a b", b=4096)
    OYGF3 = OYGF.rearrange("p (a b) -> p a b", b=1024)

    with ExitStack() as ctx:
        cms = {}
        def sb(n, s, d=F32, side=None):
            n = n + "_s"
            cm = nc.sbuf_tensor(n, s, d, side=side)
            h = cm.__enter__(); cms[n] = cm; return h
        def ps(n, s, d=F32):
            n = n + "_s"
            cm = nc.psum_tensor(n, s, d)
            h = cm.__enter__(); cms[n] = cm; return h

        sall = sb("sall", [128, 8 * 4096], BF16, side="right")
        sall3 = sall[:].rearrange("p (a b) -> p a b", b=4096)
        alo = sb("alo", [64, 4]); alp = sb("alp", [64, 32])
        selm = sb("selm", [8, 8]); selv = sb("selv", [8, 1])
        sdos = sb("sdos", [128, 8 * 64])
        sdo3 = sdos[:].rearrange("p (a b) -> p a b", b=64)
        cft = sb("cft", [128, 1024], BF16)
        nwr = sb("nwr", [1, 4096])
        idf = sb("idf", [128, 128]); idb = sb("idb", [128, 128], BF16)
        oh4 = sb("oh4", [4, 4 * 128], BF16); oh8 = sb("oh8", [8, 8 * 128], BF16)
        orf = sb("orf", [1, 128]); epsc = sb("epsc", [128, 1])
        atot = sb("atot", [64, 8]); atott = sb("atott", [8, 64])
        e8 = sb("e8", [8, 64]); e8b = sb("e8b", [8, 64], BF16)
        eb = sb("eb", [128, 8 * 64])
        eb3 = eb[:].rearrange("p (a b) -> p a b", b=64)
        cums = sb("cums", [64, 4]); ecr = sb("ecr", [4, 64])
        ecrb = sb("ecrb", [4, 64], BF16)
        ecb = sb("ecb", [128, 4 * 64])
        ecb3 = ecb[:].rearrange("p (a b) -> p a b", b=64)
        sic = sb("sic", [128, 2 * 4096], side="right")
        sic3 = sic[:].rearrange("p (a b) -> p a b", b=4096)
        sicbf = sb("sicbf", [128, 4096], BF16)
        scolb = sb("scolb", [128, 32])
        ssq = sb("ssq", [128, 1]); rstd = sb("rstd", [128, 1])
        PS8 = [ps(f"PS{i}", [128, 512]) for i in range(8)]
        latetiles = {}
        def late(n, shape, d=F32):
            latetiles[n] = None
            return n

        p = _mkp(nc)
        for dst, src in [(alo, ALO), (alp, ALP), (selm, SELM),
                         (selv, SELV), (sdos, SDO), (cft, CFT), (nwr, NW),
                         (idf, IDF), (idb, IDB), (oh4, OH4), (oh8, OH8)]:
            p.dma("sync", lambda s, dst=dst, src=src: s.dma_start(out=dst[:], in_=src))
        for j in range(8):
            p.dma("sync", lambda s, j=j: s.dma_start(
                out=sall3[:, j, :], in_=SALL[j * 128:(j + 1) * 128, :]))
        p.op("gpsimd", lambda g: g.memset(orf[:], 1.0))
        p.op("gpsimd", lambda g: g.memset(epsc[:], 1e-5))
        p.op("gpsimd", lambda g: g.memset(sic[:], 0.0))
        p.bar()
        for m in range(8):
            p.op("vector", lambda v, m=m: v.reduce_sum(
                atot[:, m:m + 1], alp[:, 4 * m:4 * m + 4], axis=AX.X))
        p.bar()
        p.op("tensor", lambda t: t.transpose(PS8[0][0:8, 0:64], atot[:], idf[0:64, 0:64]))
        p.bar()
        p.op("vector", lambda v: v.tensor_copy(atott[:], PS8[0][0:8, 0:64]))
        p.bar()
        p.op("tensor", lambda t: t.matmul(PS8[0][0:8, 0:64], selm[:], atott[:],
                                          start=True, stop=True))
        p.bar()
        p.op("scalar", lambda s: s.activation(e8[:], PS8[0][0:8, 0:64], AF.Exp))
        p.bar()
        p.op("vector", lambda v: v.tensor_scalar(e8[:], e8[:], selv[:, 0:1], None, op0=ALU.mult))
        p.bar()
        p.op("vector", lambda v: v.tensor_copy(e8b[:], e8[:]))
        p.bar()
        p.op("gpsimd", lambda g: g.memset(scolb[:, 0:1], 0.0))
        p.bar()
        for j in range(8):
            p.op("tensor", lambda t, j=j: t.matmul(
                PS8[j % 4][:, 0:64], oh8[:, j * 128:(j + 1) * 128], e8b[:],
                start=True, stop=True))
            if j % 4 == 3:
                p.bar()
                for u in range(j - 3, j + 1):
                    p.op("vector", lambda v, u=u: v.tensor_copy(
                        eb3[:, u, :], PS8[u % 4][:, 0:64]))
                p.bar()
        for j in range(8):
            for h in range(64):
                p.op("vector", lambda v, j=j, h=h: v.scalar_tensor_tensor(
                    sic3[:, (j + 1) % 2, h * 64:(h + 1) * 64],
                    sall3[:, j, h * 64:(h + 1) * 64], eb3[:, j, h:h + 1],
                    sic3[:, j % 2, h * 64:(h + 1) * 64], op0=ALU.mult, op1=ALU.add))
            p.bar()
        p.op("vector", lambda v: v.tensor_copy(sicbf[:], sic3[:, 0, :]))
        p.bar()
        cms.pop("sic_s").__exit__(None, None, None)
        cms.pop("sall_s").__exit__(None, None, None)
        nwb = sb("nwb", [128, 4096], BF16)
        ybuf = sb("ybuf", [128, 4096])
        ynb = sb("ynb", [128, 4096])
        gbuf = sb("gbuf", [128, 4096], BF16)
        sgb = sb("sgb", [128, 4096], BF16)
        sqscr = sb("sqscr", [128, 4096], BF16)
        ygb = sb("ygb", [128, 4096], BF16)
        ygslab = sb("ygslab", [128, 32 * 128], BF16)
        p.op("gpsimd", lambda g: g.memset(cums[:, 0:1], 0.0))
        p.op("vector", lambda v: v.tensor_copy(cums[:, 1:2], alo[:, 0:1]))
        p.bar()
        p.op("vector", lambda v: v.tensor_tensor(cums[:, 2:3], cums[:, 1:2], alo[:, 1:2], op=ALU.add))
        p.bar()
        p.op("vector", lambda v: v.tensor_tensor(cums[:, 3:4], cums[:, 2:3], alo[:, 2:3], op=ALU.add))
        p.bar()
        p.op("scalar", lambda s: s.activation(cums[:], cums[:], AF.Exp))
        p.bar()
        p.op("tensor", lambda t: t.transpose(PS8[0][0:4, 0:64], cums[:], idf[0:64, 0:64]))
        p.bar()
        p.op("vector", lambda v: v.tensor_copy(ecr[:], PS8[0][0:4, 0:64]))
        p.bar()
        p.op("vector", lambda v: v.tensor_copy(ecrb[:], ecr[:]))
        p.bar()
        for c in range(4):
            p.op("tensor", lambda t, c=c: t.matmul(
                PS8[c][:, 0:64], oh4[:, c * 128:(c + 1) * 128], ecrb[:],
                start=True, stop=True))
        p.bar()
        for c in range(4):
            p.op("vector", lambda v, c=c: v.tensor_copy(ecb3[:, c, :], PS8[c][:, 0:64]))
        p.bar()
        for f in range(8):
            p.op("tensor", lambda t, f=f: t.matmul(
                PS8[f % 4][:, 0:512], orf[0:1, :], nwr[0:1, f * 512:(f + 1) * 512],
                start=True, stop=True))
            if f % 4 == 3:
                p.bar()
                for u in range(f - 3, f + 1):
                    p.op("vector", lambda v, u=u: v.tensor_copy(
                        nwb[:, u * 512:(u + 1) * 512], PS8[u % 4][:, 0:512]))
                p.bar()
        for tt in range(8):
            c = tt // 2
            p.dma("sync", lambda s, tt=tt: s.dma_start(out=ybuf[:], in_=YL3[:, tt, :]))
            p.dma("sync", lambda s, tt=tt: s.dma_start(out=gbuf[:], in_=G3[:, tt, :]))
            p.bar()
            for grp in range(2):
                for i in range(32):
                    h = grp * 32 + i
                    p.op("tensor", lambda t, tt=tt, h=h, i=i: t.matmul(
                        PS8[i // 8][:, (i % 8) * 64:(i % 8) * 64 + 64],
                        cft[:, tt * 128:(tt + 1) * 128], sicbf[:, h * 64:(h + 1) * 64],
                        start=True, stop=True))
                p.bar()
                for i in range(32):
                    h = grp * 32 + i
                    p.op("vector", lambda v, tt=tt, h=h, i=i, c=c: v.tensor_scalar(
                        scolb[:, i:i + 1], sdo3[:, tt, h:h + 1], ecb3[:, c, h:h + 1],
                        None, op0=ALU.mult))
                p.bar()
                for i in range(32):
                    h = grp * 32 + i
                    p.op("vector", lambda v, tt=tt, h=h, i=i: v.scalar_tensor_tensor(
                        ybuf[:, h * 64:(h + 1) * 64],
                        PS8[i // 8][:, (i % 8) * 64:(i % 8) * 64 + 64],
                        scolb[:, i:i + 1], ybuf[:, h * 64:(h + 1) * 64],
                        op0=ALU.mult, op1=ALU.add))
                p.bar()
            p.op("scalar", lambda s: s.activation(sqscr[:], ybuf[:], AF.Square,
                                                  accum_out=ssq[:]))
            p.bar()
            p.op("scalar", lambda s: s.activation(rstd[:], ssq[:], AF.Ln,
                                                  scale=1.0 / 4096, bias=epsc[:, 0:1]))
            p.bar()
            p.op("scalar", lambda s: s.activation(rstd[:], rstd[:], AF.Exp, scale=-0.5))
            p.bar()
            p.op("vector", lambda v: v.tensor_scalar(ynb[:], ybuf[:], rstd[:, 0:1], None, op0=ALU.mult))
            p.op("scalar", lambda s: s.activation(sgb[:], gbuf[:], AF.Sigmoid))
            p.bar()
            p.op("vector", lambda v: v.tensor_tensor(ynb[:], ynb[:], nwb[:], op=ALU.mult))
            p.bar()
            p.op("vector", lambda v: v.tensor_tensor(sgb[:], sgb[:], gbuf[:], op=ALU.mult))
            p.bar()
            p.op("vector", lambda v: v.tensor_tensor(ygb[:], ynb[:], sgb[:], op=ALU.mult))
            p.bar()
            for grp in range(4):
                for i in range(8):
                    fc = grp * 8 + i
                    p.op("tensor", lambda t, fc=fc, i=i: t.transpose(
                        PS8[4 + i // 2][:, (i % 2) * 64:(i % 2) * 64 + 64].bitcast(BF16),
                        ygb[:, fc * 128:(fc + 1) * 128], idb[:]))
                p.bar()
                for i in range(8):
                    fc = grp * 8 + i
                    p.op("vector", lambda v, fc=fc, i=i: v.tensor_copy(
                        ygslab[:, fc * 128:(fc + 1) * 128],
                        PS8[4 + i // 2][:, (i % 2) * 64:(i % 2) * 64 + 64].bitcast(BF16)))
                p.bar()
            p.dma("sync", lambda s, tt=tt: s.dma_start(
                out=OYGF3[:, :, tt * 128:(tt + 1) * 128],
                in_=ygslab[:].rearrange("p (a b) -> p a b", b=128)))
            p.bar()
        _cleanup(cms)
        _fin(p, ctx)
    return nc


def stage_st2b(inputs, r1, r2a, mods):
    identf = np.eye(128, dtype=np.float32)
    identbf = np.eye(128).astype(BF)
    oh4 = np.zeros((4, 4 * 128), np.float32)
    for c in range(4):
        oh4[c, c * 128:(c + 1) * 128] = 1.0
    oh8 = np.zeros((8, 8 * 128), np.float32)
    for j in range(8):
        oh8[j, j * 128:(j + 1) * 128] = 1.0
    soutall = np.concatenate([np.asarray(r2a[j]["sout"]) for j in range(8)], axis=0)
    alastp = np.concatenate([np.asarray(r2a[j]["alast"]).T for j in range(8)], axis=1)
    nw = np.asarray(inputs["L0_attn_norm_weight"]).reshape(1, 4096)
    maps = []
    for core in range(NCORES):
        b, q = divmod(core, 4)
        selm = np.zeros((8, 8), np.float32)
        selv = np.zeros((8, 1), np.float32)
        for j in range(4 * b, 4 * b + q):
            selv[j] = 1.0
            for m in range(j + 1, 4 * b + q):
                selm[m, j] = 1.0
        maps.append(dict(
            yloc=np.asarray(r2a[core]["yloc"]), sout_g=soutall,
            alasto=np.asarray(r2a[core]["alast"]).T.copy(), alastp=alastp,
            selm=selm, selv=selv, sdo=np.asarray(r2a[core]["sdo"]),
            cfeat=np.asarray(r1[core]["cfeat"]), gate=np.asarray(r1[core]["gate"]),
            nw=nw, identf=identf, identbf=identbf,
            oh4=oh4.astype(BF), oh8=oh8.astype(BF),
        ))
    return maps



def build_st2c():
    nc = _mknc()
    YGF = nc.dram_tensor("ygf", [128, 32 * 1024], BF16, kind="ExternalInput").ap()
    X = nc.dram_tensor("xmod", [128, HKC * 1024], F32, kind="ExternalInput").ap()
    W = nc.dram_tensor("w_out", [128, 32 * 2048], F32, kind="ExternalInput").ap()
    GA = nc.dram_tensor("ga0", [128, HKC], F32, kind="ExternalInput").ap()
    OX = nc.dram_tensor("x1", [128, HKC * 1024], F32, kind="ExternalOutput").ap()

    W3 = W.rearrange("p (c m) -> p c m", m=2048)
    OX3 = OX.rearrange("p (c t) -> p c t", t=1024)

    with ExitStack() as ctx:
        cms = {}
        def sb(n, s, d=F32, side=None):
            n = n + "_s"
            cm = nc.sbuf_tensor(n, s, d, side=side)
            h = cm.__enter__(); cms[n] = cm; return h
        def ps(n, s, d=F32):
            n = n + "_s"
            cm = nc.psum_tensor(n, s, d)
            h = cm.__enter__(); cms[n] = cm; return h

        ygf = sb("ygf", [128, 32 * 1024], BF16)
        yg3 = ygf[:].rearrange("p (c t) -> p c t", t=1024)
        x_sb = sb("x", [128, HKC * 1024])
        x3 = x_sb[:].rearrange("p (c t) -> p c t", t=1024)
        ga = sb("ga", [128, HKC])
        wsl = sb("wsl", [128, 32 * 256], BF16)
        ws3 = wsl[:].rearrange("p (c m) -> p c m", m=256)
        ob = sb("ob", [128, 2 * 1024])
        ob3 = ob[:].rearrange("p (c t) -> p c t", t=1024)
        PS8 = [ps(f"PS{i}", [128, 512]) for i in range(8)]

        p = _mkp(nc)
        p.dma("sync", lambda s: s.dma_start(out=ygf[:], in_=YGF))
        p.dma("sync", lambda s: s.dma_start(out=x_sb[:], in_=X))
        p.dma("sync", lambda s: s.dma_start(out=ga[:], in_=GA))
        p.bar()
        for ms in range(8):
            p.dma("gpsimd", lambda g, ms=ms: g.dma_start(
                out=wsl[:], in_=W3[:, :, ms * 256:(ms + 1) * 256]))
            p.bar()
            for mc in range(2):
                for tt in range(8):
                    pu = PS8[mc * 2 + tt // 4][:, (tt % 4) * 128:(tt % 4) * 128 + 128]
                    for kc in range(32):
                        p.op("tensor", lambda t, mc=mc, tt=tt, kc=kc, pu=pu: t.matmul(
                            pu, ws3[:, kc, mc * 128:(mc + 1) * 128],
                            yg3[:, kc, tt * 128:(tt + 1) * 128],
                            start=(kc == 0), stop=(kc == 31)))
            p.bar()
            for mc in range(2):
                kcg = ms * 2 + mc
                for tt in range(8):
                    pu = PS8[mc * 2 + tt // 4][:, (tt % 4) * 128:(tt % 4) * 128 + 128]
                    p.op("vector", lambda v, mc=mc, tt=tt, kcg=kcg, pu=pu:
                         v.scalar_tensor_tensor(ob3[:, mc, tt * 128:(tt + 1) * 128],
                                                pu, ga[:, kcg:kcg + 1],
                                                x3[:, kcg, tt * 128:(tt + 1) * 128],
                                                op0=ALU.mult, op1=ALU.add))
            p.bar()
            p.dma("sync", lambda s, ms=ms: s.dma_start(
                out=OX3[:, ms * 2:(ms + 1) * 2, :], in_=ob3[:, :, :]))
            p.bar()
        _cleanup(cms)
        _fin(p, ctx)
    return nc


def stage_st2c(inputs, r2b, mods):
    W = km(np.asarray(inputs["L0_attn_out_proj_w"]))
    x = np.asarray(inputs["L0_attn_hidden_states"])
    maps = []
    for core in range(NCORES):
        b, q = divmod(core, 4)
        t0 = q * TPC
        xf = np.ascontiguousarray(x[b, t0:t0 + TPC].T)
        maps.append(dict(
            ygf=np.asarray(r2b[core]["ygf"]), xmod=km(xf), w_out=W,
            ga0=mods[0][b]["ga"],
        ))
    return maps



def build_st3(debug=False):
    nc = _mknc()
    X = nc.dram_tensor("x1", [128, HKC * 1024], F32, kind="ExternalInput").ap()
    CB = nc.dram_tensor("cb0", [128, HKC], F32, kind="ExternalInput").ap()
    SB_ = nc.dram_tensor("sb0", [128, HKC], F32, kind="ExternalInput").ap()
    GB = nc.dram_tensor("gb0", [128, HKC], F32, kind="ExternalInput").ap()
    RW = nc.dram_tensor("routw", [128, HKC * 8], F32, kind="ExternalInput").ap()
    WG = nc.dram_tensor("wg", [128, 8 * HKC * 1024], F32, kind="ExternalInput").ap()
    WU = nc.dram_tensor("wu", [128, 8 * HKC * 1024], F32, kind="ExternalInput").ap()
    WD = nc.dram_tensor("wd", [128, 8 * 8 * 2048], F32, kind="ExternalInput").ap()
    SHG = nc.dram_tensor("shg", [128, HKC * 2048], F32, kind="ExternalInput").ap()
    SHU = nc.dram_tensor("shu", [128, HKC * 2048], F32, kind="ExternalInput").ap()
    SHD = nc.dram_tensor("shd", [128, HKC * 2048], F32, kind="ExternalInput").ap()
    SEG = nc.dram_tensor("seg", [128, HKC], F32, kind="ExternalInput").ap()
    IDF = nc.dram_tensor("identf", [128, 128], F32, kind="ExternalInput").ap()
    OH8F = nc.dram_tensor("oh8f", [8, 8 * 128], F32, kind="ExternalInput").ap()
    OX = nc.dram_tensor("x2", [128, HKC * 1024], F32, kind="ExternalOutput").ap()
    if debug:
        OH2 = nc.dram_tensor("dh2", [128, HKC * 1024], BF16, kind="ExternalOutput").ap()
        OCOMB = nc.dram_tensor("dcomb", [8, 1024], F32, kind="ExternalOutput").ap()
        OYM = nc.dram_tensor("dym", [128, HKC * 1024], F32, kind="ExternalOutput").ap()
        OSG = nc.dram_tensor("dsg", [1, 1024], F32, kind="ExternalOutput").ap()
        OSHB = nc.dram_tensor("dshb", [128, HKC * 1024], BF16, kind="ExternalOutput").ap()

    WG4 = WG.rearrange("p (e c m) -> p e c m", c=HKC, m=1024)
    WU4 = WU.rearrange("p (e c m) -> p e c m", c=HKC, m=1024)
    WD4 = WD.rearrange("p (e c m) -> p e c m", c=8, m=2048)
    SHG3 = SHG.rearrange("p (c m) -> p c m", m=2048)
    SHU3 = SHU.rearrange("p (c m) -> p c m", m=2048)
    SHD3 = SHD.rearrange("p (c m) -> p c m", m=2048)
    OX3 = OX.rearrange("p (c t) -> p c t", t=1024)
    X3d = X.rearrange("p (c t) -> p c t", t=1024)

    with ExitStack() as ctx:
        cms = {}
        def sb(n, s, d=F32, side=None):
            n = n + "_s"
            cm = nc.sbuf_tensor(n, s, d, side=side)
            h = cm.__enter__(); cms[n] = cm; return h
        def ps(n, s, d=F32):
            n = n + "_s"
            cm = nc.psum_tensor(n, s, d)
            h = cm.__enter__(); cms[n] = cm; return h
        def free(*names):
            for n in names:
                cms.pop(n + "_s").__exit__(None, None, None)

        ocb = sb("ocb", [128, 1], BF16)
        ocf = sb("ocf", [128, 1]); orf = sb("orf", [1, 128])
        epsc = sb("epsc", [1, 1])
        cb_c = sb("cb_c", [128, HKC]); sb_c = sb("sb_c", [128, HKC])
        gb_c = sb("gb_c", [128, HKC])
        rw = sb("rw", [128, HKC * 8])
        rw3 = rw[:].rearrange("p (c m) -> p c m", m=8)
        rwb = sb("rwb", [128, HKC * 8], BF16)
        rwb3 = rwb[:].rearrange("p (c m) -> p c m", m=8)
        seg = sb("seg", [128, HKC]); segb = sb("segb", [128, HKC], BF16)
        idf = sb("idf", [128, 128])
        oh8f = sb("oh8f", [8, 8 * 128])
        h2 = sb("h2", [128, HKC * 1024], BF16)
        h23 = h2[:].rearrange("p (c t) -> p c t", t=1024)
        combF = sb("combF", [8, 1024])
        sgrow = sb("sgrow", [1, 1024])
        sgb = sb("sgb", [128, 1024])
        scr1 = sb("scr1", [128, 8]); scr2 = sb("scr2", [128, 8])
        scri = sb("scri", [128, 8], I32); scrf = sb("scrf", [128, 8])
        m1c = sb("m1c", [128, 1]); m2c = sb("m2c", [128, 1])
        p1c = sb("p1c", [128, 1]); p2c = sb("p2c", [128, 1])
        combt = sb("combt", [128, 8])

        p = _mkp(nc)
        for dst, src in [(cb_c, CB), (sb_c, SB_), (gb_c, GB), (rw, RW),
                         (seg, SEG), (idf, IDF), (oh8f, OH8F)]:
            p.dma("sync", lambda s, dst=dst, src=src: s.dma_start(out=dst[:], in_=src))
        p.op("gpsimd", lambda g: g.memset(ocb[:], 1.0))
        p.op("gpsimd", lambda g: g.memset(ocf[:], 1.0))
        p.op("gpsimd", lambda g: g.memset(orf[:], 1.0))
        p.op("gpsimd", lambda g: g.memset(epsc[:], 1e-5))
        p.bar()
        p.op("vector", lambda v: v.tensor_scalar(cb_c[:], cb_c[:], 1.0, None, op0=ALU.add))
        p.op("scalar", lambda s: s.copy(rwb[:], rw[:]))
        p.op("scalar", lambda s: s.copy(segb[:], seg[:]))
        p.bar()
        x_sb = sb("x_sb", [128, HKC * 1024], side="right")
        x3 = x_sb[:].rearrange("p (c t) -> p c t", t=1024)
        p.dma("sync", lambda s: s.dma_start(out=x_sb[:], in_=X))
        p.bar()
        emit_ln_mod(p, lambda n, sh, d=F32: sb(n, sh, d, side="right"), ps, nc,
                    x3, h23, cb_c, sb_c, ocb, ocf, orf, HKC, 1024,
                    eps_ap=epsc[0:1, 0:1], tag="b")
        p.bar()
        free("ln_m2b", "ln_t1b", "ln_MRBb", "ln_RBb", "ln_rowsb", "ln_sqb", "x_sb",
             "ln_rbb", "ln_ssb")
        PS8 = [ps(f"PS{i}", [128, 512]) for i in range(8)]
        ymoe = sb("ymoe", [128, HKC * 1024])
        ym3 = ymoe[:].rearrange("p (c t) -> p c t", t=1024)
        act32 = sb("act32", [128, 16384], BF16, side="right")
        cbc = act32
        cbc3 = act32[:, 0:8192].rearrange("p (a b) -> p a b", b=1024)
        hec3 = act32[:, 8192:16384].rearrange("p (a b) -> p a b", b=1024)
        wbig = sb("wbig", [128, 16384], BF16, side="right")
        p.op("gpsimd", lambda g: g.memset(ymoe[:], 0.0))
        p.bar()

        for tt in range(8):
            for kc in range(HKC):
                p.op("tensor", lambda t, tt=tt, kc=kc: t.matmul(
                    PS8[tt][:, 0:8], h23[:, kc, tt * 128:(tt + 1) * 128],
                    rwb3[:, kc, :], start=(kc == 0), stop=(kc == HKC - 1)))
        p.bar()
        mask2f = sb("mask2f", [128, 8])
        for tt in range(8):
            p.op("vector", lambda v, tt=tt: v.tensor_copy(scr1[:], PS8[tt][:, 0:8]))
            p.bar()
            p.op("vector", lambda v: v.reduce_max(m1c[:], scr1[:], axis=AX.X))
            p.bar()
            p.op("vector", lambda v: v.tensor_scalar(scr2[:], scr1[:], m1c[:, 0:1], None, op0=ALU.subtract))
            p.bar()
            p.op("vector", lambda v: v.tensor_scalar(scri[:], scr2[:], 0.0, None, op0=ALU.is_ge))
            p.bar()
            p.op("vector", lambda v: v.tensor_copy(scrf[:], scri[:]))
            p.bar()
            p.op("vector", lambda v: v.scalar_tensor_tensor(scr2[:], scrf[:], -1e30, scr1[:],
                                                            op0=ALU.mult, op1=ALU.add))
            p.bar()
            p.op("vector", lambda v: v.reduce_max(m2c[:], scr2[:], axis=AX.X))
            p.bar()
            p.op("vector", lambda v: v.tensor_scalar(scr2[:], scr2[:], m2c[:, 0:1], None, op0=ALU.subtract))
            p.bar()
            p.op("vector", lambda v: v.tensor_scalar(scri[:], scr2[:], 0.0, None, op0=ALU.is_ge))
            p.bar()
            p.op("vector", lambda v: v.tensor_copy(mask2f[:], scri[:]))
            p.op("vector", lambda v: v.tensor_tensor(m2c[:], m1c[:], m2c[:], op=ALU.subtract))
            p.bar()
            p.op("scalar", lambda s: s.activation(p1c[:], m2c[:], AF.Sigmoid))
            p.bar()
            p.op("vector", lambda v: v.tensor_scalar(p2c[:], p1c[:], -1.0, 1.0,
                                                     op0=ALU.mult, op1=ALU.add))
            p.op("vector", lambda v: v.tensor_scalar(combt[:], scrf[:], p1c[:, 0:1], None, op0=ALU.mult))
            p.bar()
            p.op("vector", lambda v: v.scalar_tensor_tensor(combt[:], mask2f[:], p2c[:, 0:1],
                                                            combt[:], op0=ALU.mult, op1=ALU.add))
            p.bar()
            p.op("tensor", lambda t, tt=tt: t.transpose(PS8[tt][0:8, 128:256], combt[:],
                                                        idf[:]))
            p.bar()
            p.op("vector", lambda v, tt=tt: v.tensor_copy(
                combF[:, tt * 128:(tt + 1) * 128], PS8[tt][0:8, 128:256]))
            p.bar()
        for e in range(8):
            for f in range(2):
                p.op("tensor", lambda t, e=e, f=f: t.matmul(
                    PS8[e % 4][:, f * 256:(f + 1) * 256] if False else PS8[e % 4][:, 0:512],
                    oh8f[:, e * 128:(e + 1) * 128], combF[:, f * 512:(f + 1) * 512],
                    start=True, stop=True) if f == 0 else t.matmul(
                    PS8[4 + e % 4][:, 0:512],
                    oh8f[:, e * 128:(e + 1) * 128], combF[:, f * 512:(f + 1) * 512],
                    start=True, stop=True))
            if e % 4 == 3:
                p.bar()
                for u in range(e - 3, e + 1):
                    p.op("vector", lambda v, u=u: v.tensor_copy(
                        cbc3[:, u, 0:512], PS8[u % 4][:, 0:512]))
                    p.op("vector", lambda v, u=u: v.tensor_copy(
                        cbc3[:, u, 512:1024], PS8[4 + u % 4][:, 0:512]))
                p.bar()

        wslab = wbig[:, 0:8192]
        ws3 = wslab.rearrange("p (c m) -> p c m", m=512)
        wslab2 = wbig[:, 8192:16384]
        ws23 = wslab2.rearrange("p (c m) -> p c m", m=512)
        wd3 = wbig[:].rearrange("p (c m) -> p c m", m=2048)
        for e in range(8):
            for mh in range(2):
                p.dma("gpsimd", lambda g, e=e, mh=mh: g.dma_start(
                    out=wslab, in_=WG4[:, e, :, mh * 512:(mh + 1) * 512]))
                p.dma("gpsimd", lambda g, e=e, mh=mh: g.dma_start(
                    out=wslab2, in_=WU4[:, e, :, mh * 512:(mh + 1) * 512]))
                p.bar()
                for mc in range(4):
                    mg = mh * 4 + mc
                    for f in range(2):
                        for kc in range(HKC):
                            p.op("tensor", lambda t, mc=mc, f=f, kc=kc: t.matmul(
                                PS8[f][:, 0:512], ws3[:, kc, mc * 128:(mc + 1) * 128],
                                h23[:, kc, f * 512:(f + 1) * 512],
                                start=(kc == 0), stop=(kc == HKC - 1)))
                        for kc in range(HKC):
                            p.op("tensor", lambda t, mc=mc, f=f, kc=kc: t.matmul(
                                PS8[2 + f][:, 0:512], ws23[:, kc, mc * 128:(mc + 1) * 128],
                                h23[:, kc, f * 512:(f + 1) * 512],
                                start=(kc == 0), stop=(kc == HKC - 1)))
                    p.bar()
                    for f in range(2):
                        fs = slice(f * 512, (f + 1) * 512)
                        p.op("scalar", lambda s, mg=mg, f=f, fs=fs: s.activation(
                            hec3[:, mg, fs], PS8[f][:, 0:512], AF.Silu))
                    p.bar()
                    for f in range(2):
                        fs = slice(f * 512, (f + 1) * 512)
                        p.op("vector", lambda v, mg=mg, f=f, fs=fs: v.tensor_tensor(
                            hec3[:, mg, fs], hec3[:, mg, fs], PS8[2 + f][:, 0:512], op=ALU.mult))
                    p.bar()
                    for f in range(2):
                        fs = slice(f * 512, (f + 1) * 512)
                        p.op("vector", lambda v, e=e, mg=mg, fs=fs: v.tensor_tensor(
                            hec3[:, mg, fs], hec3[:, mg, fs], cbc3[:, e, fs], op=ALU.mult))
                    p.bar()
            p.dma("gpsimd", lambda g, e=e: g.dma_start(out=wbig[:], in_=WD4[:, e, :, :]))
            p.bar()
            for mo in range(HKC):
                for f in range(2):
                    pu = PS8[4 + (mo * 2 + f) % 4]
                    for kc in range(8):
                        p.op("tensor", lambda t, mo=mo, f=f, kc=kc, pu=pu: t.matmul(
                            pu[:], wd3[:, kc, mo * 128:(mo + 1) * 128],
                            hec3[:, kc, f * 512:(f + 1) * 512],
                            start=(kc == 0), stop=(kc == 7)))
                    if (mo * 2 + f) % 4 == 3:
                        p.bar()
                        for u in range(mo * 2 + f - 3, mo * 2 + f + 1):
                            uo, uf = divmod(u, 2)
                            p.op("vector", lambda v, uo=uo, uf=uf, u=u: v.tensor_tensor(
                                ym3[:, uo, uf * 512:(uf + 1) * 512],
                                ym3[:, uo, uf * 512:(uf + 1) * 512],
                                PS8[4 + u % 4][:], op=ALU.add))
                        p.bar()

        if debug:
            p.dma("sync", lambda s: s.dma_start(out=OH2, in_=h2[:]))
            p.dma("sync", lambda s: s.dma_start(out=OCOMB, in_=combF[:]))
            p.dma("sync", lambda s: s.dma_start(out=OYM, in_=ymoe[:]))
            p.bar()
        shb3 = act32[:].rearrange("p (c t) -> p c t", t=1024)
        wl3 = wbig[:, 0:8192].rearrange("p (c m) -> p c m", m=512)
        wl23 = wbig[:, 8192:16384].rearrange("p (c m) -> p c m", m=512)
        for ms in range(4):
            p.dma("gpsimd", lambda g, ms=ms: g.dma_start(
                out=wbig[:, 0:8192], in_=SHG3[:, :, ms * 512:(ms + 1) * 512]))
            p.dma("gpsimd", lambda g, ms=ms: g.dma_start(
                out=wbig[:, 8192:16384], in_=SHU3[:, :, ms * 512:(ms + 1) * 512]))
            p.bar()
            for mc in range(4):
                mg = ms * 4 + mc
                for f in range(2):
                    for kc in range(HKC):
                        p.op("tensor", lambda t, mc=mc, f=f, kc=kc: t.matmul(
                            PS8[f][:, 0:512], wl3[:, kc, mc * 128:(mc + 1) * 128],
                            h23[:, kc, f * 512:(f + 1) * 512],
                            start=(kc == 0), stop=(kc == HKC - 1)))
                    for kc in range(HKC):
                        p.op("tensor", lambda t, mc=mc, f=f, kc=kc: t.matmul(
                            PS8[2 + f][:, 0:512], wl23[:, kc, mc * 128:(mc + 1) * 128],
                            h23[:, kc, f * 512:(f + 1) * 512],
                            start=(kc == 0), stop=(kc == HKC - 1)))
                p.bar()
                for f in range(2):
                    fs = slice(f * 512, (f + 1) * 512)
                    p.op("scalar", lambda s, mg=mg, f=f, fs=fs: s.activation(
                        shb3[:, mg, fs], PS8[f][:, 0:512], AF.Silu))
                p.bar()
                for f in range(2):
                    fs = slice(f * 512, (f + 1) * 512)
                    p.op("vector", lambda v, mg=mg, f=f, fs=fs: v.tensor_tensor(
                        shb3[:, mg, fs], shb3[:, mg, fs], PS8[2 + f][:, 0:512], op=ALU.mult))
                p.bar()
        for tt in range(8):
            for kc in range(HKC):
                p.op("tensor", lambda t, tt=tt, kc=kc: t.matmul(
                    PS8[tt % 4][:, 0:1], h23[:, kc, tt * 128:(tt + 1) * 128],
                    segb[:, kc:kc + 1], start=(kc == 0), stop=(kc == HKC - 1)))
            if tt % 4 == 3:
                p.bar()
                for u in range(tt - 3, tt + 1):
                    p.op("scalar", lambda s, u=u: s.activation(
                        m1c[:], PS8[u % 4][:, 0:1], AF.Sigmoid))
                    p.bar()
                    p.op("tensor", lambda t, u=u: t.transpose(
                        PS8[4][0:1, 0:128], m1c[:], idf[:]))
                    p.bar()
                    p.op("vector", lambda v, u=u: v.tensor_copy(
                        sgrow[:, u * 128:(u + 1) * 128], PS8[4][0:1, 0:128]))
                    p.bar()
        for f in range(2):
            p.op("tensor", lambda t, f=f: t.matmul(
                PS8[f][:, 0:512], orf[0:1, :], sgrow[0:1, f * 512:(f + 1) * 512],
                start=True, stop=True))
        p.bar()
        for f in range(2):
            p.op("vector", lambda v, f=f: v.tensor_copy(
                sgb[:, f * 512:(f + 1) * 512], PS8[f][:, 0:512]))
        p.bar()
        if debug:
            p.dma("sync", lambda s: s.dma_start(out=OSG, in_=sgrow[:]))
            p.dma("sync", lambda s: s.dma_start(out=OSHB, in_=act32[:]))
            p.bar()
        wsd3 = wbig[:, 0:8192].rearrange("p (c m) -> p c m", m=512)
        xbuf = wbig[:, 8192:16384].bitcast(F32)
        scrt = sb("scrt", [128, 512], side="right")
        scrt2 = sb("scrt2", [128, 512], side="right")
        for ms in range(4):
            p.dma("gpsimd", lambda g, ms=ms: g.dma_start(
                out=wbig[:, 0:8192], in_=SHD3[:, :, ms * 512:(ms + 1) * 512]))
            p.bar()
            for mc in range(4):
                mo = ms * 4 + mc
                for f in range(2):
                    pu = PS8[4 + f]
                    for kc in range(HKC):
                        p.op("tensor", lambda t, mc=mc, f=f, kc=kc, pu=pu: t.matmul(
                            pu[:], wsd3[:, kc, mc * 128:(mc + 1) * 128],
                            shb3[:, kc, f * 512:(f + 1) * 512],
                            start=(kc == 0), stop=(kc == HKC - 1)))
                p.bar()
                for f in range(2):
                    fs = slice(f * 512, (f + 1) * 512)
                    p.op("vector", lambda v, f=f, fs=fs: v.tensor_tensor(
                        scrt[:] if f == 0 else scrt2[:], PS8[4 + f][:], sgb[:, fs], op=ALU.mult))
                p.bar()
                for f in range(2):
                    fs = slice(f * 512, (f + 1) * 512)
                    p.op("vector", lambda v, mo=mo, f=f, fs=fs: v.tensor_tensor(
                        ym3[:, mo, fs], ym3[:, mo, fs], scrt[:] if f == 0 else scrt2[:], op=ALU.add))
                p.bar()
        for kc in range(HKC):
            p.dma("sync", lambda s, kc=kc: s.dma_start(
                out=xbuf[:, kc % 2 * 1024:(kc % 2 + 1) * 1024], in_=X3d[:, kc, :]))
            p.bar()
            p.op("vector", lambda v, kc=kc: v.scalar_tensor_tensor(
                ym3[:, kc, :], ym3[:, kc, :], gb_c[:, kc:kc + 1],
                xbuf[:, kc % 2 * 1024:(kc % 2 + 1) * 1024], op0=ALU.mult, op1=ALU.add))
            p.bar()
            p.dma("sync", lambda s, kc=kc: s.dma_start(out=OX3[:, kc, :], in_=ym3[:, kc, :]))
        p.bar()
        _cleanup(cms)
        _fin(p, ctx)
    return nc


def stage_st3(inputs, r2c, mods):
    wg = np.asarray(inputs["L0_ffn_w_gate"])
    wu = np.asarray(inputs["L0_ffn_w_up"])
    wd = np.asarray(inputs["L0_ffn_w_down"])
    WG = np.concatenate([km(wg[e].T) for e in range(8)], axis=1)
    WU = np.concatenate([km(wu[e].T) for e in range(8)], axis=1)
    WD = np.concatenate([km(wd[e].T) for e in range(8)], axis=1)
    SHG = km(np.asarray(inputs["L0_ffn_sh_gate"]).T)
    SHU = km(np.asarray(inputs["L0_ffn_sh_up"]).T)
    SHD = km(np.asarray(inputs["L0_ffn_sh_down"]).T)
    RW = km(np.asarray(inputs["L0_ffn_gate_w"]).T)
    SEG = km(np.asarray(inputs["L0_ffn_se_gate_w"]).T)
    identf = np.eye(128, dtype=np.float32)
    oh8f = np.zeros((8, 8 * 128), np.float32)
    for j in range(8):
        oh8f[j, j * 128:(j + 1) * 128] = 1.0
    maps = []
    for core in range(NCORES):
        b, q = divmod(core, 4)
        maps.append(dict(
            x1=np.asarray(r2c[core]["x1"]),
            cb0=mods[0][b]["cb"], sb0=mods[0][b]["sb"], gb0=mods[0][b]["gb"],
            routw=RW, wg=WG, wu=WU, wd=WD, shg=SHG, shu=SHU, shd=SHD, seg=SEG,
            identf=identf, oh8f=oh8f,
        ))
    return maps



def build_st4a():
    nc = _mknc()
    X = nc.dram_tensor("x2", [128, HKC * 1024], F32, kind="ExternalInput").ap()
    CA = nc.dram_tensor("ca1", [128, HKC], F32, kind="ExternalInput").ap()
    SA = nc.dram_tensor("sa1", [128, HKC], F32, kind="ExternalInput").ap()
    ANW = nc.dram_tensor("anw", [128, HKC], F32, kind="ExternalInput").ap()
    QW = nc.dram_tensor("qw", [128, HKC * 2048], F32, kind="ExternalInput").ap()
    KW = nc.dram_tensor("kw", [128, HKC * 2048], F32, kind="ExternalInput").ap()
    VW = nc.dram_tensor("vw", [128, HKC * 4096], F32, kind="ExternalInput").ap()
    GW = nc.dram_tensor("gw", [128, HKC * 4096], F32, kind="ExternalInput").ap()
    COS = nc.dram_tensor("cosx", [128, 1024], F32, kind="ExternalInput").ap()
    SIN = nc.dram_tensor("sinx", [128, 1024], F32, kind="ExternalInput").ap()
    CDQ = nc.dram_tensor("cdq", [128, 8 * 64], F32, kind="ExternalInput").ap()
    KD = nc.dram_tensor("kdc", [128, 8 * 64], F32, kind="ExternalInput").ap()
    CDG = nc.dram_tensor("cdg", [128, 8 * 1024], BF16, kind="ExternalInput").ap()
    CD64 = nc.dram_tensor("cd64", [128, 8], F32, kind="ExternalInput").ap()
    DMT = nc.dram_tensor("dmaskT", [64, 8 * 64], F32, kind="ExternalInput").ap()
    IDB = nc.dram_tensor("identbf", [128, 128], BF16, kind="ExternalInput").ap()
    OH = nc.dram_tensor("hres", [128, HKC * 1024], F32, kind="ExternalOutput").ap()
    OOP = nc.dram_tensor("opart", [128, 8 * 4096], BF16, kind="ExternalOutput").ap()
    OQG = nc.dram_tensor("qg", [128, HKC * 1024], BF16, kind="ExternalOutput").ap()
    OG = nc.dram_tensor("gtok", [128, 8 * 4096], BF16, kind="ExternalOutput").ap()
    OSR = nc.dram_tensor("soutr", [128, 2 * 8 * 512], BF16, kind="ExternalOutput").ap()

    QW3 = QW.rearrange("p (c m) -> p c m", m=2048)
    KW3 = KW.rearrange("p (c m) -> p c m", m=2048)
    VW3 = VW.rearrange("p (c m) -> p c m", m=4096)
    GW3 = GW.rearrange("p (c m) -> p c m", m=4096)
    OG3 = OG.rearrange("p (a b) -> p a b", b=4096)
    OOP3 = OOP.rearrange("p (a b) -> p a b", b=4096)

    with ExitStack() as ctx:
        cms = {}
        def sb(n, s, d=F32, side=None):
            n = n + "_s"
            cm = nc.sbuf_tensor(n, s, d, side=side)
            h = cm.__enter__(); cms[n] = cm; return h
        def ps(n, s, d=F32):
            n = n + "_s"
            cm = nc.psum_tensor(n, s, d)
            h = cm.__enter__(); cms[n] = cm; return h
        def free(*names):
            for n in names:
                cms.pop(n + "_s").__exit__(None, None, None)

        ocb = sb("ocb", [128, 1], BF16)
        ocf = sb("ocf", [128, 1]); orf = sb("orf", [1, 128])
        epsc = sb("epsc", [1, 1]); epsc2 = sb("eps2", [1, 1])
        ca_c = sb("ca_c", [128, HKC]); sa_c = sb("sa_c", [128, HKC])
        anw_c = sb("anw_c", [128, HKC])
        cosb = sb("cosb", [128, 1024]); sinb = sb("sinb", [128, 1024])
        cdq = sb("cdq", [128, 8 * 64]); kdc = sb("kdc", [128, 8 * 64])
        cdg = sb("cdg", [128, 8 * 1024], BF16)
        cd64 = sb("cd64", [128, 8])
        dmt = sb("dmt", [64, 8 * 64])
        idb = sb("idb", [128, 128], BF16)
        hh = sb("hh", [128, HKC * 1024], BF16, side="right")
        hh3 = hh[:].rearrange("p (c t) -> p c t", t=1024)

        x_sb = sb("x_sb", [128, HKC * 1024], side="right")
        x3 = x_sb[:].rearrange("p (c t) -> p c t", t=1024)
        hres = x_sb
        hr3 = x3

        p = _mkp(nc)
        for dst, src in [(ca_c, CA), (sa_c, SA), (anw_c, ANW), (cosb, COS),
                         (sinb, SIN), (cdq, CDQ), (kdc, KD), (cd64, CD64),
                         (dmt, DMT), (idb, IDB), (cdg, CDG)]:
            p.dma("sync", lambda s, dst=dst, src=src: s.dma_start(out=dst[:], in_=src))
        p.dma("sync", lambda s: s.dma_start(out=x_sb[:], in_=X))
        p.op("gpsimd", lambda g: g.memset(ocb[:], 1.0))
        p.op("gpsimd", lambda g: g.memset(ocf[:], 1.0))
        p.op("gpsimd", lambda g: g.memset(orf[:], 1.0))
        p.op("gpsimd", lambda g: g.memset(epsc[:], 1e-5))
        p.op("gpsimd", lambda g: g.memset(epsc2[:], 1e-6))
        p.bar()
        p.op("vector", lambda v: v.tensor_scalar(ca_c[:], ca_c[:], 1.0, None, op0=ALU.add))
        p.bar()
        emit_ln_mod(p, lambda n, sh, d=F32: sb(n, sh, d, side="right"), ps, nc,
                    x3, hr3, ca_c, sa_c, ocb, ocf, orf, HKC, 1024,
                    eps_ap=epsc[0:1, 0:1], tag="a4")
        p.bar()
        free("ln_m2a4", "ln_t1a4", "ln_MRBa4", "ln_RBa4", "ln_rowsa4", "ln_sqa4",
             "ln_rba4", "ln_ssa4")
        p.dma("sync", lambda s: s.dma_start(out=OH, in_=hres[:]))
        sqb = sb("sqb", [128, HKC * 1024], BF16, side="right")
        sq3 = sqb[:].rearrange("p (c t) -> p c t", t=1024)
        rrow = sb("rrow", [1, 1024])
        RBr = sb("RBr", [128, 1024])
        SSr = ps("SSr", [1, 512])
        RBp = ps("RBp", [128, 512])
        p.op("scalar", lambda s: s.activation(sqb[:], hres[:], AF.Square))
        p.bar()
        for f in range(2):
            for kc in range(HKC):
                p.op("tensor", lambda t, f=f, kc=kc: t.matmul(
                    SSr[:, 0:512], ocb[:, 0:1], sq3[:, kc, f * 512:(f + 1) * 512],
                    start=(kc == 0), stop=(kc == HKC - 1)))
            p.bar()
            p.op("scalar", lambda s, f=f: s.activation(
                rrow[:, f * 512:(f + 1) * 512], SSr[:, 0:512], AF.Ln,
                scale=1.0 / 2048, bias=epsc2[0:1, 0:1]))
            p.bar()
        p.op("scalar", lambda s: s.activation(rrow[:], rrow[:], AF.Exp, scale=-0.5))
        p.bar()
        for f in range(2):
            p.op("tensor", lambda t, f=f: t.matmul(
                RBp[:, 0:512], orf[0:1, :], rrow[0:1, f * 512:(f + 1) * 512],
                start=True, stop=True))
            p.bar()
            p.op("vector", lambda v, f=f: v.tensor_copy(
                RBr[:, f * 512:(f + 1) * 512], RBp[:, 0:512]))
            p.bar()
        for kc in range(HKC):
            p.op("vector", lambda v, kc=kc: v.scalar_tensor_tensor(
                hh3[:, kc, :], hr3[:, kc, :], anw_c[:, kc:kc + 1], RBr[:],
                op0=ALU.mult, op1=ALU.mult))
        p.bar()
        free("RBr", "rrow", "sqb", "x_sb", "RBp", "SSr")

        qb = sb("qb", [128, HKC * 1024], BF16)
        qb3 = qb[:].rearrange("p (c t) -> p c t", t=1024)
        kb = sb("kb", [128, HKC * 1024], BF16)
        kb3 = kb[:].rearrange("p (c t) -> p c t", t=1024)
        vtok = sb("vtok", [128, 8 * 4096], BF16)
        vt3 = vtok[:].rearrange("p (a b) -> p a b", b=4096)
        PS8 = [ps(f"PS{i}", [128, 512]) for i in range(8)]
        wsl = sb("wsl", [128, HKC * 256], BF16, side="right")
        wl3 = wsl[:].rearrange("p (c m) -> p c m", m=256)
        qtmp = sb("qtmp", [128, 2 * 1024], side="right")
        qt3 = qtmp[:].rearrange("p (a b) -> p a b", b=1024)

        for which, W3s, dst3 in [(0, QW3, qb3), (1, KW3, kb3)]:
            for hd in range(8):
                p.dma("gpsimd", lambda g, W3s=W3s, hd=hd: g.dma_start(
                    out=wsl[:], in_=W3s[:, :, hd * 256:(hd + 1) * 256]))
                p.bar()
                for mc in range(2):
                    for f in range(2):
                        pu = PS8[mc * 2 + f]
                        for kc in range(HKC):
                            p.op("tensor", lambda t, mc=mc, f=f, kc=kc, pu=pu: t.matmul(
                                pu[:], wl3[:, kc, mc * 128:(mc + 1) * 128],
                                hh3[:, kc, f * 512:(f + 1) * 512],
                                start=(kc == 0), stop=(kc == HKC - 1)))
                p.bar()
                for mc in range(2):
                    for f in range(2):
                        p.op("vector", lambda v, mc=mc, f=f: v.tensor_copy(
                            qt3[:, mc, f * 512:(f + 1) * 512], PS8[mc * 2 + f][:]))
                p.bar()
                o1, o2 = dst3[:, 2 * hd, :], dst3[:, 2 * hd + 1, :]
                scl = 0.0625 if which == 0 else 1.0
                p.op("vector", lambda v, o1=o1: v.tensor_tensor(o1, qt3[:, 0, :], cosb[:], op=ALU.mult))
                p.op("vector", lambda v, o2=o2: v.tensor_tensor(o2, qt3[:, 1, :], cosb[:], op=ALU.mult))
                p.bar()
                p.op("vector", lambda v: v.tensor_tensor(qt3[:, 0, :], qt3[:, 0, :], sinb[:], op=ALU.mult))
                p.op("vector", lambda v: v.tensor_tensor(qt3[:, 1, :], qt3[:, 1, :], sinb[:], op=ALU.mult))
                p.bar()
                p.op("vector", lambda v, o1=o1: v.tensor_tensor(o1, o1, qt3[:, 1, :], op=ALU.subtract))
                p.op("vector", lambda v, o2=o2: v.tensor_tensor(o2, o2, qt3[:, 0, :], op=ALU.add))
                p.bar()
                if which == 0:
                    p.op("vector", lambda v, o1=o1: v.tensor_scalar(o1, o1, scl, None, op0=ALU.mult))
                    p.op("vector", lambda v, o2=o2: v.tensor_scalar(o2, o2, scl, None, op0=ALU.mult))
                    p.bar()
        for which, W3s in [(0, VW3), (1, GW3)]:
            for fs in range(16):
                p.dma("gpsimd", lambda g, W3s=W3s, fs=fs: g.dma_start(
                    out=wsl[:], in_=W3s[:, :, fs * 256:(fs + 1) * 256]))
                p.bar()
                for tt in range(8):
                    for kc in range(HKC):
                        p.op("tensor", lambda t, tt=tt, kc=kc: t.matmul(
                            PS8[tt][:, 0:256], hh3[:, kc, tt * 128:(tt + 1) * 128],
                            wl3[:, kc, :], start=(kc == 0), stop=(kc == HKC - 1)))
                p.bar()
                if which == 0:
                    for tt in range(8):
                        p.op("vector", lambda v, tt=tt, fs=fs: v.tensor_copy(
                            vt3[:, tt, fs * 256:(fs + 1) * 256], PS8[tt][:, 0:256]))
                    p.bar()
                else:
                    gsl = qtmp[:].bitcast(BF16)
                    for tt in range(8):
                        p.op("vector", lambda v, tt=tt: v.tensor_copy(
                            gsl[:, tt * 256:(tt + 1) * 256], PS8[tt][:, 0:256]))
                    p.bar()
                    p.dma("sync", lambda s, fs=fs: s.dma_start(
                        out=OG3[:, :, fs * 256:(fs + 1) * 256],
                        in_=gsl[:, 0:2048].rearrange("p (a b) -> p a b", b=256)))
                    p.bar()
        p.bar()
        qg3 = qtmp[:].bitcast(BF16)[:, 0:2048].rearrange("p (a b) -> p a b", b=1024)
        OQG3 = OQG.rearrange("p (c t) -> p c t", t=1024)
        for hd in range(8):
            for kc2 in range(2):
                p.op("vector", lambda v, hd=hd, kc2=kc2: v.tensor_tensor(
                    qg3[:, kc2, :], qb3[:, 2 * hd + kc2, :],
                    cdg[:, hd * 1024:(hd + 1) * 1024], op=ALU.mult))
            p.bar()
            p.dma("sync", lambda s, hd=hd: s.dma_start(
                out=OQG3[:, 2 * hd:2 * hd + 2, :], in_=qg3[:, :, :]))
            p.bar()
        free("qtmp", "wsl", "hh")

        sloc = sb("sloc", [128, 2 * 8 * 512], BF16)
        sl4 = sloc[:].rearrange("p (k h v) -> p k h v", k=2, h=8)
        vstg = sb("vstg", [64, 8 * 512], BF16)
        vs3 = vstg[:].rearrange("p (h v) -> p h v", v=512)
        kstg = sb("kstg", [64, 16 * 128], BF16)
        ks3 = kstg[:].rearrange("p (a b) -> p a b", b=128)
        qcd = sb("qcd", [128, 16 * 64], BF16)
        qc3 = qcd[:].rearrange("p (a b) -> p a b", b=64)
        kkd = sb("kkd", [128, 16 * 64], BF16)
        kk3 = kkd[:].rearrange("p (a b) -> p a b", b=64)
        attns = sb("attns", [64, 8 * 64], BF16)
        at3 = attns[:].rearrange("p (a b) -> p a b", b=64)
        obuf = sb("obuf", [128, 4096], BF16)
        p.op("gpsimd", lambda g: g.memset(sloc[:], 0.0))
        p.bar()
        for c in range(16):
            tt = c // 2
            pb = (c % 2) * 64
            for hd in range(8):
                p.op("vector", lambda v, tt=tt, pb=pb, hd=hd: v.tensor_copy(
                    vs3[:, hd, :], vt3[pb:pb + 64, tt, hd * 512:(hd + 1) * 512]))
                for kc2 in range(2):
                    p.op("vector", lambda v, tt=tt, pb=pb, hd=hd, kc2=kc2: v.tensor_tensor(
                        qc3[:, 2 * hd + kc2, :],
                        qb3[:, 2 * hd + kc2, tt * 128 + pb:tt * 128 + pb + 64],
                        cdq[:, hd * 64:(hd + 1) * 64], op=ALU.mult))
                    p.op("vector", lambda v, tt=tt, pb=pb, hd=hd, kc2=kc2: v.tensor_tensor(
                        kk3[:, 2 * hd + kc2, :],
                        kb3[:, 2 * hd + kc2, tt * 128 + pb:tt * 128 + pb + 64],
                        kdc[:, hd * 64:(hd + 1) * 64], op=ALU.mult))
            p.bar()
            for hd in range(8):
                for kc2 in range(2):
                    p.op("tensor", lambda t, tt=tt, pb=pb, hd=hd, kc2=kc2: t.matmul(
                        PS8[0][0:64, hd * 64:(hd + 1) * 64],
                        kb3[:, 2 * hd + kc2, tt * 128 + pb:tt * 128 + pb + 64],
                        qb3[:, 2 * hd + kc2, tt * 128 + pb:tt * 128 + pb + 64],
                        start=(kc2 == 0), stop=(kc2 == 1)))
                for kc2 in range(2):
                    u = 2 * hd + kc2
                    p.op("tensor", lambda t, u=u: t.transpose(
                        PS8[1 + u // 8][0:64, (u % 8) * 64:(u % 8) * 64 + 64].bitcast(BF16),
                        kk3[:, u, :], idb[:]))
            p.bar()
            for hd in range(8):
                p.op("vector", lambda v, hd=hd: v.tensor_tensor(
                    at3[:, hd, :], PS8[0][0:64, hd * 64:(hd + 1) * 64],
                    dmt[:, hd * 64:(hd + 1) * 64], op=ALU.mult))
            p.op("vector", lambda v: v.tensor_copy(
                kstg[:, 0:1024], PS8[1][0:64, :].bitcast(BF16)))
            p.op("vector", lambda v: v.tensor_copy(
                kstg[:, 1024:2048], PS8[2][0:64, :].bitcast(BF16)))
            p.bar()
            for half in range(2):
                for hh_ in range(4):
                    hd = half * 4 + hh_
                    pu = PS8[2 + hh_]
                    p.op("tensor", lambda t, hd=hd, pu=pu: t.matmul(
                        pu[0:64, 0:512], at3[:, hd, :], vs3[:, hd, :],
                        start=True, stop=False))
                    p.op("tensor", lambda t, hd=hd, pu=pu: t.matmul(
                        pu[0:64, 0:512], qc3[:, 2 * hd, :], sl4[:, 0, hd, :],
                        start=False, stop=False))
                    p.op("tensor", lambda t, hd=hd, pu=pu: t.matmul(
                        pu[0:64, 0:512], qc3[:, 2 * hd + 1, :], sl4[:, 1, hd, :],
                        start=False, stop=True))
                p.bar()
                for hh_ in range(4):
                    hd = half * 4 + hh_
                    p.op("vector", lambda v, pb=pb, hd=hd, hh_=hh_: v.tensor_copy(
                        obuf[pb:pb + 64, hd * 512:(hd + 1) * 512], PS8[2 + hh_][0:64, 0:512]))
                p.bar()
            for grp in range(4):
                for j in range(4):
                    u = grp * 4 + j
                    hd, kc2 = divmod(u, 2)
                    p.op("tensor", lambda t, hd=hd, kc2=kc2, j=j: t.matmul(
                        PS8[2 + j][:, 0:512], ks3[:, 2 * hd + kc2, :], vs3[:, hd, :],
                        start=True, stop=True))
                p.bar()
                for j in range(4):
                    u = grp * 4 + j
                    hd, kc2 = divmod(u, 2)
                    p.op("vector", lambda v, hd=hd, kc2=kc2, j=j: v.scalar_tensor_tensor(
                        sl4[:, kc2, hd, :], sl4[:, kc2, hd, :], cd64[:, hd:hd + 1],
                        PS8[2 + j][:, 0:512], op0=ALU.mult, op1=ALU.add))
                p.bar()
            if c % 2 == 1:
                p.dma("sync", lambda s, tt=tt: s.dma_start(
                    out=OOP3[:, tt, :], in_=obuf[:]))
                p.bar()
        p.dma("sync", lambda s: s.dma_start(out=OSR, in_=sloc[:]))
        _cleanup(cms)
        _fin(p, ctx)
    return nc


def stage_st4a(inputs, r3, mods):
    if r3 is None:
        r3 = [dict(x2=np.zeros((1, 1), np.float32))] * NCORES
        mods = {1: [dict(ca=np.zeros((128, HKC), np.float32),
                         sa=np.zeros((128, HKC), np.float32))] * 2}
    NH, DK = 8, 256
    gamma = 1.0 - np.exp2(-5.0 - np.arange(NH, dtype=np.float64))
    i64 = np.arange(64, dtype=np.float64)
    cdq = np.zeros((128, 8 * 64), np.float32)
    kdc = np.zeros((128, 8 * 64), np.float32)
    cdg = np.zeros((128, 8 * 1024), np.float32)
    dmt = np.zeros((64, 8 * 64), np.float32)
    for h in range(NH):
        cdq[:, h * 64:(h + 1) * 64] = (gamma[h] ** (i64 + 1.0))[None, :]
        kdc[:, h * 64:(h + 1) * 64] = (gamma[h] ** (63.0 - i64))[None, :]
        tloc = np.arange(1024, dtype=np.float64)
        cdg[:, h * 1024:(h + 1) * 1024] = (gamma[h] ** (tloc + 1.0))[None, :]
        diff = i64[None, :] - i64[:, None]
        dmt[:, h * 64:(h + 1) * 64] = np.where(diff >= 0, gamma[h] ** diff, 0.0)
    cd64 = np.broadcast_to((gamma ** 64.0).astype(np.float32), (128, 8)).copy()
    inv_freq = 1.0 / 10000.0 ** (np.arange(0, DK, 2, dtype=np.float64) / DK)
    QW = km(np.asarray(inputs["L1_attn_q_w"]))
    KW = km(np.asarray(inputs["L1_attn_k_w"]))
    VW = km(np.asarray(inputs["L1_attn_v_w"]))
    GW = km(np.asarray(inputs["L1_attn_g_w"]))
    anw = col16(np.asarray(inputs["L1_attn_attn_norm_w"]))
    identbf = np.eye(128).astype(BF)
    maps = []
    for core in range(NCORES):
        b, q = divmod(core, 4)
        tg = q * 1024 + np.arange(1024, dtype=np.float64)
        ang = tg[None, :] * inv_freq[:, None]
        maps.append(dict(
            x2=np.asarray(r3[core]["x2"]),
            ca1=mods[1][b]["ca"], sa1=mods[1][b]["sa"], anw=anw,
            qw=QW, kw=KW, vw=VW, gw=GW,
            cosx=np.cos(ang).astype(np.float32), sinx=np.sin(ang).astype(np.float32),
            cdq=cdq, kdc=kdc, cdg=cdg.astype(BF), cd64=cd64, dmaskT=dmt,
            identbf=identbf,
        ))
    return maps



def build_st4b():
    nc = _mknc()
    OP = nc.dram_tensor("opart", [128, 8 * 4096], BF16, kind="ExternalInput").ap()
    QG = nc.dram_tensor("qg", [128, HKC * 1024], BF16, kind="ExternalInput").ap()
    GT = nc.dram_tensor("gtok", [128, 8 * 4096], BF16, kind="ExternalInput").ap()
    SRA = nc.dram_tensor("soutr_g", [1024, 8192], BF16, kind="ExternalInput").ap()
    WC = nc.dram_tensor("wcols", [128, 64], F32, kind="ExternalInput").ap()
    GNW = nc.dram_tensor("gnw", [1, 512], F32, kind="ExternalInput").ap()
    IDB = nc.dram_tensor("identbf", [128, 128], BF16, kind="ExternalInput").ap()
    OOGF = nc.dram_tensor("ogf", [128, 32 * 1024], BF16, kind="ExternalOutput").ap()

    OP3 = OP.rearrange("p (a b) -> p a b", b=4096)
    GT3 = GT.rearrange("p (a b) -> p a b", b=4096)
    QG3 = QG.rearrange("p (c t) -> p c t", t=1024)
    OOGF3 = OOGF.rearrange("p (a b) -> p a b", b=1024)

    with ExitStack() as ctx:
        cms = {}
        def sb(n, s, d=F32, side=None):
            n = n + "_s"
            cm = nc.sbuf_tensor(n, s, d, side=side)
            h = cm.__enter__(); cms[n] = cm; return h
        def ps(n, s, d=F32):
            n = n + "_s"
            cm = nc.psum_tensor(n, s, d)
            h = cm.__enter__(); cms[n] = cm; return h
        def free(*names):
            for n in names:
                cms.pop(n + "_s").__exit__(None, None, None)

        orf = sb("orf", [1, 128]); eps2 = sb("eps2", [128, 1])
        wc = sb("wc", [128, 64]); gnwr = sb("gnwr", [1, 512])
        idb = sb("idb", [128, 128], BF16)
        qg = sb("qg", [128, HKC * 1024], BF16)
        qg3 = qg[:].rearrange("p (c t) -> p c t", t=1024)
        sicr = sb("sicr", [128, 8192], BF16)
        sic4 = sicr[:].rearrange("p (k h v) -> p k h v", k=2, h=8)
        gnwb = sb("gnwb", [128, 512])
        obuf = sb("obuf", [128, 4096])
        opb = sb("opb", [128, 4096], BF16)
        onb = sb("onb", [128, 4096], BF16)
        gbuf = sb("gbuf", [128, 4096], BF16)
        sgb = sb("sgb", [128, 4096], BF16)
        sqscr = sb("sqscr", [128, 512], BF16)
        ssq8 = sb("ssq8", [128, 8]); rstd8 = sb("rstd8", [128, 8])
        ogsl = sb("ogsl", [128, 32 * 128], BF16)
        PS8 = [ps(f"PS{i}", [128, 512]) for i in range(8)]

        p = _mkp(nc)
        for dst, src in [(wc, WC), (gnwr, GNW), (idb, IDB), (qg, QG)]:
            p.dma("sync", lambda s, dst=dst, src=src: s.dma_start(out=dst[:], in_=src))
        p.op("gpsimd", lambda g: g.memset(orf[:], 1.0))
        p.op("gpsimd", lambda g: g.memset(eps2[:], 1e-6))
        p.bar()
        p.op("tensor", lambda t: t.matmul(PS8[0][:, 0:512], orf[0:1, :], gnwr[0:1, :],
                                          start=True, stop=True))
        p.bar()
        p.op("vector", lambda v: v.tensor_copy(gnwb[:], PS8[0][:, 0:512]))
        p.bar()
        sicp = sb("sicp", [128, 8192], side="right")
        sbj = sb("sbj", [128, 2 * 8192], BF16, side="right")
        sbj3 = sbj[:].rearrange("p (a b) -> p a b", b=8192)
        p.op("gpsimd", lambda g: g.memset(sicp[:], 0.0))
        p.dma("sync", lambda s: s.dma_start(out=sbj3[:, 0, :], in_=SRA[0:128, :]))
        p.bar()
        for j in range(8):
            if j < 7:
                p.dma("sync", lambda s, j=j: s.dma_start(
                    out=sbj3[:, (j + 1) % 2, :], in_=SRA[(j + 1) * 128:(j + 2) * 128, :]))
            for u in range(16):
                p.op("vector", lambda v, j=j, u=u: v.scalar_tensor_tensor(
                    sicp[:, u * 512:(u + 1) * 512],
                    sbj3[:, j % 2, u * 512:(u + 1) * 512],
                    wc[:, j * 8 + u % 8:j * 8 + u % 8 + 1],
                    sicp[:, u * 512:(u + 1) * 512],
                    op0=ALU.mult, op1=ALU.add))
            p.bar()
        p.op("vector", lambda v: v.tensor_copy(sicr[:], sicp[:]))
        p.bar()
        free("sbj", "sicp")

        for tt in range(8):
            p.dma("sync", lambda s, tt=tt: s.dma_start(out=opb[:], in_=OP3[:, tt, :]))
            p.dma("sync", lambda s, tt=tt: s.dma_start(out=gbuf[:], in_=GT3[:, tt, :]))
            p.bar()
            for hd in range(8):
                for kc2 in range(2):
                    p.op("tensor", lambda t, tt=tt, hd=hd, kc2=kc2: t.matmul(
                        PS8[hd][:, 0:512],
                        qg3[:, 2 * hd + kc2, tt * 128:(tt + 1) * 128],
                        sic4[:, kc2, hd, :],
                        start=(kc2 == 0), stop=(kc2 == 1)))
            p.bar()
            for hd in range(8):
                hs = slice(hd * 512, (hd + 1) * 512)
                p.op("vector", lambda v, hd=hd, hs=hs: v.tensor_tensor(
                    obuf[:, hs], PS8[hd][:, 0:512], opb[:, hs], op=ALU.add))
            p.bar()
            for hd in range(8):
                hs = slice(hd * 512, (hd + 1) * 512)
                p.op("scalar", lambda s, hd=hd, hs=hs: s.activation(
                    sqscr[:], obuf[:, hs], AF.Square, accum_out=ssq8[:, hd:hd + 1]))
            p.bar()
            p.op("scalar", lambda s: s.activation(rstd8[:], ssq8[:], AF.Ln,
                                                  scale=1.0 / 512, bias=eps2[:, 0:1]))
            p.bar()
            p.op("scalar", lambda s: s.activation(rstd8[:], rstd8[:], AF.Exp, scale=-0.5))
            p.op("scalar", lambda s2: s2.activation(sgb[:], gbuf[:], AF.Sigmoid))
            p.bar()
            for hd in range(8):
                hs = slice(hd * 512, (hd + 1) * 512)
                p.op("vector", lambda v, hd=hd, hs=hs: v.tensor_scalar(
                    onb[:, hs], obuf[:, hs], rstd8[:, hd:hd + 1], None, op0=ALU.mult))
            p.bar()
            for hd in range(8):
                hs = slice(hd * 512, (hd + 1) * 512)
                p.op("vector", lambda v, hd=hd, hs=hs: v.tensor_tensor(
                    onb[:, hs], onb[:, hs], gnwb[:], op=ALU.mult))
            p.bar()
            p.op("vector", lambda v: v.tensor_tensor(sgb[:], sgb[:], gbuf[:], op=ALU.mult))
            p.bar()
            p.op("vector", lambda v: v.tensor_tensor(onb[:], onb[:], sgb[:], op=ALU.mult))
            p.bar()
            for grp in range(4):
                for i in range(8):
                    fc = grp * 8 + i
                    p.op("tensor", lambda t, fc=fc, i=i: t.transpose(
                        PS8[4 + i // 2][:, (i % 2) * 64:(i % 2) * 64 + 64].bitcast(BF16),
                        onb[:, fc * 128:(fc + 1) * 128], idb[:]))
                p.bar()
                for i in range(8):
                    fc = grp * 8 + i
                    p.op("vector", lambda v, fc=fc, i=i: v.tensor_copy(
                        ogsl[:, fc * 128:(fc + 1) * 128],
                        PS8[4 + i // 2][:, (i % 2) * 64:(i % 2) * 64 + 64].bitcast(BF16)))
                p.bar()
            p.dma("sync", lambda s, tt=tt: s.dma_start(
                out=OOGF3[:, :, tt * 128:(tt + 1) * 128],
                in_=ogsl[:].rearrange("p (a b) -> p a b", b=128)))
            p.bar()
        _cleanup(cms)
        _fin(p, ctx)
    return nc


def stage_st4b(inputs, r4a, mods):
    NH = 8
    gamma = 1.0 - np.exp2(-5.0 - np.arange(NH, dtype=np.float64))
    sra = np.concatenate([np.asarray(r4a[j]["soutr"]) for j in range(8)], axis=0)
    gnw = np.asarray(inputs["L1_attn_gnorm_w"]).reshape(1, 512)
    identbf = np.eye(128).astype(BF)
    maps = []
    for core in range(NCORES):
        b, q = divmod(core, 4)
        wc = np.zeros((8, 8), np.float64)
        for j in range(4 * b, 4 * b + q):
            steps = q - 1 - (j - 4 * b)
            wc[j] = gamma ** (1024.0 * steps)
        wcols = np.zeros((128, 64), np.float32)
        for j in range(8):
            wcols[:, j * 8:(j + 1) * 8] = wc[j][None, :]
        maps.append(dict(
            opart=np.asarray(r4a[core]["opart"]), qg=np.asarray(r4a[core]["qg"]),
            gtok=np.asarray(r4a[core]["gtok"]), soutr_g=sra,
            wcols=wcols, gnw=gnw, identbf=identbf,
        ))
    return maps



def build_st4c():
    nc = _mknc()
    OGF = nc.dram_tensor("ogf", [128, 32 * 1024], BF16, kind="ExternalInput").ap()
    H = nc.dram_tensor("hres", [128, HKC * 1024], F32, kind="ExternalInput").ap()
    X = nc.dram_tensor("x2", [128, HKC * 1024], F32, kind="ExternalInput").ap()
    OW = nc.dram_tensor("ow", [128, 32 * 2048], F32, kind="ExternalInput").ap()
    MNW = nc.dram_tensor("mnw", [128, HKC], F32, kind="ExternalInput").ap()
    GAW = nc.dram_tensor("gatew", [128, HKC * 11264], F32, kind="ExternalInput").ap()
    DWW = nc.dram_tensor("downw", [128, 44 * 2048], F32, kind="ExternalInput").ap()
    GA = nc.dram_tensor("ga1", [128, HKC], F32, kind="ExternalInput").ap()
    OX = nc.dram_tensor("x3", [128, HKC * 1024], F32, kind="ExternalOutput").ap()

    OW3 = OW.rearrange("p (c m) -> p c m", m=2048)
    GAW3 = GAW.rearrange("p (c m) -> p c m", m=11264)
    DWW3 = DWW.rearrange("p (c m) -> p c m", m=2048)
    OX3 = OX.rearrange("p (c t) -> p c t", t=1024)
    X3d = X.rearrange("p (c t) -> p c t", t=1024)
    H3d = H.rearrange("p (c t) -> p c t", t=1024)

    with ExitStack() as ctx:
        cms = {}
        def sb(n, s, d=F32, side=None):
            n = n + "_s"
            cm = nc.sbuf_tensor(n, s, d, side=side)
            h = cm.__enter__(); cms[n] = cm; return h
        def ps(n, s, d=F32):
            n = n + "_s"
            cm = nc.psum_tensor(n, s, d)
            h = cm.__enter__(); cms[n] = cm; return h
        def free(*names):
            for n in names:
                cms.pop(n + "_s").__exit__(None, None, None)

        ocb = sb("ocb", [128, 1], BF16)
        ocf = sb("ocf", [128, 1]); orf = sb("orf", [1, 128])
        eps2 = sb("eps2", [1, 1])
        mnw_c = sb("mnw_c", [128, HKC]); ga_c = sb("ga_c", [128, HKC])
        r2 = sb("r2", [128, HKC * 1024])
        r23 = r2[:].rearrange("p (c t) -> p c t", t=1024)
        h3b = sb("h3b", [128, HKC * 1024], BF16)
        h33 = h3b[:].rearrange("p (c t) -> p c t", t=1024)
        hff = sb("hff", [128, 11 * 1024], BF16)
        hf3 = hff[:].rearrange("p (c t) -> p c t", t=1024)
        PS8 = [ps(f"PS{i}", [128, 512]) for i in range(8)]

        p = _mkp(nc)
        p.op("gpsimd", lambda g: g.memset(ocb[:], 1.0))
        p.op("gpsimd", lambda g: g.memset(ocf[:], 1.0))
        p.op("gpsimd", lambda g: g.memset(orf[:], 1.0))
        p.op("gpsimd", lambda g: g.memset(eps2[:], 1e-6))
        p.dma("sync", lambda s: s.dma_start(out=mnw_c[:], in_=MNW))
        p.dma("sync", lambda s: s.dma_start(out=ga_c[:], in_=GA))
        p.bar()
        ogf = sb("ogf", [128, 32 * 1024], BF16, side="right")
        yg3 = ogf[:].rearrange("p (c t) -> p c t", t=1024)
        wsl = sb("wsl", [128, 32 * 256], BF16, side="right")
        ws3 = wsl[:].rearrange("p (c m) -> p c m", m=256)
        hbuf = sb("hbuf", [128, 2 * 1024], side="right")
        hb3 = hbuf[:].rearrange("p (c t) -> p c t", t=1024)
        p.dma("sync", lambda s: s.dma_start(out=ogf[:], in_=OGF))
        p.bar()
        for ms in range(8):
            p.dma("gpsimd", lambda g, ms=ms: g.dma_start(
                out=wsl[:], in_=OW3[:, :, ms * 256:(ms + 1) * 256]))
            p.dma("sync", lambda s, ms=ms: s.dma_start(
                out=hbuf[:], in_=H3d[:, ms * 2:(ms + 1) * 2, :].rearrange("p c t -> p (c t)")))
            p.bar()
            for mc in range(2):
                for tt in range(8):
                    pu = PS8[mc * 2 + tt // 4][:, (tt % 4) * 128:(tt % 4) * 128 + 128]
                    for kc in range(32):
                        p.op("tensor", lambda t, mc=mc, tt=tt, kc=kc, pu=pu: t.matmul(
                            pu, ws3[:, kc, mc * 128:(mc + 1) * 128],
                            yg3[:, kc, tt * 128:(tt + 1) * 128],
                            start=(kc == 0), stop=(kc == 31)))
            p.bar()
            for mc in range(2):
                for tt in range(8):
                    pu = PS8[mc * 2 + tt // 4][:, (tt % 4) * 128:(tt % 4) * 128 + 128]
                    p.op("vector", lambda v, ms=ms, mc=mc, tt=tt, pu=pu: v.tensor_tensor(
                        r23[:, ms * 2 + mc, tt * 128:(tt + 1) * 128], pu,
                        hb3[:, mc, tt * 128:(tt + 1) * 128], op=ALU.add))
            p.bar()
        free("hbuf", "wsl", "ogf")
        sqb = sb("sqb", [128, HKC * 1024], BF16, side="right")
        sq3 = sqb[:].rearrange("p (c t) -> p c t", t=1024)
        rrow = sb("rrow", [1, 1024], side="right")
        RBr = sb("RBr", [128, 1024], side="right")
        p.op("scalar", lambda s: s.activation(sqb[:], r2[:], AF.Square))
        p.bar()
        for f in range(2):
            for kc in range(HKC):
                p.op("tensor", lambda t, f=f, kc=kc: t.matmul(
                    PS8[0][0:1, 0:512], ocb[:, 0:1], sq3[:, kc, f * 512:(f + 1) * 512],
                    start=(kc == 0), stop=(kc == HKC - 1)))
            p.bar()
            p.op("scalar", lambda s, f=f: s.activation(
                rrow[0:1, f * 512:(f + 1) * 512], PS8[0][0:1, 0:512], AF.Ln,
                scale=1.0 / 2048, bias=eps2[0:1, 0:1]))
            p.bar()
        p.op("scalar", lambda s: s.activation(rrow[:], rrow[:], AF.Exp, scale=-0.5))
        p.bar()
        for f in range(2):
            p.op("tensor", lambda t, f=f: t.matmul(
                PS8[1][:, 0:512], orf[0:1, :], rrow[0:1, f * 512:(f + 1) * 512],
                start=True, stop=True))
            p.bar()
            p.op("vector", lambda v, f=f: v.tensor_copy(
                RBr[:, f * 512:(f + 1) * 512], PS8[1][:, 0:512]))
            p.bar()
        for kc in range(HKC):
            p.op("vector", lambda v, kc=kc: v.scalar_tensor_tensor(
                h33[:, kc, :], r23[:, kc, :], mnw_c[:, kc:kc + 1], RBr[:],
                op0=ALU.mult, op1=ALU.mult))
        p.bar()
        free("RBr", "rrow", "sqb")
        wga = sb("wga", [128, HKC * 256], BF16, side="right")
        wg3 = wga[:].rearrange("p (c m) -> p c m", m=256)
        wdw = sb("wdw", [128, 11 * 2048], BF16, side="right")
        wd3 = wdw[:].rearrange("p (c m) -> p c m", m=2048)
        for slab in range(4):
            for ic in range(11):
                g = slab * 11 + ic
                p.dma("gpsimd", lambda g_, g=g: g_.dma_start(
                    out=wga[:, 0:HKC * 128], in_=GAW3[:, :, g * 128:(g + 1) * 128]))
                p.dma("gpsimd", lambda g_, g=g: g_.dma_start(
                    out=wga[:, HKC * 128:], in_=GAW3[:, :, 5632 + g * 128:5632 + (g + 1) * 128]))
                p.bar()
                wg4 = wga[:].rearrange("p (u c m) -> p u c m", u=2, m=128)
                for f in range(2):
                    for kc in range(HKC):
                        p.op("tensor", lambda t, f=f, kc=kc, wg4=wg4: t.matmul(
                            PS8[f][:, 0:512], wg4[:, 0, kc, :],
                            h33[:, kc, f * 512:(f + 1) * 512],
                            start=(kc == 0), stop=(kc == HKC - 1)))
                    for kc in range(HKC):
                        p.op("tensor", lambda t, f=f, kc=kc, wg4=wg4: t.matmul(
                            PS8[2 + f][:, 0:512], wg4[:, 1, kc, :],
                            h33[:, kc, f * 512:(f + 1) * 512],
                            start=(kc == 0), stop=(kc == HKC - 1)))
                p.bar()
                for f in range(2):
                    fs = slice(f * 512, (f + 1) * 512)
                    p.op("scalar", lambda s, ic=ic, f=f, fs=fs: s.activation(
                        hf3[:, ic, fs], PS8[f][:, 0:512], AF.Silu))
                p.bar()
                for f in range(2):
                    fs = slice(f * 512, (f + 1) * 512)
                    p.op("vector", lambda v, ic=ic, f=f, fs=fs: v.tensor_tensor(
                        hf3[:, ic, fs], hf3[:, ic, fs], PS8[2 + f][:, 0:512], op=ALU.mult))
                p.bar()
            p.dma("gpsimd", lambda g_, slab=slab: g_.dma_start(
                out=wdw[:], in_=DWW3[:, slab * 11:(slab + 1) * 11, :].rearrange("p c m -> p (c m)")))
            p.bar()
            for mo in range(HKC):
                for f in range(2):
                    pu = PS8[4 + (mo * 2 + f) % 4]
                    for kc in range(11):
                        p.op("tensor", lambda t, mo=mo, f=f, kc=kc, pu=pu: t.matmul(
                            pu[:], wd3[:, kc, mo * 128:(mo + 1) * 128],
                            hf3[:, kc, f * 512:(f + 1) * 512],
                            start=(kc == 0), stop=(kc == 10)))
                    if (mo * 2 + f) % 4 == 3:
                        p.bar()
                        for u in range(mo * 2 + f - 3, mo * 2 + f + 1):
                            uo, uf = divmod(u, 2)
                            p.op("vector", lambda v, uo=uo, uf=uf, u=u: v.tensor_tensor(
                                r23[:, uo, uf * 512:(uf + 1) * 512],
                                r23[:, uo, uf * 512:(uf + 1) * 512],
                                PS8[4 + u % 4][:], op=ALU.add))
                        p.bar()
        xbuf = sb("xbuf", [128, 2 * 1024], side="right")
        for kc in range(HKC):
            p.dma("sync", lambda s, kc=kc: s.dma_start(
                out=xbuf[:, kc % 2 * 1024:(kc % 2 + 1) * 1024], in_=X3d[:, kc, :]))
            p.bar()
            p.op("vector", lambda v, kc=kc: v.scalar_tensor_tensor(
                r23[:, kc, :], r23[:, kc, :], ga_c[:, kc:kc + 1],
                xbuf[:, kc % 2 * 1024:(kc % 2 + 1) * 1024], op0=ALU.mult, op1=ALU.add))
            p.bar()
            p.dma("sync", lambda s, kc=kc: s.dma_start(out=OX3[:, kc, :], in_=r23[:, kc, :]))
        p.bar()
        _cleanup(cms)
        _fin(p, ctx)
    return nc


def stage_st4c(inputs, r3, r4a, r4b, mods):
    OW = km(np.asarray(inputs["L1_attn_o_w"]))
    GAW = km(np.asarray(inputs["L1_attn_gate_w"]))
    DWW = km(np.asarray(inputs["L1_attn_down_w"]))
    mnw = col16(np.asarray(inputs["L1_attn_mlp_norm_w"]))
    maps = []
    for core in range(NCORES):
        b, q = divmod(core, 4)
        maps.append(dict(
            ogf=np.asarray(r4b[core]["ogf"]), hres=np.asarray(r4a[core]["hres"]),
            x2=np.asarray(r3[core]["x2"]), ow=OW, mnw=mnw,
            gatew=GAW, downw=DWW, ga1=mods[1][b]["ga"],
        ))
    return maps



def build_st5a():
    nc = _mknc()
    GUP = nc.dram_tensor("gup", [128, 11 * 256], I32, kind="ExternalInput").ap()
    GSC = nc.dram_tensor("gsc", [128, 11 * 16], F16, kind="ExternalInput").ap()
    DPK = nc.dram_tensor("dpk", [128, 2 * 704], I32, kind="ExternalInput").ap()
    DSC = nc.dram_tensor("dsc", [128, 2 * 44], F16, kind="ExternalInput").ap()
    IDB = nc.dram_tensor("identbf", [128, 128], BF16, kind="ExternalInput").ap()
    OWGp = [nc.dram_tensor(f"wgtc{pp}", [128, 4 * 1408], BF16,
                           kind="ExternalOutput").ap() for pp in range(4)]
    OWDp = [nc.dram_tensor(f"wdtc{pp}", [128, 11 * 256], BF16,
                           kind="ExternalOutput").ap() for pp in range(4)]
    GUP3 = GUP.rearrange("p (c m) -> p c m", m=256)
    DPK3 = DPK.rearrange("p (c m) -> p c m", m=704)

    with ExitStack() as ctx:
        cms = {}
        def sb(n, s, d=F32, side=None):
            n = n + "_s"
            cm = nc.sbuf_tensor(n, s, d, side=side)
            h = cm.__enter__(); cms[n] = cm; return h
        def ps(n, s, d=F32):
            n = n + "_s"
            cm = nc.psum_tensor(n, s, d)
            h = cm.__enter__(); cms[n] = cm; return h

        idb = sb("idb", [128, 128], BF16)
        gup = sb("gup", [128, 11 * 256], I32)
        gu3 = gup[:].rearrange("p (c m) -> p c m", m=256)
        gsch = sb("gsch", [128, 11 * 16], F16)
        gsc = sb("gsc", [128, 11 * 16])
        gs3 = gsc[:].rearrange("p (c m) -> p c m", m=16)
        dpk = sb("dpk", [128, 2 * 704], I32)
        dp3 = dpk[:].rearrange("p (c m) -> p c m", m=704)
        dsch = sb("dsch", [128, 2 * 44], F16)
        dsc = sb("dsc", [128, 2 * 44])
        ds3 = dsc[:].rearrange("p (c m) -> p c m", m=44)
        W = 704
        c128 = sb("c128", [128, W], I32)
        m4 = sb("m4", [128, 8 * W], I32)
        m43 = m4[:].rearrange("p (a b) -> p a b", b=W)
        bits = sb("bits", [128, 8 * W], I32)
        bt3 = bits[:].rearrange("p (a b) -> p a b", b=W)
        sgn = sb("sgn", [128, 8 * W], I32)
        sg3 = sgn[:].rearrange("p (a b) -> p a b", b=W)
        fsc = sb("fsc", [128, 8 * W])
        fs3 = fsc[:].rearrange("p (a b) -> p a b", b=W)
        ind = sb("ind", [128, 8 * W])
        in3 = ind[:].rearrange("p (a b) -> p a b", b=W)
        v1b = sb("v1b", [128, 8 * W])
        v13 = v1b[:].rearrange("p (a b) -> p a b", b=W)
        wnat = sb("wnat", [128, 5632])
        wbf = sb("wbf", [128, 5632], BF16)
        slab = sb("slab", [128, 44 * 128], BF16)
        PS8 = [ps(f"PS{i}", [128, 512]) for i in range(8)]

        p = _mkp(nc)
        p.dma("sync", lambda s: s.dma_start(out=idb[:], in_=IDB))
        p.dma("sync", lambda s: s.dma_start(out=gup[:], in_=GUP))
        p.dma("sync", lambda s: s.dma_start(out=gsch[:], in_=GSC))
        p.dma("sync", lambda s: s.dma_start(out=dpk[:], in_=DPK))
        p.dma("sync", lambda s: s.dma_start(out=dsch[:], in_=DSC))
        p.op("gpsimd", lambda g: g.memset(c128[:], 1056964608))
        p.bar()
        p.op("vector", lambda v: v.tensor_copy(gsc[:], gsch[:]))
        p.op("vector", lambda v: v.tensor_copy(dsc[:], dsch[:]))
        p.bar()

        def dequant(pk_ap, width, nh, scol_fn, ngrp, out3, nchunks_t):
            wn = wnat[:].rearrange("(p) (a) -> p a") if False else wnat
            for j in range(8):
                p.op("vector", lambda v, j=j: v.tensor_scalar(
                    m43[:, j, 0:width], pk_ap, 4 * j, 7,
                    op0=ALU.logical_shift_right, op1=ALU.bitwise_and))
                p.op("vector", lambda v, j=j: v.tensor_scalar(
                    sg3[:, j, 0:width], pk_ap, 4 * j + 3, 1,
                    op0=ALU.logical_shift_right, op1=ALU.bitwise_and))
            p.bar()
            for j in range(8):
                p.op("vector", lambda v, j=j: v.tensor_scalar(
                    bt3[:, j, 0:width], m43[:, j, 0:width], 22, None,
                    op0=ALU.logical_shift_left))
                p.op("vector", lambda v, j=j: v.tensor_scalar(
                    sg3[:, j, 0:width], sg3[:, j, 0:width], -2, 1,
                    op0=ALU.mult, op1=ALU.add))
                p.op("vector", lambda v, j=j: v.tensor_scalar(
                    m43[:, j, 0:width], m43[:, j, 0:width], 2, None, op0=ALU.is_lt))
            p.bar()
            for j in range(8):
                p.op("vector", lambda v, j=j: v.tensor_tensor(
                    bt3[:, j, 0:width], bt3[:, j, 0:width], c128[:, 0:width], op=ALU.add))
            p.bar()
            for j in range(8):
                p.op("vector", lambda v, j=j: v.tensor_copy(
                    fs3[:, j, 0:width], sg3[:, j, 0:width]))
                p.op("vector", lambda v, j=j: v.tensor_copy(
                    in3[:, j, 0:width], m43[:, j, 0:width]))
            p.bar()
            for j in range(8):
                p.op("vector", lambda v, j=j: v.scalar_tensor_tensor(
                    v13[:, j, 0:width], in3[:, j, 0:width], 1.0,
                    bt3[:, j, 0:width].bitcast(F32), op0=ALU.add, op1=ALU.mult))
            p.bar()
            for j in range(8):
                p.op("vector", lambda v, j=j: v.tensor_tensor(
                    v13[:, j, 0:width], v13[:, j, 0:width], in3[:, j, 0:width],
                    op=ALU.subtract))
            p.bar()
            import os as _os
            if _os.environ.get("ST5A_NOSTRIDE"):
                for j in range(8):
                    p.op("vector", lambda v, j=j, width=width: v.tensor_tensor(
                        wnat[:, j * width:(j + 1) * width], v13[:, j, 0:width],
                        fs3[:, j, 0:width], op=ALU.mult))
            else:
                for j in range(8):
                    p.op("vector", lambda v, j=j, width=width: v.tensor_tensor(
                        wnat[:, j:width * 8:8], v13[:, j, 0:width], fs3[:, j, 0:width],
                        op=ALU.mult))
            p.bar()
            for g in range(ngrp):
                p.op("vector", lambda v, g=g: v.tensor_scalar(
                    wbf[:, g * 128:(g + 1) * 128], wnat[:, g * 128:(g + 1) * 128],
                    scol_fn(g), None, op0=ALU.mult))
            p.bar()
            for grp in range((ngrp + 7) // 8):
                for i in range(min(8, ngrp - grp * 8)):
                    hc = grp * 8 + i
                    p.op("tensor", lambda t, hc=hc, i=i: t.transpose(
                        PS8[4 + i // 2][:, (i % 2) * 64:(i % 2) * 64 + 64].bitcast(BF16),
                        wbf[:, hc * 128:(hc + 1) * 128], idb[:]))
                p.bar()
                for i in range(min(8, ngrp - grp * 8)):
                    hc = grp * 8 + i
                    p.op("vector", lambda v, hc=hc, i=i: v.tensor_copy(
                        slab[:, hc * 128:(hc + 1) * 128],
                        PS8[4 + i // 2][:, (i % 2) * 64:(i % 2) * 64 + 64].bitcast(BF16)))
                p.bar()

        for oc in range(11):
            dequant(gu3[:, oc, :], 256, 2048, lambda g, oc=oc: gs3[:, oc, g:g + 1], 16,
                    None, 16)
            for pp in range(4):
                p.dma("sync", lambda s, oc=oc, pp=pp: s.dma_start(
                    out=OWGp[pp].rearrange("p (c m) -> p c m", m=1408)[:, :, oc * 128:(oc + 1) * 128],
                    in_=slab[:, pp * 512:(pp + 1) * 512].rearrange("p (a b) -> p a b", b=128)))
            p.bar()
        for hc2 in range(2):
            dequant(dp3[:, hc2, :], 704, 5632, lambda g, hc2=hc2: ds3[:, hc2, g:g + 1], 44,
                    None, 44)
            for pp in range(4):
                p.dma("sync", lambda s, hc2=hc2, pp=pp: s.dma_start(
                    out=OWDp[pp].rearrange("p (c m) -> p c m", m=256)[:, :, hc2 * 128:(hc2 + 1) * 128],
                    in_=slab[:, pp * 11 * 128:(pp + 1) * 11 * 128].rearrange("p (a b) -> p a b", b=128)))
            p.bar()
        _cleanup(cms)
        _fin(p, ctx)
    return nc


def stage_st5a(inputs):
    gup = np.asarray(inputs["L1_ffn_gate_up_packed"])
    gsc = np.asarray(inputs["L1_ffn_gate_up_scales"]).T
    dpk = np.asarray(inputs["L1_ffn_down_packed"])
    dsc = np.asarray(inputs["L1_ffn_down_scales"]).T
    identbf = np.eye(128).astype(BF)
    maps = []
    for core in range(NCORES):
        go = slice(core * 1408, (core + 1) * 1408)
        dh = slice(core * 256, (core + 1) * 256)
        maps.append(dict(
            gup=km(gup[go].astype(np.int32)),
            gsc=km(np.ascontiguousarray(gsc[go])),
            dpk=km(dpk[dh].astype(np.int32)),
            dsc=km(np.ascontiguousarray(dsc[dh])),
            identbf=identbf,
        ))
    return maps



def build_st5b():
    nc = _mknc()
    X = nc.dram_tensor("x3", [128, HKC * 1024], F32, kind="ExternalInput").ap()
    CB = nc.dram_tensor("cb1", [128, HKC], F32, kind="ExternalInput").ap()
    SB_ = nc.dram_tensor("sb1", [128, HKC], F32, kind="ExternalInput").ap()
    GB = nc.dram_tensor("gb1", [128, HKC], F32, kind="ExternalInput").ap()
    WGTp = [nc.dram_tensor(f"wgt_g{pp}", [1024, 4 * 1408], BF16,
                           kind="ExternalInput").ap() for pp in range(4)]
    WDTp = [nc.dram_tensor(f"wdt_g{pp}", [1024, 11 * 256], BF16,
                           kind="ExternalInput").ap() for pp in range(4)]
    OX = nc.dram_tensor("xout", [128, HKC * 1024], F32, kind="ExternalOutput").ap()
    OX3 = OX.rearrange("p (c t) -> p c t", t=1024)
    X3d = X.rearrange("p (c t) -> p c t", t=1024)

    with ExitStack() as ctx:
        cms = {}
        def sb(n, s, d=F32, side=None):
            n = n + "_s"
            cm = nc.sbuf_tensor(n, s, d, side=side)
            h = cm.__enter__(); cms[n] = cm; return h
        def ps(n, s, d=F32):
            n = n + "_s"
            cm = nc.psum_tensor(n, s, d)
            h = cm.__enter__(); cms[n] = cm; return h
        def free(*names):
            for n in names:
                cms.pop(n + "_s").__exit__(None, None, None)

        ocb = sb("ocb", [128, 1], BF16)
        ocf = sb("ocf", [128, 1]); orf = sb("orf", [1, 128])
        epsc = sb("epsc", [1, 1]); eps2 = sb("eps2", [1, 1])
        cb_c = sb("cb_c", [128, HKC]); sb_c = sb("sb_c", [128, HKC])
        gb_c = sb("gb_c", [128, HKC])
        h5 = sb("h5", [128, HKC * 1024], BF16)
        h53 = h5[:].rearrange("p (c t) -> p c t", t=1024)

        p = _mkp(nc)
        p.dma("sync", lambda s: s.dma_start(out=cb_c[:], in_=CB))
        p.dma("sync", lambda s: s.dma_start(out=sb_c[:], in_=SB_))
        p.dma("sync", lambda s: s.dma_start(out=gb_c[:], in_=GB))
        p.op("gpsimd", lambda g: g.memset(ocb[:], 1.0))
        p.op("gpsimd", lambda g: g.memset(ocf[:], 1.0))
        p.op("gpsimd", lambda g: g.memset(orf[:], 1.0))
        p.op("gpsimd", lambda g: g.memset(epsc[:], 1e-5))
        p.op("gpsimd", lambda g: g.memset(eps2[:], 1e-6))
        p.bar()
        p.op("vector", lambda v: v.tensor_scalar(cb_c[:], cb_c[:], 1.0, None, op0=ALU.add))
        p.bar()
        x_sb = sb("x_sb", [128, HKC * 1024], side="right")
        x3 = x_sb[:].rearrange("p (c t) -> p c t", t=1024)
        p.dma("sync", lambda s: s.dma_start(out=x_sb[:], in_=X))
        p.bar()
        emit_ln_mod(p, lambda n, sh, d=F32: sb(n, sh, d, side="right"), ps, nc,
                    x3, h53, cb_c, sb_c, ocb, ocf, orf, HKC, 1024,
                    eps_ap=epsc[0:1, 0:1], tag="b5")
        p.bar()
        free("ln_m2b5", "ln_t1b5", "ln_MRBb5", "ln_RBb5", "ln_rowsb5", "ln_sqb5",
             "x_sb", "ln_rbb5", "ln_ssb5")
        PS8 = [ps(f"PS{i}", [128, 512]) for i in range(8)]
        y5 = sb("y5", [128, HKC * 1024])
        y53 = y5[:].rearrange("p (c t) -> p c t", t=1024)
        hff = sb("hff", [128, 11 * 1024], BF16)
        hf3 = hff[:].rearrange("p (c t) -> p c t", t=1024)
        wga = sb("wga", [128, HKC * 256], BF16, side="right")
        wg4 = wga[:].rearrange("p (u c m) -> p u c m", u=2, m=128)
        wdw = sb("wdw", [128, 11 * 2048], BF16, side="right")
        wd3 = wdw[:].rearrange("p (c m) -> p c m", m=2048)
        p.op("gpsimd", lambda g: g.memset(y5[:], 0.0))
        p.bar()
        for slab in range(4):
            for ic in range(11):
                g = slab * 11 + ic
                for u, gg in [(0, g), (1, 44 + g)]:
                    rk, oc = divmod(gg, 11)
                    for pp in range(4):
                        p.dma("sync", lambda s, u=u, rk=rk, oc=oc, pp=pp: s.dma_start(
                            out=wga[:, u * HKC * 128 + pp * 512:u * HKC * 128 + (pp + 1) * 512],
                            in_=WGTp[pp][rk * 128:(rk + 1) * 128, :]
                            .rearrange("p (c m) -> p c m", m=1408)[:, :, oc * 128:(oc + 1) * 128]))
                p.bar()
                for f in range(2):
                    for kc in range(HKC):
                        p.op("tensor", lambda t, f=f, kc=kc: t.matmul(
                            PS8[f][:, 0:512], wg4[:, 0, kc, :],
                            h53[:, kc, f * 512:(f + 1) * 512],
                            start=(kc == 0), stop=(kc == HKC - 1)))
                    for kc in range(HKC):
                        p.op("tensor", lambda t, f=f, kc=kc: t.matmul(
                            PS8[2 + f][:, 0:512], wg4[:, 1, kc, :],
                            h53[:, kc, f * 512:(f + 1) * 512],
                            start=(kc == 0), stop=(kc == HKC - 1)))
                p.bar()
                for f in range(2):
                    fs = slice(f * 512, (f + 1) * 512)
                    p.op("scalar", lambda s, ic=ic, fs=fs, f=f: s.activation(
                        hf3[:, ic, fs], PS8[f][:, 0:512], AF.Silu))
                p.bar()
                for f in range(2):
                    fs = slice(f * 512, (f + 1) * 512)
                    p.op("vector", lambda v, ic=ic, fs=fs, f=f: v.tensor_tensor(
                        hf3[:, ic, fs], hf3[:, ic, fs], PS8[2 + f][:, 0:512], op=ALU.mult))
                p.bar()
            for rk in range(8):
                p.dma("sync", lambda s, slab=slab, rk=rk: s.dma_start(
                    out=wd3[:, :, rk * 256:(rk + 1) * 256],
                    in_=WDTp[slab][rk * 128:(rk + 1) * 128, :]
                    .rearrange("p (c m) -> p c m", m=256)))
            p.bar()
            for mo in range(HKC):
                for f in range(2):
                    pu = PS8[4 + (mo * 2 + f) % 4]
                    for kc in range(11):
                        p.op("tensor", lambda t, mo=mo, f=f, kc=kc, pu=pu: t.matmul(
                            pu[:], wd3[:, kc, mo * 128:(mo + 1) * 128],
                            hf3[:, kc, f * 512:(f + 1) * 512],
                            start=(kc == 0), stop=(kc == 10)))
                    if (mo * 2 + f) % 4 == 3:
                        p.bar()
                        for u in range(mo * 2 + f - 3, mo * 2 + f + 1):
                            uo, uf = divmod(u, 2)
                            p.op("vector", lambda v, uo=uo, uf=uf, u=u: v.tensor_tensor(
                                y53[:, uo, uf * 512:(uf + 1) * 512],
                                y53[:, uo, uf * 512:(uf + 1) * 512],
                                PS8[4 + u % 4][:], op=ALU.add))
                        p.bar()
        free("wdw", "wga")
        sqb = sb("sqb", [128, HKC * 1024], BF16, side="right")
        sq3 = sqb[:].rearrange("p (c t) -> p c t", t=1024)
        rrow = sb("rrow", [1, 1024], side="right")
        RBr = sb("RBr", [128, 1024], side="right")
        xbuf = sb("xbuf", [128, 2 * 1024], side="right")
        p.op("scalar", lambda s: s.activation(sqb[:], y5[:], AF.Square))
        p.bar()
        for f in range(2):
            for kc in range(HKC):
                p.op("tensor", lambda t, f=f, kc=kc: t.matmul(
                    PS8[0][0:1, 0:512], ocb[:, 0:1], sq3[:, kc, f * 512:(f + 1) * 512],
                    start=(kc == 0), stop=(kc == HKC - 1)))
            p.bar()
            p.op("scalar", lambda s, f=f: s.activation(
                rrow[0:1, f * 512:(f + 1) * 512], PS8[0][0:1, 0:512], AF.Ln,
                scale=1.0 / 2048, bias=eps2[0:1, 0:1]))
            p.bar()
        p.op("scalar", lambda s: s.activation(rrow[:], rrow[:], AF.Exp, scale=-0.5))
        p.bar()
        for f in range(2):
            p.op("tensor", lambda t, f=f: t.matmul(
                PS8[1][:, 0:512], orf[0:1, :], rrow[0:1, f * 512:(f + 1) * 512],
                start=True, stop=True))
            p.bar()
            p.op("vector", lambda v, f=f: v.tensor_copy(
                RBr[:, f * 512:(f + 1) * 512], PS8[1][:, 0:512]))
            p.bar()
        for kc in range(HKC):
            p.dma("sync", lambda s, kc=kc: s.dma_start(
                out=xbuf[:, kc % 2 * 1024:(kc % 2 + 1) * 1024], in_=X3d[:, kc, :]))
            p.bar()
            p.op("vector", lambda v, kc=kc: v.tensor_tensor(
                y53[:, kc, :], y53[:, kc, :], RBr[:], op=ALU.mult))
            p.bar()
            p.op("vector", lambda v, kc=kc: v.scalar_tensor_tensor(
                y53[:, kc, :], y53[:, kc, :], gb_c[:, kc:kc + 1],
                xbuf[:, kc % 2 * 1024:(kc % 2 + 1) * 1024], op0=ALU.mult, op1=ALU.add))
            p.bar()
            p.dma("sync", lambda s, kc=kc: s.dma_start(out=OX3[:, kc, :], in_=y53[:, kc, :]))
        p.bar()
        _cleanup(cms)
        _fin(p, ctx)
    return nc


def stage_st5b(inputs, r4c, r5a, mods):
    d = {}
    for pp in range(4):
        d[f"wgt_g{pp}"] = np.concatenate(
            [np.asarray(r5a[c][f"wgtc{pp}"]) for c in range(8)], axis=0)
        d[f"wdt_g{pp}"] = np.concatenate(
            [np.asarray(r5a[c][f"wdtc{pp}"]) for c in range(8)], axis=0)
    maps = []
    for core in range(NCORES):
        b, q = divmod(core, 4)
        maps.append(dict(
            x3=np.asarray(r4c[core]["x3"]),
            cb1=mods[1][b]["cb"], sb1=mods[1][b]["sb"], gb1=mods[1][b]["gb"],
            **d,
        ))
    return maps


RG8 = [list(range(8))]


def _cc_allgather(p, in_h, out_h):
    p.bar()
    p.op("gpsimd", lambda g, in_h=in_h, out_h=out_h: g.collective_compute(
        "AllGather", ALU.bypass, replica_groups=RG8,
        ins=[in_h.ap().opt()], outs=[out_h.ap().opt()]))
    p.bar()


def _emit_glue_mods():
    px, p = _FUSE.nc_proxy, _FUSE.p
    px.stage = "glmod"
    nc = px
    OH16 = nc.dram_tensor("oh16", [16, 8], F32, kind="ExternalInput").ap()
    IDF = nc.dram_tensor("identf", [128, 128], F32, kind="ExternalInput").ap()
    m_in = [px.registry["mod0"], px.registry["mod1"]]
    m_g = [nc.dram_tensor(f"mod{l}_g", [16, 1536], F32, addr_space="Shared")
           for l in range(2)]
    mrow_d = nc.dram_tensor("mrow_d", [1, 12288], F32)
    names = [["sa0", "ca0", "ga0", "sb0", "cb0", "gb0"],
             ["sa1", "ca1", "ga1", "sb1", "cb1", "gb1"]]
    outs = [[nc.dram_tensor(n, [128, HKC], F32) for n in names[l]] for l in range(2)]
    with ExitStack() as ctx:
        sbuf = lambda n, sh, d=F32: ctx.enter_context(nc.sbuf_tensor(n, sh, d))
        psum = lambda n, sh, d=F32: ctx.enter_context(nc.psum_tensor(n, sh, d))
        oh16 = sbuf("oh16", [16, 8])
        idf = sbuf("idf", [128, 128])
        gsb = sbuf("gsb", [16, 1536])
        mrow = sbuf("mrow", [1, 12288])
        msb = sbuf("msb", [96, 128])
        modc = sbuf("modc", [128, 96])
        PSg = psum("PSg", [128, 1536])
        p.bar()
        p.dma("sync", lambda s: s.dma_start(out=oh16[:], in_=OH16))
        p.dma("sync", lambda s: s.dma_start(out=idf[:], in_=IDF))
        for l in range(2):
            _cc_allgather(p, m_in[l], m_g[l])
        p.bar()
        for l in range(2):
            p.dma("sync", lambda s, l=l: s.dma_start(out=gsb[:], in_=m_g[l].ap()))
            p.bar()
            for j in range(8):
                for f in range(3):
                    p.op("tensor", lambda t, j=j, f=f: t.matmul(
                        PSg[0:1, f * 512:(f + 1) * 512], oh16[:, j:j + 1],
                        gsb[:, f * 512:(f + 1) * 512], start=True, stop=True))
                p.bar()
                for f in range(3):
                    p.op("vector", lambda v, j=j, f=f: v.tensor_copy(
                        mrow[:, j * 1536 + f * 512:j * 1536 + (f + 1) * 512],
                        PSg[0:1, f * 512:(f + 1) * 512]))
                p.bar()
            p.dma("sync", lambda s: s.dma_start(out=mrow_d.ap(), in_=mrow[:]))
            p.bar()
            p.dma("sync", lambda s: s.dma_start(
                out=msb[:], in_=mrow_d.ap().rearrange("x (a b) -> (x a) b", b=128)))
            p.bar()
            p.op("tensor", lambda t: t.transpose(PSg[:, 0:96], msb[:], idf[0:96, 0:96]))
            p.bar()
            p.op("vector", lambda v: v.tensor_copy(modc[:], PSg[:, 0:96]))
            p.bar()
            for v_ in range(6):
                p.dma("sync", lambda s, l=l, v_=v_: s.dma_start(
                    out=outs[l][v_].ap(), in_=modc[:, v_ * 16:(v_ + 1) * 16]))
            p.bar()


def _emit_glue_mamba():
    px, p = _FUSE.nc_proxy, _FUSE.p
    px.stage = "glmam"
    nc = px
    sout = px.registry["sout"]
    alast = px.registry["alast"]
    sout_g = nc.dram_tensor("sout_g", [1024, 4096], BF16, addr_space="Shared")
    alast_g = nc.dram_tensor("alast_g", [32, 64], F32, addr_space="Shared")
    alastp = nc.dram_tensor("alastp", [64, 32], F32)
    alasto = nc.dram_tensor("alasto", [64, 4], F32)
    IDF = nc.dram_tensor("identf", [128, 128], F32, kind="ExternalInput").ap()
    with ExitStack() as ctx:
        sbuf = lambda n, sh, d=F32: ctx.enter_context(nc.sbuf_tensor(n, sh, d))
        psum = lambda n, sh, d=F32: ctx.enter_context(nc.psum_tensor(n, sh, d))
        idf = sbuf("idf", [128, 128])
        ag = sbuf("ag", [32, 64])
        al = sbuf("al", [4, 64])
        tp = sbuf("tp", [64, 32])
        to = sbuf("to", [64, 4])
        PSg = psum("PSg", [128, 512])
        p.bar()
        p.dma("sync", lambda s: s.dma_start(out=idf[:], in_=IDF))
        _cc_allgather(p, sout, sout_g)
        _cc_allgather(p, alast, alast_g)
        p.bar()
        p.dma("sync", lambda s: s.dma_start(out=ag[:], in_=alast_g.ap()))
        p.dma("sync", lambda s: s.dma_start(out=al[:], in_=alast.ap()))
        p.bar()
        p.op("tensor", lambda t: t.transpose(PSg[0:64, 0:32], ag[:], idf[0:32, 0:32]))
        p.op("tensor", lambda t: t.transpose(PSg[0:64, 64:68], al[:], idf[0:4, 0:4]))
        p.bar()
        p.op("vector", lambda v: v.tensor_copy(tp[:], PSg[0:64, 0:32]))
        p.op("vector", lambda v: v.tensor_copy(to[:], PSg[0:64, 64:68]))
        p.bar()
        p.dma("sync", lambda s: s.dma_start(out=alastp.ap(), in_=tp[:]))
        p.dma("sync", lambda s: s.dma_start(out=alasto.ap(), in_=to[:]))
        p.bar()


def _emit_glue_ret():
    px, p = _FUSE.nc_proxy, _FUSE.p
    px.stage = "glret"
    nc = px
    soutr = px.registry["soutr"]
    soutr_g = nc.dram_tensor("soutr_g", [1024, 8192], BF16, addr_space="Shared")
    p.bar()
    _cc_allgather(p, soutr, soutr_g)
    p.bar()


def _emit_glue_fp4():
    px, p = _FUSE.nc_proxy, _FUSE.p
    px.stage = "glfp4"
    nc = px
    import os as _os
    ngw = int(_os.environ.get("FP4_GW", "4"))
    ngd = int(_os.environ.get("FP4_GD", "4"))
    p.bar()
    for pp in range(4):
        wgt_gp = nc.dram_tensor(f"wgt_g{pp}", [1024, 4 * 1408], BF16)
        if pp < ngw:
            _cc_allgather(p, px.registry[f"wgtc{pp}"], wgt_gp)
        wdt_gp = nc.dram_tensor(f"wdt_g{pp}", [1024, 11 * 256], BF16)
        if pp < ngd:
            _cc_allgather(p, px.registry[f"wdtc{pp}"], wdt_gp)
    p.bar()


FUSE_STAGES = ["st0", "glmods", "st1", "st2a", "glmam", "st2b", "st2c", "st3",
               "st4a", "glret", "st4b", "st4c", "st5a", "glfp4", "st5b"]


def _emit_glue_rd():
    px, p = _FUSE.nc_proxy, _FUSE.p
    px.stage = "glrd"
    nc = px
    wgt_g = px.registry["wgt_g0"]
    wdt_g = px.registry["wdt_g0"]
    xout = px.registry.get("xout")
    if xout is None:
        xout = nc.dram_tensor("xout", [128, HKC * 1024], F32, kind="ExternalOutput")
    with ExitStack() as ctx:
        t1 = ctx.enter_context(nc.sbuf_tensor("t1", [128, 512], BF16))
        t2 = ctx.enter_context(nc.sbuf_tensor("t2", [128, 512], F32))
        p.bar()
        p.dma("sync", lambda s: s.dma_start(out=t1[:, 0:256],
                                            in_=wgt_g.ap()[300:428, 0:256]))
        p.dma("sync", lambda s: s.dma_start(out=t1[:, 256:512],
                                            in_=wdt_g.ap()[300:428, 0:256]))
        p.bar()
        p.op("vector", lambda v: v.tensor_copy(t2[:], t1[:]))
        p.bar()
        p.dma("sync", lambda s: s.dma_start(out=xout.ap()[:, 0:512], in_=t2[:]))
        p.bar()


def _emit_glue_rd2():
    px, p = _FUSE.nc_proxy, _FUSE.p
    px.stage = "glrd2"
    nc = px
    wdtc0 = px.registry["wdtc0"]
    xout = px.registry.get("xout")
    if xout is None:
        xout = nc.dram_tensor("xout", [128, HKC * 1024], F32, kind="ExternalOutput")
    with ExitStack() as ctx:
        t1 = ctx.enter_context(nc.sbuf_tensor("t1", [128, 512], BF16))
        t2 = ctx.enter_context(nc.sbuf_tensor("t2", [128, 512], F32))
        p.bar()
        p.dma("sync", lambda s: s.dma_start(out=t1[:], in_=wdtc0.ap()[:, 0:512]))
        p.bar()
        p.op("vector", lambda v: v.tensor_copy(t2[:], t1[:]))
        p.bar()
        p.dma("sync", lambda s: s.dma_start(out=xout.ap()[:, 0:512], in_=t2[:]))
        p.bar()


def build_fused(stages=None):
    global _FUSE
    import os
    if stages is None:
        env = os.environ.get("FUSE_STAGES", "")
        stages = env.split(",") if env else FUSE_STAGES
    _FUSE = _Fuse()
    px, p = _FUSE.nc_proxy, _FUSE.p
    emitters = {
        "st0": lambda: build_st0(), "glmods": _emit_glue_mods,
        "st1": lambda: build_st1(), "st2a": lambda: build_st2a(),
        "glmam": _emit_glue_mamba, "st2b": lambda: build_st2b(),
        "st2c": lambda: build_st2c(), "st3": lambda: build_st3(),
        "st4a": lambda: build_st4a(), "glret": _emit_glue_ret,
        "st4b": lambda: build_st4b(), "st4c": lambda: build_st4c(),
        "st5a": lambda: build_st5a(), "glfp4": _emit_glue_fp4,
        "st5b": lambda: build_st5b(), "glrd": _emit_glue_rd,
        "glrd2": _emit_glue_rd2,
    }
    try:
        for st in stages:
            px.stage = st
            emitters[st]()
        with ExitStack() as ctx:
            p.finalize(ctx)
        nc = _FUSE.real_nc
    finally:
        _FUSE = None
    return nc


def fused_in_maps(inputs):
    maps0 = stage_st0(inputs)
    maps1 = stage_st1(inputs, None)
    maps2a = stage_st2a(inputs, None)
    maps5a = stage_st5a(inputs)
    nw = np.asarray(inputs["L0_attn_norm_weight"]).reshape(1, 4096)
    oh8 = np.zeros((8, 8 * 128), np.float32)
    for j in range(8):
        oh8[j, j * 128:(j + 1) * 128] = 1.0
    oh8b = oh8.astype(BF)
    oh4 = np.zeros((4, 4 * 128), np.float32)
    for c in range(4):
        oh4[c, c * 128:(c + 1) * 128] = 1.0
    oh4b = oh4.astype(BF)
    identf = np.eye(128, dtype=np.float32)
    identbf = np.eye(128).astype(BF)
    x = np.asarray(inputs["L0_attn_hidden_states"])
    W_out = km(np.asarray(inputs["L0_attn_out_proj_w"]))
    wg = np.asarray(inputs["L0_ffn_w_gate"]); wu = np.asarray(inputs["L0_ffn_w_up"])
    wd = np.asarray(inputs["L0_ffn_w_down"])
    WG = np.concatenate([km(wg[e].T) for e in range(8)], axis=1)
    WU = np.concatenate([km(wu[e].T) for e in range(8)], axis=1)
    WD = np.concatenate([km(wd[e].T) for e in range(8)], axis=1)
    SHG = km(np.asarray(inputs["L0_ffn_sh_gate"]).T)
    SHU = km(np.asarray(inputs["L0_ffn_sh_up"]).T)
    SHD = km(np.asarray(inputs["L0_ffn_sh_down"]).T)
    RW = km(np.asarray(inputs["L0_ffn_gate_w"]).T)
    SEG = km(np.asarray(inputs["L0_ffn_se_gate_w"]).T)
    m4a = stage_st4a(inputs, None, None)
    gnw = np.asarray(inputs["L1_attn_gnorm_w"]).reshape(1, 512)
    NH = 8
    gamma = 1.0 - np.exp2(-5.0 - np.arange(NH, dtype=np.float64))
    OW = km(np.asarray(inputs["L1_attn_o_w"]))
    GAW = km(np.asarray(inputs["L1_attn_gate_w"]))
    DWW = km(np.asarray(inputs["L1_attn_down_w"]))
    mnw = col16(np.asarray(inputs["L1_attn_mlp_norm_w"]))
    maps = []
    for core in range(NCORES):
        b, q = divmod(core, 4)
        oh16 = np.zeros((16, 8), np.float32)
        for j in range(8):
            oh16[2 * j + b, j] = 1.0
        selm = np.zeros((8, 8), np.float32)
        selv = np.zeros((8, 1), np.float32)
        for j in range(4 * b, 4 * b + q):
            selv[j] = 1.0
            for m_ in range(j + 1, 4 * b + q):
                selm[m_, j] = 1.0
        wc = np.zeros((8, 8), np.float64)
        for j in range(4 * b, 4 * b + q):
            steps = q - 1 - (j - 4 * b)
            wc[j] = gamma ** (1024.0 * steps)
        wcols = np.zeros((128, 64), np.float32)
        for j in range(8):
            wcols[:, j * 8:(j + 1) * 8] = wc[j][None, :]
        t0 = q * TPC
        xf = np.ascontiguousarray(x[b, t0:t0 + TPC].T)
        d = {}
        d.update(maps0[core])
        d.update({k: v for k, v in maps1[core].items()
                  if k not in ("ca0", "sa0")})
        d.update({k: v for k, v in maps2a[core].items()
                  if k not in ("xtok", "bfeat", "cfeat", "btok", "dtp", "adt")})
        d.update(maps5a[core])
        d.update(dict(
            oh16=oh16, selm=selm, selv=selv, nw=nw, oh8=oh8b,
            xmod=km(xf), w_out=W_out,
            routw=RW, wg=WG, wu=WU, wd=WD, shg=SHG, shu=SHU, shd=SHD, seg=SEG,
            oh8f=oh8, wcols=wcols, gnw=gnw,
            ow=OW, mnw=mnw, gatew=GAW, downw=DWW,
        ))
        d.update({k: v for k, v in m4a[core].items()
                  if k not in ("x2", "ca1", "sa1")})
        maps.append(d)
    return maps


_BUILT = {}


def _nc(name):
    if name not in _BUILT:
        _BUILT[name] = globals()["build_" + name]()
    return _BUILT[name]


def _run(name, maps):
    res = run_bass_kernel_spmd(_nc(name), maps, core_ids=list(range(NCORES)))
    return res.results


def kernel(**inputs):
    if "fused" not in _BUILT:
        _BUILT["fused"] = build_fused()
    maps = fused_in_maps(inputs)
    res = run_bass_kernel_spmd(_BUILT["fused"], maps, core_ids=list(range(NCORES)))
    r = res.results
    out = np.zeros((2, 4096, 2048), np.float32)
    for core in range(NCORES):
        b, q = divmod(core, 4)
        t0 = q * TPC
        out[b, t0:t0 + TPC] = (np.asarray(r[core]["xout"]).reshape(128, HKC, TPC)
                               .transpose(1, 0, 2).reshape(HID, TPC).T)
    return out


# version 09
# speedup vs baseline: 1.0986x; 1.0986x over previous
import math
from contextlib import ExitStack

import numpy as np
import ml_dtypes
import concourse.bass as bass
import concourse.mybir as mybir
import concourse.bass_utils as _BU
from concourse.bass_utils import run_bass_kernel_spmd

if not getattr(_BU, "_jobs_patched", False):
    _BU._jobs_patched = True
    _orig_run_command = _BU.run_command

    def _run_command_j2(argv, **kw):
        argv = list(argv)
        for i, a in enumerate(argv):
            if a == "--jobs" and i + 1 < len(argv):
                argv[i + 1] = "2"
        return _orig_run_command(argv, **kw)

    _BU.run_command = _run_command_j2

F32 = mybir.dt.float32
BF16 = mybir.dt.bfloat16
F16 = mybir.dt.float16
I32 = mybir.dt.int32
AF = mybir.ActivationFunctionType
ALU = mybir.AluOpType
AX = mybir.AxisListType

NCORES = 8
TPC = 1024
HID = 2048
HKC = HID // 128

BF = ml_dtypes.bfloat16


def km(w):
    K, M = w.shape
    assert K % 128 == 0, (K, M)
    return np.ascontiguousarray(
        w.reshape(K // 128, 128, M).transpose(1, 0, 2).reshape(128, -1))


def col16(v):
    return np.ascontiguousarray(v.reshape(-1, 128).T)


ENGINES = ("sync", "scalar", "vector", "tensor", "gpsimd")


class P:
    def __init__(self, nc):
        self.nc = nc
        self.items = []

    def op(self, eng, fn):
        self.items.append(("op", eng, fn))

    def dma(self, eng, fn):
        self.items.append(("dma", eng, fn))

    def bar(self):
        self.items.append(("bar",))

    def finalize(self, ctx):
        nc = self.nc
        BAR = ctx.enter_context(nc.semaphore("BAR"))
        DMAC = ctx.enter_context(nc.semaphore("DMAC"))
        block = ctx.enter_context(nc.Block())

        phases = []
        cur = []
        for it in self.items:
            if it[0] == "bar":
                phases.append(cur)
                cur = []
            else:
                cur.append(it)
        if cur:
            phases.append(cur)

        plan = {e: [] for e in ENGINES}
        cum_bar = 0
        cum_dma = 0
        for ph in phases:
            by_eng = {e: [] for e in ENGINES}
            for kind, eng, fn in ph:
                by_eng[eng].append((kind, fn))
            wait_bar = cum_bar
            wait_dma = cum_dma
            for e in ENGINES:
                ops = by_eng[e]
                if not ops:
                    continue
                last_compute = None
                for i, (kind, fn) in enumerate(ops):
                    if kind == "op":
                        last_compute = i
                first = True
                for i, (kind, fn) in enumerate(ops):
                    plan[e].append((
                        kind, fn,
                        wait_bar if first else None,
                        wait_dma if first else None,
                        (kind == "op" and i == last_compute),
                    ))
                    first = False
                if last_compute is not None:
                    cum_bar += 1
            cum_dma += sum(1 for kind, _, _ in ph if kind == "dma")

        total_dma = cum_dma
        total_bar = cum_bar

        def emit(e, seq):
            for kind, fn, wb, wd, incb in plan[e]:
                if wb:
                    seq.wait_ge(BAR, wb)
                if wd:
                    seq.wait_ge(DMAC, 16 * wd)
                inst = fn(seq)
                if kind == "dma":
                    inst.then_inc(DMAC, 16)
                elif incb:
                    inst.then_inc(BAR, 1)

        for e in ENGINES:
            if not plan[e]:
                continue
            deco = getattr(block, e)
            if e == "sync":
                @deco
                def _(seq, e=e):
                    emit(e, seq)
                    seq.wait_ge(BAR, total_bar)
                    seq.wait_ge(DMAC, 16 * total_dma)
            else:
                @deco
                def _(seq, e=e):
                    emit(e, seq)



_FUSE = None

INTERMEDIATES = {
    "mod0", "mod1", "ca0", "sa0", "ga0", "sb0", "cb0", "gb0",
    "ca1", "sa1", "ga1", "sb1", "cb1", "gb1",
    "gate", "xtok", "bfeat", "cfeat", "btok", "dtp", "adt",
    "yloc", "sout", "alast", "sdo", "alasto", "alastp", "sout_g",
    "ygf", "x1", "x2", "x3", "hres", "opart", "qg", "gtok", "soutr",
    "soutr_g", "ogf",
    "wgtc0", "wgtc1", "wgtc2", "wgtc3", "wdtc0", "wdtc1", "wdtc2", "wdtc3",
    "wgt_g0", "wgt_g1", "wgt_g2", "wgt_g3", "wdt_g0", "wdt_g1", "wdt_g2", "wdt_g3",
    "mod0_g", "mod1_g", "mrow_d", "alast_g",
}


class _NCProxy:
    def __init__(self, nc):
        self._nc = nc
        self.stage = ""
        self.rename = {}
        self.registry = {}

    def dram_tensor(self, name, shape, dtype, kind=None, **kw):
        name = self.rename.get(name, name)
        key = name
        if key in self.registry:
            return self.registry[key]
        if name in INTERMEDIATES:
            h = self._nc.dram_tensor(name, shape, dtype, **kw)
        elif name == "xout":
            h = self._nc.dram_tensor(name, shape, dtype, kind="ExternalOutput")
        else:
            h = self._nc.dram_tensor(name, shape, dtype, kind="ExternalInput")
        self.registry[key] = h
        return h

    def sbuf_tensor(self, name, *a, **kw):
        return self._nc.sbuf_tensor(self.stage + "_" + name, *a, **kw)

    def psum_tensor(self, name, *a, **kw):
        return self._nc.psum_tensor(self.stage + "_" + name, *a, **kw)

    def __getattr__(self, k):
        return getattr(self._nc, k)


class _Fuse:
    def __init__(self):
        self.real_nc = bass.Bass("TRN2", target_bir_lowering=False)
        self.nc_proxy = _NCProxy(self.real_nc)
        self.p = P(self.nc_proxy)


def _mknc():
    if _FUSE is not None:
        return _FUSE.nc_proxy
    return bass.Bass("TRN2", target_bir_lowering=False)


def _mkp(nc):
    if _FUSE is not None:
        return _FUSE.p
    return P(nc)


def _fin(p, ctx):
    if _FUSE is None:
        p.finalize(ctx)


def _cleanup(cms):
    for n in reversed(list(cms)):
        cms.pop(n).__exit__(None, None, None)



def build_st0():
    nc = _mknc()
    C0 = nc.dram_tensor("c0", [128, HKC * 2], F32, kind="ExternalInput").ap()
    C1 = nc.dram_tensor("c1", [128, HKC * 2], F32, kind="ExternalInput").ap()
    WM0 = nc.dram_tensor("wm0", [128, HKC * 1536], F32, kind="ExternalInput").ap()
    WM1 = nc.dram_tensor("wm1", [128, HKC * 1536], F32, kind="ExternalInput").ap()
    MSK = nc.dram_tensor("clampmask", [2, 1536], F32, kind="ExternalInput").ap()
    O0 = nc.dram_tensor("mod0", [2, 1536], F32, kind="ExternalOutput").ap()
    O1 = nc.dram_tensor("mod1", [2, 1536], F32, kind="ExternalOutput").ap()

    with ExitStack() as ctx:
        cms = {}
        sb = lambda n, s, d=F32: ctx.enter_context(nc.sbuf_tensor(n, s, d))
        ps = lambda n, s, d=F32: ctx.enter_context(nc.psum_tensor(n, s, d))
        c_sb = [sb(f"c{l}_sb", [128, HKC * 2]) for l in range(2)]
        c_bf = [sb(f"c{l}_bf", [128, HKC * 2]) for l in range(2)]
        wm_sb = sb("wm_sb", [128, HKC * 1536])
        msk_sb = sb("msk_sb", [2, 1536])
        m_sb = [sb(f"m{l}_sb", [2, 1536]) for l in range(2)]
        ta = sb("ta", [2, 1536]); tb = sb("tb", [2, 1536])
        ti = sb("ti", [2, 1536]); td = sb("td", [2, 1536])
        tii = sb("tii", [2, 1536], I32)
        PS = [ps(f"PS{l}", [2, 1536]) for l in range(2)]

        p = _mkp(nc)
        for l, C in enumerate([C0, C1]):
            p.dma("sync", lambda s, C=C, l=l: s.dma_start(out=c_sb[l][:], in_=C))
        p.dma("sync", lambda s: s.dma_start(out=msk_sb[:], in_=MSK))
        p.bar()
        for l in range(2):
            p.op("scalar", lambda s, l=l: s.activation(c_bf[l][:], c_sb[l][:], AF.Silu))
        p.bar()
        for l, WM in enumerate([WM0, WM1]):
            p.dma("gpsimd", lambda g, WM=WM: g.dma_start(out=wm_sb[:], in_=WM))
            p.bar()
            c3 = c_bf[l][:].rearrange("p (c m) -> p c m", m=2)
            w3 = wm_sb[:].rearrange("p (c m) -> p c m", m=1536)
            for f in range(3):
                for kc in range(HKC):
                    p.op("tensor", lambda t, l=l, f=f, kc=kc, c3=c3, w3=w3:
                         t.matmul(PS[l][:, f * 512:(f + 1) * 512],
                                  c3[:, kc, :], w3[:, kc, f * 512:(f + 1) * 512],
                                  start=(kc == 0), stop=(kc == HKC - 1)))
            p.bar()
        for l in range(2):
            v_ = lambda fn: p.op("vector", fn)
            v_(lambda v, l=l: v.tensor_scalar(ta[:], PS[l][:], 0.25, None, op0=ALU.max))
            v_(lambda v, l=l: v.tensor_scalar(tb[:], PS[l][:], -0.25, None, op0=ALU.min))
            v_(lambda v, l=l: v.tensor_scalar(tii[:], PS[l][:], 0.0, None, op0=ALU.is_ge))
            v_(lambda v: v.tensor_copy(ti[:], tii[:]))
            v_(lambda v: v.tensor_tensor(td[:], ta[:], tb[:], op=ALU.subtract))
            v_(lambda v: v.tensor_tensor(td[:], ti[:], td[:], op=ALU.mult))
            v_(lambda v: v.tensor_tensor(td[:], td[:], tb[:], op=ALU.add))
            v_(lambda v, l=l: v.tensor_tensor(td[:], td[:], PS[l][:], op=ALU.subtract))
            v_(lambda v: v.tensor_tensor(td[:], td[:], msk_sb[:], op=ALU.mult))
            v_(lambda v, l=l: v.tensor_tensor(m_sb[l][:], td[:], PS[l][:], op=ALU.add))
            p.bar()
        p.dma("sync", lambda s: s.dma_start(out=O0, in_=m_sb[0][:]))
        p.dma("sync", lambda s: s.dma_start(out=O1, in_=m_sb[1][:]))
        _cleanup(cms)
        _fin(p, ctx)
    return nc


def stage_st0(inputs):
    maps = []
    msk = np.zeros((2, 12288), np.float32)
    msk[:, 2 * 2048:3 * 2048] = 1.0
    msk[:, 5 * 2048:6 * 2048] = 1.0
    for core in range(NCORES):
        c = core
        sl = slice(1536 * c, 1536 * (c + 1))
        maps.append(dict(
            c0=km(np.asarray(inputs["L0_c"]).T.copy()),
            c1=km(np.asarray(inputs["L1_c"]).T.copy()),
            wm0=km(np.asarray(inputs["L0_wmod"])[:, sl]),
            wm1=km(np.asarray(inputs["L1_wmod"])[:, sl]),
            clampmask=np.ascontiguousarray(msk[:, sl]),
        ))
    return maps


def post_st0(results):
    out = {}
    for l in range(2):
        full = np.concatenate([np.asarray(r[f"mod{l}"]) for r in results], axis=1)
        assert full.shape == (2, 12288)
        per_b = []
        for b in range(2):
            six = full[b].reshape(6, 2048)
            per_b.append({k: col16(six[i]) for i, k in
                          enumerate(["sa", "ca", "ga", "sb", "cb", "gb"])})
        out[l] = per_b
    return out



def emit_ln_mod(p, sb, ps, nc, X3, OUT3, ca1_col, sa_col,
                ones_col_bf, ones_col_f, ones_row_f, KC, T, eps_ap=None, tag=""):
    D = KC * 128
    fts = [(f0, min(T, f0 + 512)) for f0 in range(0, T, 512)]
    sq = sb(f"ln_sq{tag}", [128, KC * T], BF16)
    sq3 = sq[:].rearrange("p (c t) -> p c t", t=T)
    SS = ps(f"ln_ss{tag}", [1, 1024])
    rows = sb(f"ln_rows{tag}", [1, 3 * T])
    rb_ps = ps(f"ln_rb{tag}", [128, 1024])
    RB = sb(f"ln_RB{tag}", [128, T])
    MRB = sb(f"ln_MRB{tag}", [128, T])
    t1 = sb(f"ln_t1{tag}", [128, T])
    m2 = sb(f"ln_m2{tag}", [128, T])

    p.op("scalar", lambda s: s.activation(sq[:], X3.rearrange("p c t -> p (c t)"),
                                          AF.Square))
    p.bar()
    for f0, f1 in fts:
        n = f1 - f0
        for kc in range(KC):
            p.op("tensor", lambda t, kc=kc, f0=f0, f1=f1, n=n:
                 t.matmul(SS[:, 0:n], ones_col_f[:, 0:1], X3[:, kc, f0:f1],
                          start=(kc == 0), stop=(kc == KC - 1)))
        for kc in range(KC):
            p.op("tensor", lambda t, kc=kc, f0=f0, f1=f1, n=n:
                 t.matmul(SS[:, 512:512 + n], ones_col_bf[:, 0:1], sq3[:, kc, f0:f1],
                          start=(kc == 0), stop=(kc == KC - 1)))
        p.bar()
        mr = slice(2 * T + f0, 2 * T + f1)
        r0 = slice(f0, f1)
        r1 = slice(T + f0, T + f1)
        p.op("vector", lambda v, n=n, mr=mr: v.tensor_scalar(
            rows[:, mr], SS[:, 0:n], 1.0 / D, None, op0=ALU.mult))
        p.bar()
        p.op("scalar", lambda s, mr=mr, r0=r0: s.activation(rows[:, r0], rows[:, mr], AF.Square))
        p.bar()
        p.op("vector", lambda v, n=n, r0=r0, r1=r1: v.scalar_tensor_tensor(
            rows[:, r1], SS[:, 512:512 + n], 1.0 / D, rows[:, r0],
            op0=ALU.mult, op1=ALU.subtract))
        p.bar()
        p.op("scalar", lambda s, r0=r0, r1=r1: s.activation(rows[:, r0], rows[:, r1],
                                                            AF.Ln, bias=eps_ap))
        p.bar()
        p.op("scalar", lambda s, r0=r0: s.activation(rows[:, r0], rows[:, r0],
                                                     AF.Exp, scale=-0.5))
        p.bar()
        p.op("vector", lambda v, mr=mr, r0=r0, r1=r1: v.tensor_tensor(
            rows[:, r1], rows[:, mr], rows[:, r0], op=ALU.mult))
        p.bar()
        p.op("tensor", lambda t, n=n, r0=r0: t.matmul(
            rb_ps[:, 0:n], ones_row_f[0:1, :], rows[0:1, r0], start=True, stop=True))
        p.op("tensor", lambda t, n=n, r1=r1: t.matmul(
            rb_ps[:, 512:512 + n], ones_row_f[0:1, :], rows[0:1, r1], start=True, stop=True))
        p.bar()
        p.op("vector", lambda v, f0=f0, f1=f1, n=n: v.tensor_copy(RB[:, f0:f1], rb_ps[:, 0:n]))
        p.op("vector", lambda v, f0=f0, f1=f1, n=n: v.tensor_copy(MRB[:, f0:f1], rb_ps[:, 512:512 + n]))
        p.bar()
    for kc in range(KC):
        p.op("vector", lambda v, kc=kc: v.scalar_tensor_tensor(
            t1[:], X3[:, kc, :], ca1_col[:, kc:kc + 1], RB[:], op0=ALU.mult, op1=ALU.mult))
        p.op("vector", lambda v, kc=kc: v.tensor_scalar(
            m2[:], MRB[:], ca1_col[:, kc:kc + 1], None, op0=ALU.mult))
        p.op("vector", lambda v: v.tensor_tensor(t1[:], t1[:], m2[:], op=ALU.subtract))
        p.op("vector", lambda v, kc=kc: v.tensor_scalar(
            OUT3[:, kc, :], t1[:], sa_col[:, kc:kc + 1], None, op0=ALU.add))


TH = 1028
NXBC = 34


def build_st1():
    nc = _mknc()
    X = nc.dram_tensor("xh", [128, HKC * TH], F32, kind="ExternalInput").ap()
    W = nc.dram_tensor("w_in", [128, HKC * 8512], F32, kind="ExternalInput").ap()
    CA = nc.dram_tensor("ca0", [128, HKC], F32, kind="ExternalInput").ap()
    SA = nc.dram_tensor("sa0", [128, HKC], F32, kind="ExternalInput").ap()
    CW = nc.dram_tensor("convw", [128, NXBC * 4], F32, kind="ExternalInput").ap()
    CB = nc.dram_tensor("convb", [128, NXBC], F32, kind="ExternalInput").ap()
    DTB = nc.dram_tensor("dtb", [1, 64], F32, kind="ExternalInput").ap()
    ALOG = nc.dram_tensor("alog", [1, 64], F32, kind="ExternalInput").ap()
    HALO = nc.dram_tensor("halo", [128, 1], F32, kind="ExternalInput").ap()
    IDB = nc.dram_tensor("identbf", [128, 128], BF16, kind="ExternalInput").ap()
    OGATE = nc.dram_tensor("gate", [128, 8 * 4096], BF16, kind="ExternalOutput").ap()
    OXTOK = nc.dram_tensor("xtok", [128, 8 * 4096], BF16, kind="ExternalOutput").ap()
    OBF = nc.dram_tensor("bfeat", [128, 1024], BF16, kind="ExternalOutput").ap()
    OCF = nc.dram_tensor("cfeat", [128, 1024], BF16, kind="ExternalOutput").ap()
    OBT = nc.dram_tensor("btok", [128, 8 * 128], BF16, kind="ExternalOutput").ap()
    ODTP = nc.dram_tensor("dtp", [128, 8 * 64], F32, kind="ExternalOutput").ap()
    OADT = nc.dram_tensor("adt", [128, 8 * 64], F32, kind="ExternalOutput").ap()

    W3 = W.rearrange("p (c m) -> p c m", m=8512)
    X3v = X.rearrange("p (c t) -> p c t", t=TH)

    with ExitStack() as ctx:
        cms = {}
        def sb(n, s, d=F32, side=None):
            cm = nc.sbuf_tensor(n, s, d, side=side)
            h = cm.__enter__()
            cms[n] = cm
            return h
        def sbr(n, s, d=F32):
            return sb(n, s, d, side="right")
        def ps(n, s, d=F32):
            cm = nc.psum_tensor(n, s, d)
            h = cm.__enter__()
            cms[n] = cm
            return h
        def free(*names):
            for n in names:
                cms.pop(n).__exit__(None, None, None)

        ocb = sb("ocb", [128, 1], BF16)
        ocf = sb("ocf", [128, 1]); orf = sb("orf", [1, 128])
        epsc = sb("epsc", [1, 1])
        ca_c = sb("ca_c", [128, HKC]); sa_c = sb("sa_c", [128, HKC])
        cw_sb = sb("cw_sb", [128, NXBC * 4]); cb_sb = sb("cb_sb", [128, NXBC])
        dtb_sb = sb("dtb_sb", [1, 64]); alog_sb = sb("alog_sb", [1, 64])
        halo_sb = sb("halo_sb", [128, 1]); idb = sb("idb", [128, 128], BF16)
        x_sb = sbr("x_sb", [128, HKC * TH])
        h0 = sb("h0", [128, HKC * TH], BF16)
        h03 = h0[:].rearrange("p (c t) -> p c t", t=TH)
        x3 = x_sb[:].rearrange("p (c t) -> p c t", t=TH)

        p = _mkp(nc)
        p.dma("sync", lambda s: s.dma_start(out=x_sb[:], in_=X))
        p.dma("sync", lambda s: s.dma_start(out=ca_c[:], in_=CA))
        p.dma("sync", lambda s: s.dma_start(out=sa_c[:], in_=SA))
        p.dma("sync", lambda s: s.dma_start(out=cw_sb[:], in_=CW))
        p.dma("sync", lambda s: s.dma_start(out=cb_sb[:], in_=CB))
        p.dma("sync", lambda s: s.dma_start(out=dtb_sb[:], in_=DTB))
        p.dma("sync", lambda s: s.dma_start(out=alog_sb[:], in_=ALOG))
        p.dma("sync", lambda s: s.dma_start(out=halo_sb[:], in_=HALO))
        p.dma("sync", lambda s: s.dma_start(out=idb[:], in_=IDB))
        p.op("gpsimd", lambda g: g.memset(ocb[:], 1.0))
        p.op("gpsimd", lambda g: g.memset(ocf[:], 1.0))
        p.op("gpsimd", lambda g: g.memset(orf[:], 1.0))
        p.op("gpsimd", lambda g: g.memset(epsc[:], 1e-5))
        p.bar()
        p.op("vector", lambda v: v.tensor_scalar(ca_c[:], ca_c[:], 1.0, None, op0=ALU.add))
        p.bar()
        emit_ln_mod(p, sbr, ps, nc, x3, h03, ca_c, sa_c, ocb, ocf, orf, HKC, TH, eps_ap=epsc[0:1, 0:1])
        p.bar()
        free("ln_m2", "ln_t1", "ln_MRB", "ln_RB", "ln_rows", "ln_sq", "x_sb",
             "ln_rb", "ln_ss")

        PS8 = [ps(f"PS{i}", [128, 512]) for i in range(8)]
        wslab = sbr("wslab", [128, 2 * HKC * 512], BF16)
        wpar = [wslab[:, par * HKC * 512:(par + 1) * HKC * 512]
                .rearrange("p (c m) -> p c m", m=512) for par in range(2)]
        gslab = sbr("gslab", [128, 8 * 512], BF16)
        g3 = gslab[:].rearrange("p (a b) -> p a b", b=512)
        OG3 = OGATE.rearrange("p (a b) -> p a b", b=4096)

        def ld1g(fs, par):
            p.dma("gpsimd", lambda g, fs=fs, par=par: g.dma_start(
                out=wslab[:, par * HKC * 512:(par + 1) * HKC * 512],
                in_=W3[:, :, fs * 512:(fs + 1) * 512]))

        ld1g(0, 0)
        p.bar()
        for fs in range(8):
            par = fs % 2
            if fs + 1 < 8:
                ld1g(fs + 1, (fs + 1) % 2)
            if fs > 0:
                p.dma("sync", lambda s, fs=fs: s.dma_start(
                    out=OG3[:, :, (fs - 1) * 512:fs * 512], in_=g3[:, :, :]))
            for tt in range(8):
                for kc in range(HKC):
                    p.op("tensor", lambda t, tt=tt, kc=kc, par=par:
                         t.matmul(PS8[tt][:], h03[:, kc, 4 + 128 * tt:4 + 128 * (tt + 1)],
                                  wpar[par][:, kc, :], start=(kc == 0), stop=(kc == HKC - 1)))
            p.bar()
            for tt in range(8):
                p.op("vector", lambda v, tt=tt: v.tensor_copy(g3[:, tt, :], PS8[tt][:]))
            p.bar()
        p.dma("sync", lambda s: s.dma_start(
            out=OG3[:, :, 7 * 512:8 * 512], in_=g3[:, :, :]))
        xbc = sb("xbc", [128, NXBC * TH], BF16)
        xb3 = xbc[:].rearrange("p (c t) -> p c t", t=TH)
        wpx = [wslab[:, par * HKC * 256:(par + 1) * HKC * 256]
               .rearrange("p (c m) -> p c m", m=256) for par in range(2)]

        def ld1x(ms, par):
            p.dma("gpsimd", lambda g, ms=ms, par=par: g.dma_start(
                out=wslab[:, par * HKC * 256:(par + 1) * HKC * 256],
                in_=W3[:, :, 4096 + ms * 256:4096 + (ms + 1) * 256]))

        ld1x(0, 0)
        p.bar()
        for ms in range(17):
            par = ms % 2
            if ms + 1 < 17:
                ld1x(ms + 1, (ms + 1) % 2)
            for mc in range(2):
                fc = ms * 2 + mc
                for fi, (f0, f1) in enumerate([(0, 512), (512, 1024), (1024, 1028)]):
                    pnum = mc * 3 + fi
                    for kc in range(HKC):
                        p.op("tensor", lambda t, mc=mc, f0=f0, f1=f1, kc=kc, pnum=pnum, par=par:
                             t.matmul(PS8[pnum][:, 0:f1 - f0],
                                      wpx[par][:, kc, mc * 128:(mc + 1) * 128],
                                      h03[:, kc, f0:f1],
                                      start=(kc == 0), stop=(kc == HKC - 1)))
            p.bar()
            for mc in range(2):
                fc = ms * 2 + mc
                for fi, (f0, f1) in enumerate([(0, 512), (512, 1024), (1024, 1028)]):
                    p.op("vector", lambda v, fc=fc, f0=f0, f1=f1, fi=fi, mc=mc:
                         v.tensor_copy(xb3[:, fc, f0:f1], PS8[mc * 3 + fi][:, 0:f1 - f0]))
            p.bar()
        p.dma("gpsimd", lambda g: g.dma_start(
            out=wslab[:, 0:HKC * 64], in_=W3[:, :, 8448:8512]))
        p.bar()
        wsd = wslab[:].rearrange("p (c m) -> p c m", m=64)
        for tt in range(8):
            for kc in range(HKC):
                p.op("tensor", lambda t, tt=tt, kc=kc:
                     t.matmul(PS8[tt][:, 0:64], h03[:, kc, 4 + 128 * tt:4 + 128 * (tt + 1)],
                              wsd[:, kc, :], start=(kc == 0), stop=(kc == HKC - 1)))
        p.bar()
        dtraw = sb("dtraw", [128, 8 * 64])
        dr3 = dtraw[:].rearrange("p (a b) -> p a b", b=64)
        for tt in range(8):
            p.op("vector", lambda v, tt=tt: v.tensor_copy(dr3[:, tt, :], PS8[tt][:, 0:64]))
        p.bar()
        free("gslab", "wslab")

        for c in range(NXBC):
            p.op("vector", lambda v, c=c: v.tensor_scalar(
                xb3[:, c, 0:4], xb3[:, c, 0:4], halo_sb[:, 0:1], None, op0=ALU.mult))
        p.bar()
        acc = [sb(f"acc{i}", [128, 1024]) for i in range(2)]
        for c in range(NXBC):
            a = acc[c % 2][:]
            p.op("vector", lambda v, c=c, a=a: v.tensor_scalar(
                a, xb3[:, c, 1:1025], cw_sb[:, 4 * c:4 * c + 1], None, op0=ALU.mult))
            for k in range(1, 4):
                p.op("vector", lambda v, c=c, k=k, a=a: v.scalar_tensor_tensor(
                    a, xb3[:, c, 1 + k:1025 + k], cw_sb[:, 4 * c + k:4 * c + k + 1], a,
                    op0=ALU.mult, op1=ALU.add))
            p.bar()
            p.op("scalar", lambda s, c=c, a=a: s.activation(
                xb3[:, c, 0:1024], a, AF.Silu, bias=cb_sb[:, c:c + 1]))
            if c % 2 == 1 or c == NXBC - 1:
                p.bar()
        p.bar()
        p.dma("sync", lambda s: s.dma_start(out=OBF, in_=xb3[:, 32, 0:1024]))
        p.dma("sync", lambda s: s.dma_start(out=OCF, in_=xb3[:, 33, 0:1024]))

        dscr = sb("dscr", [128, 8 * 64]); dscr2 = sb("dscr2", [128, 8 * 64])
        arow = sb("arow", [1, 64])
        PB = PS8[1]
        p.op("scalar", lambda s: s.activation(arow[:], alog_sb[:], AF.Exp))
        p.bar()
        p.op("vector", lambda v: v.tensor_scalar(arow[:], arow[:], -1.0, None, op0=ALU.mult))
        p.bar()
        p.op("tensor", lambda t: t.matmul(PB[:, 0:64], orf[0:1, :], dtb_sb[0:1, :],
                                          start=True, stop=True))
        p.op("tensor", lambda t: t.matmul(PB[:, 64:128], orf[0:1, :], arow[0:1, :],
                                          start=True, stop=True))
        p.bar()
        dtbb = sb("dtbb", [128, 64]); ab = sb("ab", [128, 64])
        p.op("vector", lambda v: v.tensor_copy(dtbb[:], PB[:, 0:64]))
        p.op("vector", lambda v: v.tensor_copy(ab[:], PB[:, 64:128]))
        p.bar()
        d3 = dscr[:].rearrange("p (a b) -> p a b", b=64)
        for tt in range(8):
            p.op("vector", lambda v, tt=tt: v.tensor_tensor(
                d3[:, tt, :], dr3[:, tt, :], dtbb[:], op=ALU.add))
        p.bar()
        p.op("scalar", lambda s: s.activation(dscr2[:], dscr[:], AF.Abs))
        p.bar()
        p.op("scalar", lambda s: s.activation(dscr2[:], dscr2[:], AF.Exp, scale=-1.0))
        p.bar()
        p.op("scalar", lambda s: s.activation(dscr2[:], dscr2[:], AF.Ln, bias=ocf[:, 0:1]))
        p.bar()
        dtp_sb = sb("dtp_sb", [128, 8 * 64]); adt_sb = sb("adt_sb", [128, 8 * 64])
        p.op("vector", lambda v: v.tensor_scalar(dscr[:], dscr[:], 0.0, None, op0=ALU.max))
        p.bar()
        p.op("vector", lambda v: v.tensor_tensor(dtp_sb[:], dscr[:], dscr2[:], op=ALU.add))
        p.bar()
        a3 = adt_sb[:].rearrange("p (a b) -> p a b", b=64)
        dt3 = dtp_sb[:].rearrange("p (a b) -> p a b", b=64)
        for tt in range(8):
            p.op("vector", lambda v, tt=tt: v.tensor_tensor(
                a3[:, tt, :], dt3[:, tt, :], ab[:], op=ALU.mult))
        p.bar()
        p.dma("sync", lambda s: s.dma_start(out=ODTP, in_=dtp_sb[:]))
        p.dma("sync", lambda s: s.dma_start(out=OADT, in_=adt_sb[:]))

        PT = PS8[0][:].bitcast(BF16)
        xslab = sb("xslab", [128, 8 * 128], BF16)
        xs3 = xslab[:].rearrange("p (a b) -> p a b", b=128)
        OX3 = OXTOK.rearrange("p (a b) -> p a b", b=4096)
        OBT3 = OBT.rearrange("p (a b) -> p a b", b=128)
        for fc in range(33):
            for tt in range(8):
                p.op("tensor", lambda t, fc=fc, tt=tt: t.transpose(
                    PT[:, tt * 128:(tt + 1) * 128],
                    xb3[:, fc, 128 * tt:128 * (tt + 1)], idb[:]))
            p.bar()
            p.op("vector", lambda v: v.tensor_copy(xslab[:], PT[:]))
            p.bar()
            if fc < 32:
                p.dma("sync", lambda s, fc=fc: s.dma_start(
                    out=OX3[:, :, fc * 128:(fc + 1) * 128], in_=xs3[:, :, :]))
            else:
                p.dma("sync", lambda s: s.dma_start(out=OBT3[:, :, :], in_=xs3[:, :, :]))
            p.bar()
        _cleanup(cms)
        _fin(p, ctx)
    return nc


def stage_st1(inputs, mods):
    modget = (lambda l, b, k: mods[l][b][k]) if mods is not None else (
        lambda l, b, k: np.zeros((128, HKC), np.float32))
    x = np.asarray(inputs["L0_attn_hidden_states"])
    W = km(np.asarray(inputs["L0_attn_in_proj_w"]))
    cw = np.asarray(inputs["L0_attn_conv_w"])
    cwk = np.ascontiguousarray(cw.reshape(NXBC, 128, 4).transpose(1, 0, 2).reshape(128, -1))
    cbk = col16x(np.asarray(inputs["L0_attn_conv_b"]), NXBC)
    dtb = np.asarray(inputs["L0_attn_dt_bias"]).reshape(1, 64)
    alog = np.asarray(inputs["L0_attn_A_log"]).reshape(1, 64)
    identbf = np.eye(128).astype(BF)
    maps = []
    for core in range(NCORES):
        b, q = divmod(core, 4)
        t0 = q * TPC
        xh = np.zeros((TH, HID), np.float32)
        xh[4:] = x[b, t0:t0 + TPC]
        if q > 0:
            xh[0:4] = x[b, t0 - 4:t0]
        xf = np.ascontiguousarray(xh.T)
        maps.append(dict(
            xh=km2(xf, TH),
            w_in=W,
            ca0=modget(0, b, "ca"), sa0=modget(0, b, "sa"),
            convw=cwk, convb=cbk, dtb=dtb, alog=alog,
            halo=np.full((128, 1), 1.0 if q > 0 else 0.0, np.float32),
            identbf=identbf,
        ))
    return maps


def col16x(v, n):
    return np.ascontiguousarray(v.reshape(n, 128).T)


def km2(w, M):
    return km(w)



def build_st2a():
    nc = _mknc()
    XT = nc.dram_tensor("xtok", [128, 8 * 4096], BF16, kind="ExternalInput").ap()
    BFT = nc.dram_tensor("bfeat", [128, 1024], BF16, kind="ExternalInput").ap()
    CFT = nc.dram_tensor("cfeat", [128, 1024], BF16, kind="ExternalInput").ap()
    BTK = nc.dram_tensor("btok", [128, 8 * 128], BF16, kind="ExternalInput").ap()
    DTP = nc.dram_tensor("dtp", [128, 8 * 64], F32, kind="ExternalInput").ap()
    ADT = nc.dram_tensor("adt", [128, 8 * 64], F32, kind="ExternalInput").ap()
    DROW = nc.dram_tensor("drow", [1, 64], F32, kind="ExternalInput").ap()
    IDF = nc.dram_tensor("identf", [128, 128], F32, kind="ExternalInput").ap()
    IDB = nc.dram_tensor("identbf", [128, 128], BF16, kind="ExternalInput").ap()
    TRIU = nc.dram_tensor("maskneg", [128, 128], F32, kind="ExternalInput").ap()
    OH4 = nc.dram_tensor("oh4", [4, 4 * 128], BF16, kind="ExternalInput").ap()
    OH64 = nc.dram_tensor("oh64", [64, 64 * 128], F32, kind="ExternalInput").ap()
    OYL = nc.dram_tensor("yloc", [128, 8 * 4096], F32, kind="ExternalOutput").ap()
    OSOUT = nc.dram_tensor("sout", [128, 64 * 64], BF16, kind="ExternalOutput").ap()
    OALAST = nc.dram_tensor("alast", [4, 64], F32, kind="ExternalOutput").ap()
    OSDO = nc.dram_tensor("sdo", [128, 8 * 64], F32, kind="ExternalOutput").ap()

    with ExitStack() as ctx:
        cms = {}
        def sb(n, s, d=F32, side=None):
            n = n + "_s"
            cm = nc.sbuf_tensor(n, s, d, side=side)
            h = cm.__enter__(); cms[n] = cm; return h
        def ps(n, s, d=F32):
            n = n + "_s"
            cm = nc.psum_tensor(n, s, d)
            h = cm.__enter__(); cms[n] = cm; return h

        xt = sb("xt", [128, 8 * 4096], BF16)
        xt3 = xt[:].rearrange("p (a b) -> p a b", b=4096)
        bft = sb("bft", [128, 1024], BF16)
        cft = sb("cft", [128, 1024], BF16)
        btk = sb("btk", [128, 8 * 128], BF16)
        btk3 = btk[:].rearrange("p (a b) -> p a b", b=128)
        dtps = sb("dtps", [128, 8 * 64])
        dtp3 = dtps[:].rearrange("p (a b) -> p a b", b=64)
        adts = sb("adts", [128, 8 * 64])
        adt3 = adts[:].rearrange("p (a b) -> p a b", b=64)
        drows = sb("drows", [1, 64]); idf = sb("idf", [128, 128])
        idb = sb("idb", [128, 128], BF16); mneg = sb("mneg", [128, 128])
        oh4 = sb("oh4", [4, 4 * 128], BF16)
        oh64 = sb("oh64", [64, 64 * 128])
        orf = sb("orf", [1, 128])
        ones64 = sb("ones64", [64, 256])
        adtF = sb("adtF", [64, 8 * 128])
        acu = sb("acu", [64, 1024])
        acub = sb("acub", [64, 1024], BF16)
        natok = sb("natok", [128, 8 * 64])
        natok3 = natok[:].rearrange("p (a b) -> p a b", b=64)
        sdo = sb("sdo", [128, 8 * 64])
        sdo3 = sdo[:].rearrange("p (a b) -> p a b", b=64)
        dsF = sb("dsF", [64, 1024], BF16)
        dstok = sb("dstok", [128, 8 * 64])
        dstok3 = dstok[:].rearrange("p (a b) -> p a b", b=64)
        alast_sb = sb("alast_sb", [4, 64])
        e4bf = sb("e4bf", [4, 64], BF16)
        eac = sb("eac", [128, 4 * 64])
        eac3 = eac[:].rearrange("p (a b) -> p a b", b=64)
        dcols = sb("dcols", [128, 64])
        sloc = sb("sloc", [128, 64 * 64], BF16)
        sloc3 = sloc[:].rearrange("p (a b) -> p a b", b=64)
        spa = sb("spa", [128, 256], BF16)
        spb = sb("spb", [128, 128], BF16)
        argb = sb("argb", [128, 8 * 128])
        ag3 = argb[:].rearrange("p (a b) -> p a b", b=128)
        ebuf = sb("ebuf", [128, 8 * 384], BF16)
        eb3 = ebuf[:].rearrange("p (a b) -> p a b", b=384)
        mbuf = sb("mbuf", [128, 8 * 384], BF16)
        mb3 = mbuf[:].rearrange("p (a b) -> p a b", b=384)
        bdb = sb("bdb", [128, 8 * 256], BF16)
        bdb3 = bdb[:].rearrange("p (a b) -> p a b", b=128)
        xsb = sb("xsb", [128, 8 * 128], BF16)
        xsb3 = xsb[:].rearrange("p (a b) -> p a b", b=64)
        yls = [sb(f"yls{i}", [128, 4096]) for i in range(2)]
        PS8 = [ps(f"PS{i}", [128, 512]) for i in range(8)]
        OYL3 = OYL.rearrange("p (a b) -> p a b", b=4096)

        p = _mkp(nc)
        for dst, src in [(xt, XT), (bft, BFT), (cft, CFT), (btk, BTK), (dtps, DTP),
                         (adts, ADT), (drows, DROW), (idf, IDF), (idb, IDB),
                         (mneg, TRIU), (oh4, OH4), (oh64, OH64)]:
            p.dma("sync", lambda s, dst=dst, src=src: s.dma_start(out=dst[:], in_=src))
        p.op("gpsimd", lambda g: g.memset(orf[:], 1.0))
        p.op("gpsimd", lambda g: g.memset(ones64[:], 1.0))
        p.op("gpsimd", lambda g: g.memset(sloc[:], 0.0))
        p.bar()
        for tt in range(8):
            p.op("tensor", lambda t, tt=tt: t.transpose(
                PS8[tt % 4][0:64, 0:128], adt3[:, tt, :], idf[:]))
            if tt % 4 == 3:
                p.bar()
                for u in range(tt - 3, tt + 1):
                    p.op("vector", lambda v, u=u: v.tensor_copy(
                        adtF[:, u * 128:(u + 1) * 128], PS8[u % 4][0:64, 0:128]))
                p.bar()
        for c in range(4):
            p.op("vector", lambda v, c=c: v.tensor_tensor_scan(
                acu[:, c * 256:(c + 1) * 256], ones64[:, :],
                adtF[:, c * 256:(c + 1) * 256], 0.0, op0=ALU.mult, op1=ALU.add))
        p.bar()
        p.op("tensor", lambda t: t.transpose(PS8[0][0:4, 0:64],
                                             acu[:, 255:1024:256], idf[0:64, 0:64]))
        p.bar()
        p.op("vector", lambda v: v.tensor_copy(alast_sb[:], PS8[0][0:4, 0:64]))
        p.bar()
        p.dma("sync", lambda s: s.dma_start(out=OALAST, in_=alast_sb[:]))
        p.op("scalar", lambda s: s.activation(e4bf[:], alast_sb[:], AF.Exp))
        p.bar()
        for c in range(4):
            p.op("tensor", lambda t, c=c: t.matmul(
                PS8[c][:, 0:64], oh4[:, c * 128:(c + 1) * 128], e4bf[:],
                start=True, stop=True))
        p.bar()
        for c in range(4):
            p.op("vector", lambda v, c=c: v.tensor_copy(eac3[:, c, :], PS8[c][:, 0:64]))
        p.bar()
        p.op("tensor", lambda t: t.matmul(PS8[4][:, 0:64], orf[0:1, :], drows[0:1, :],
                                          start=True, stop=True))
        p.bar()
        p.op("vector", lambda v: v.tensor_copy(dcols[:], PS8[4][:, 0:64]))
        p.bar()
        for c in range(4):
            p.op("scalar", lambda s, c=c: s.activation(
                dsF[:, c * 256:(c + 1) * 256], acu[:, c * 256:(c + 1) * 256],
                AF.Exp, scale=-1.0, bias=acu[:, c * 256 + 255:c * 256 + 256]))
        p.bar()
        for tt in range(8):
            p.op("tensor", lambda t, tt=tt: t.transpose(
                PS8[tt % 2][:, 0:64], acu[:, tt * 128:(tt + 1) * 128], idf[0:64, 0:64]))
            p.op("tensor", lambda t, tt=tt: t.transpose(
                PS8[2 + tt % 2][:, 0:32].bitcast(BF16),
                dsF[:, tt * 128:(tt + 1) * 128], idb[0:64, 0:64]))
            p.bar()
            p.op("vector", lambda v, tt=tt: v.tensor_scalar(
                natok3[:, tt, :], PS8[tt % 2][:, 0:64], -1.0, None, op0=ALU.mult))
            p.op("vector", lambda v, tt=tt: v.tensor_copy(
                dstok3[:, tt, :], PS8[2 + tt % 2][:, 0:32].bitcast(BF16)))
            p.bar()
        p.op("scalar", lambda s: s.activation(sdo[:], natok[:], AF.Exp, scale=-1.0))
        p.bar()
        p.dma("sync", lambda s: s.dma_start(out=OSDO, in_=sdo[:]))

        for c in range(4):
            cs = slice(c * 256, (c + 1) * 256)
            p.op("tensor", lambda t, c=c: t.matmul(
                PS8[5][:, 0:256], bft[:, c * 256:c * 256 + 128], cft[:, c * 256:(c + 1) * 256],
                start=True, stop=True))
            p.op("tensor", lambda t, c=c: t.matmul(
                PS8[6][:, 0:128], bft[:, c * 256 + 128:(c + 1) * 256],
                cft[:, c * 256 + 128:(c + 1) * 256], start=True, stop=True))
            p.bar()
            p.op("vector", lambda v: v.tensor_copy(spa[:], PS8[5][:, 0:256]))
            p.op("vector", lambda v: v.tensor_copy(spb[:], PS8[6][:, 0:128]))
            p.bar()
            for hg in range(16):
                hs = range(hg * 4, hg * 4 + 4)
                for i, h in enumerate(hs):
                    p.op("tensor", lambda t, h=h, i=i, c=c: t.matmul(
                        PS8[i // 2][:, (i % 2) * 256:(i % 2) * 256 + 256],
                        oh64[:, h * 128:(h + 1) * 128], acu[:, c * 256:(c + 1) * 256],
                        start=True, stop=True))
                p.bar()
                for i, h in enumerate(hs):
                    pcb = PS8[i // 2][:, (i % 2) * 256:(i % 2) * 256 + 256]
                    p.op("vector", lambda v, i=i, pcb=pcb: v.tensor_tensor(
                        ag3[:, 2 * i, :], pcb[:, 0:128], mneg[:], op=ALU.add))
                    p.op("vector", lambda v, i=i, pcb=pcb: v.tensor_tensor(
                        ag3[:, 2 * i + 1, :], pcb[:, 128:256], mneg[:], op=ALU.add))
                p.bar()
                for i, h in enumerate(hs):
                    pcb = PS8[i // 2][:, (i % 2) * 256:(i % 2) * 256 + 256]
                    p.op("scalar", lambda s, i=i, h=h, c=c: s.activation(
                        eb3[:, i, 0:128], ag3[:, 2 * i, :], AF.Exp,
                        bias=natok3[:, 2 * c, h:h + 1]))
                    p.op("scalar", lambda s, i=i, h=h, c=c, pcb=pcb: s.activation(
                        eb3[:, i, 128:256], pcb[:, 128:256], AF.Exp,
                        bias=natok3[:, 2 * c, h:h + 1]))
                    p.op("scalar", lambda s, i=i, h=h, c=c: s.activation(
                        eb3[:, i, 256:384], ag3[:, 2 * i + 1, :], AF.Exp,
                        bias=natok3[:, 2 * c + 1, h:h + 1]))
                p.bar()
                for i, h in enumerate(hs):
                    p.op("vector", lambda v, i=i: v.tensor_tensor(
                        mb3[:, i, 0:256], eb3[:, i, 0:256], spa[:], op=ALU.mult))
                    p.op("vector", lambda v, i=i: v.tensor_tensor(
                        mb3[:, i, 256:384], eb3[:, i, 256:384], spb[:], op=ALU.mult))
                    for sc in range(2):
                        p.op("vector", lambda v, i=i, h=h, c=c, sc=sc: v.tensor_scalar(
                            bdb3[:, 2 * i + sc, :], btk3[:, 2 * c + sc, :],
                            dstok3[:, 2 * c + sc, h:h + 1], None, op0=ALU.mult))
                        p.op("vector", lambda v, i=i, h=h, c=c, sc=sc: v.tensor_scalar(
                            xsb3[:, 2 * i + sc, :], xt3[:, 2 * c + sc, h * 64:(h + 1) * 64],
                            dtp3[:, 2 * c + sc, h:h + 1], None, op0=ALU.mult))
                p.bar()
                for i, h in enumerate(hs):
                    o = i * 64
                    p.op("tensor", lambda t, i=i, o=o: t.matmul(
                        PS8[2][:, o:o + 64], mb3[:, i, 0:128], xsb3[:, 2 * i, :],
                        start=True, stop=True))
                    p.op("tensor", lambda t, i=i, o=o: t.matmul(
                        PS8[2][:, 256 + o:256 + o + 64], mb3[:, i, 128:256],
                        xsb3[:, 2 * i, :], start=True, stop=False))
                    p.op("tensor", lambda t, i=i, o=o: t.matmul(
                        PS8[2][:, 256 + o:256 + o + 64], mb3[:, i, 256:384],
                        xsb3[:, 2 * i + 1, :], start=False, stop=True))
                    p.op("tensor", lambda t, i=i, h=h, c=c, o=o: t.matmul(
                        PS8[3][:, o:o + 64], cft[:, c * 256:c * 256 + 128],
                        sloc3[:, h, :], start=True, stop=True))
                    p.op("tensor", lambda t, i=i, h=h, c=c, o=o: t.matmul(
                        PS8[3][:, 256 + o:256 + o + 64], cft[:, c * 256 + 128:(c + 1) * 256],
                        sloc3[:, h, :], start=True, stop=True))
                    p.op("tensor", lambda t, i=i, o=o: t.matmul(
                        PS8[4][:, o:o + 64], bdb3[:, 2 * i, :], xsb3[:, 2 * i, :],
                        start=True, stop=False))
                    p.op("tensor", lambda t, i=i, o=o: t.matmul(
                        PS8[4][:, o:o + 64], bdb3[:, 2 * i + 1, :], xsb3[:, 2 * i + 1, :],
                        start=False, stop=True))
                p.bar()
                for i, h in enumerate(hs):
                    o = i * 64
                    hc = slice(h * 64, (h + 1) * 64)
                    for sc in range(2):
                        yd = PS8[2][:, sc * 256 + o:sc * 256 + o + 64]
                        tgt = yls[sc][:]
                        p.op("vector", lambda v, i=i, h=h, c=c, sc=sc, yd=yd, tgt=tgt, hc=hc:
                             v.scalar_tensor_tensor(tgt[:, hc], xt3[:, 2 * c + sc, hc],
                                                    dcols[:, h:h + 1], yd,
                                                    op0=ALU.mult, op1=ALU.add))
                p.bar()
                for i, h in enumerate(hs):
                    o = i * 64
                    hc = slice(h * 64, (h + 1) * 64)
                    for sc in range(2):
                        yo = PS8[3][:, sc * 256 + o:sc * 256 + o + 64]
                        tgt = yls[sc][:]
                        p.op("vector", lambda v, i=i, h=h, c=c, sc=sc, yo=yo, tgt=tgt, hc=hc:
                             v.scalar_tensor_tensor(tgt[:, hc], yo,
                                                    sdo3[:, 2 * c + sc, h:h + 1], tgt[:, hc],
                                                    op0=ALU.mult, op1=ALU.add))
                    p.op("vector", lambda v, i=i, h=h, c=c:
                         v.scalar_tensor_tensor(sloc3[:, h, :], sloc3[:, h, :],
                                                eac3[:, c, h:h + 1], PS8[4][:, i * 64:i * 64 + 64],
                                                op0=ALU.mult, op1=ALU.add))
                p.bar()
            for sc in range(2):
                p.dma("sync", lambda s, c=c, sc=sc: s.dma_start(
                    out=OYL3[:, 2 * c + sc, :], in_=yls[sc][:]))
            p.bar()
        p.dma("sync", lambda s: s.dma_start(out=OSOUT, in_=sloc[:]))
        _cleanup(cms)
        _fin(p, ctx)
    return nc


def stage_st2a(inputs, r1):
    if r1 is None:
        z = np.zeros((1, 1), np.float32)
        r1 = [dict(xtok=z, bfeat=z, cfeat=z, btok=z, dtp=z, adt=z)] * NCORES
    identf = np.eye(128, dtype=np.float32)
    identbf = np.eye(128).astype(BF)
    maskneg = np.where(np.triu(np.ones((128, 128), bool)), 0.0, -3e38).astype(np.float32)
    oh4 = np.zeros((4, 4 * 128), np.float32)
    for c in range(4):
        oh4[c, c * 128:(c + 1) * 128] = 1.0
    oh64 = np.zeros((64, 64 * 128), np.float32)
    for h in range(64):
        oh64[h, h * 128:(h + 1) * 128] = 1.0
    drow = np.asarray(inputs["L0_attn_D"]).reshape(1, 64)
    maps = []
    for core in range(NCORES):
        r = r1[core]
        maps.append(dict(
            xtok=np.asarray(r["xtok"]), bfeat=np.asarray(r["bfeat"]),
            cfeat=np.asarray(r["cfeat"]), btok=np.asarray(r["btok"]),
            dtp=np.asarray(r["dtp"]), adt=np.asarray(r["adt"]),
            drow=drow, identf=identf, identbf=identbf,
            maskneg=maskneg,
            oh4=oh4.astype(BF), oh64=oh64,
        ))
    return maps



def build_st2b():
    nc = _mknc()
    YL = nc.dram_tensor("yloc", [128, 8 * 4096], F32, kind="ExternalInput").ap()
    SALL = nc.dram_tensor("sout_g", [1024, 4096], BF16, kind="ExternalInput").ap()
    ALO = nc.dram_tensor("alasto", [64, 4], F32, kind="ExternalInput").ap()
    ALP = nc.dram_tensor("alastp", [64, 32], F32, kind="ExternalInput").ap()
    SELM = nc.dram_tensor("selm", [8, 8], F32, kind="ExternalInput").ap()
    SELV = nc.dram_tensor("selv", [8, 1], F32, kind="ExternalInput").ap()
    SDO = nc.dram_tensor("sdo", [128, 8 * 64], F32, kind="ExternalInput").ap()
    CFT = nc.dram_tensor("cfeat", [128, 1024], BF16, kind="ExternalInput").ap()
    GATE = nc.dram_tensor("gate", [128, 8 * 4096], BF16, kind="ExternalInput").ap()
    NW = nc.dram_tensor("nw", [1, 4096], F32, kind="ExternalInput").ap()
    IDF = nc.dram_tensor("identf", [128, 128], F32, kind="ExternalInput").ap()
    IDB = nc.dram_tensor("identbf", [128, 128], BF16, kind="ExternalInput").ap()
    OH4 = nc.dram_tensor("oh4", [4, 4 * 128], BF16, kind="ExternalInput").ap()
    OH8 = nc.dram_tensor("oh8", [8, 8 * 128], BF16, kind="ExternalInput").ap()
    OYGF = nc.dram_tensor("ygf", [128, 32 * 1024], BF16, kind="ExternalOutput").ap()

    YL3 = YL.rearrange("p (a b) -> p a b", b=4096)
    G3 = GATE.rearrange("p (a b) -> p a b", b=4096)
    OYGF3 = OYGF.rearrange("p (a b) -> p a b", b=1024)

    with ExitStack() as ctx:
        cms = {}
        def sb(n, s, d=F32, side=None):
            n = n + "_s"
            cm = nc.sbuf_tensor(n, s, d, side=side)
            h = cm.__enter__(); cms[n] = cm; return h
        def ps(n, s, d=F32):
            n = n + "_s"
            cm = nc.psum_tensor(n, s, d)
            h = cm.__enter__(); cms[n] = cm; return h

        sall = sb("sall", [128, 8 * 4096], BF16, side="right")
        sall3 = sall[:].rearrange("p (a b) -> p a b", b=4096)
        alo = sb("alo", [64, 4]); alp = sb("alp", [64, 32])
        selm = sb("selm", [8, 8]); selv = sb("selv", [8, 1])
        sdos = sb("sdos", [128, 8 * 64])
        sdo3 = sdos[:].rearrange("p (a b) -> p a b", b=64)
        cft = sb("cft", [128, 1024], BF16)
        nwr = sb("nwr", [1, 4096])
        idf = sb("idf", [128, 128]); idb = sb("idb", [128, 128], BF16)
        oh4 = sb("oh4", [4, 4 * 128], BF16); oh8 = sb("oh8", [8, 8 * 128], BF16)
        orf = sb("orf", [1, 128]); epsc = sb("epsc", [128, 1])
        atot = sb("atot", [64, 8]); atott = sb("atott", [8, 64])
        e8 = sb("e8", [8, 64]); e8b = sb("e8b", [8, 64], BF16)
        eb = sb("eb", [128, 8 * 64])
        eb3 = eb[:].rearrange("p (a b) -> p a b", b=64)
        cums = sb("cums", [64, 4]); ecr = sb("ecr", [4, 64])
        ecrb = sb("ecrb", [4, 64], BF16)
        ecb = sb("ecb", [128, 4 * 64])
        ecb3 = ecb[:].rearrange("p (a b) -> p a b", b=64)
        sic = sb("sic", [128, 2 * 4096], side="right")
        sic3 = sic[:].rearrange("p (a b) -> p a b", b=4096)
        sicbf = sb("sicbf", [128, 4096], BF16)
        scolb = sb("scolb", [128, 32])
        ssq = sb("ssq", [128, 1]); rstd = sb("rstd", [128, 1])
        PS8 = [ps(f"PS{i}", [128, 512]) for i in range(8)]
        latetiles = {}
        def late(n, shape, d=F32):
            latetiles[n] = None
            return n

        p = _mkp(nc)
        for dst, src in [(alo, ALO), (alp, ALP), (selm, SELM),
                         (selv, SELV), (sdos, SDO), (cft, CFT), (nwr, NW),
                         (idf, IDF), (idb, IDB), (oh4, OH4), (oh8, OH8)]:
            p.dma("sync", lambda s, dst=dst, src=src: s.dma_start(out=dst[:], in_=src))
        for j in range(8):
            p.dma("sync", lambda s, j=j: s.dma_start(
                out=sall3[:, j, :], in_=SALL[j * 128:(j + 1) * 128, :]))
        p.op("gpsimd", lambda g: g.memset(orf[:], 1.0))
        p.op("gpsimd", lambda g: g.memset(epsc[:], 1e-5))
        p.op("gpsimd", lambda g: g.memset(sic[:], 0.0))
        p.bar()
        for m in range(8):
            p.op("vector", lambda v, m=m: v.reduce_sum(
                atot[:, m:m + 1], alp[:, 4 * m:4 * m + 4], axis=AX.X))
        p.bar()
        p.op("tensor", lambda t: t.transpose(PS8[0][0:8, 0:64], atot[:], idf[0:64, 0:64]))
        p.bar()
        p.op("vector", lambda v: v.tensor_copy(atott[:], PS8[0][0:8, 0:64]))
        p.bar()
        p.op("tensor", lambda t: t.matmul(PS8[0][0:8, 0:64], selm[:], atott[:],
                                          start=True, stop=True))
        p.bar()
        p.op("scalar", lambda s: s.activation(e8[:], PS8[0][0:8, 0:64], AF.Exp))
        p.bar()
        p.op("vector", lambda v: v.tensor_scalar(e8[:], e8[:], selv[:, 0:1], None, op0=ALU.mult))
        p.bar()
        p.op("vector", lambda v: v.tensor_copy(e8b[:], e8[:]))
        p.bar()
        p.op("gpsimd", lambda g: g.memset(scolb[:, 0:1], 0.0))
        p.bar()
        for j in range(8):
            p.op("tensor", lambda t, j=j: t.matmul(
                PS8[j % 4][:, 0:64], oh8[:, j * 128:(j + 1) * 128], e8b[:],
                start=True, stop=True))
            if j % 4 == 3:
                p.bar()
                for u in range(j - 3, j + 1):
                    p.op("vector", lambda v, u=u: v.tensor_copy(
                        eb3[:, u, :], PS8[u % 4][:, 0:64]))
                p.bar()
        for j in range(8):
            for h in range(64):
                p.op("vector", lambda v, j=j, h=h: v.scalar_tensor_tensor(
                    sic3[:, (j + 1) % 2, h * 64:(h + 1) * 64],
                    sall3[:, j, h * 64:(h + 1) * 64], eb3[:, j, h:h + 1],
                    sic3[:, j % 2, h * 64:(h + 1) * 64], op0=ALU.mult, op1=ALU.add))
            p.bar()
        p.op("vector", lambda v: v.tensor_copy(sicbf[:], sic3[:, 0, :]))
        p.bar()
        cms.pop("sic_s").__exit__(None, None, None)
        cms.pop("sall_s").__exit__(None, None, None)
        nwb = sb("nwb", [128, 4096], BF16)
        ybuf = sb("ybuf", [128, 4096])
        ynb = sb("ynb", [128, 4096])
        gbuf = sb("gbuf", [128, 4096], BF16)
        sgb = sb("sgb", [128, 4096], BF16)
        sqscr = sb("sqscr", [128, 4096], BF16)
        ygb = sb("ygb", [128, 4096], BF16)
        ygslab = sb("ygslab", [128, 32 * 128], BF16)
        p.op("gpsimd", lambda g: g.memset(cums[:, 0:1], 0.0))
        p.op("vector", lambda v: v.tensor_copy(cums[:, 1:2], alo[:, 0:1]))
        p.bar()
        p.op("vector", lambda v: v.tensor_tensor(cums[:, 2:3], cums[:, 1:2], alo[:, 1:2], op=ALU.add))
        p.bar()
        p.op("vector", lambda v: v.tensor_tensor(cums[:, 3:4], cums[:, 2:3], alo[:, 2:3], op=ALU.add))
        p.bar()
        p.op("scalar", lambda s: s.activation(cums[:], cums[:], AF.Exp))
        p.bar()
        p.op("tensor", lambda t: t.transpose(PS8[0][0:4, 0:64], cums[:], idf[0:64, 0:64]))
        p.bar()
        p.op("vector", lambda v: v.tensor_copy(ecr[:], PS8[0][0:4, 0:64]))
        p.bar()
        p.op("vector", lambda v: v.tensor_copy(ecrb[:], ecr[:]))
        p.bar()
        for c in range(4):
            p.op("tensor", lambda t, c=c: t.matmul(
                PS8[c][:, 0:64], oh4[:, c * 128:(c + 1) * 128], ecrb[:],
                start=True, stop=True))
        p.bar()
        for c in range(4):
            p.op("vector", lambda v, c=c: v.tensor_copy(ecb3[:, c, :], PS8[c][:, 0:64]))
        p.bar()
        for f in range(8):
            p.op("tensor", lambda t, f=f: t.matmul(
                PS8[f % 4][:, 0:512], orf[0:1, :], nwr[0:1, f * 512:(f + 1) * 512],
                start=True, stop=True))
            if f % 4 == 3:
                p.bar()
                for u in range(f - 3, f + 1):
                    p.op("vector", lambda v, u=u: v.tensor_copy(
                        nwb[:, u * 512:(u + 1) * 512], PS8[u % 4][:, 0:512]))
                p.bar()
        for tt in range(8):
            c = tt // 2
            p.dma("sync", lambda s, tt=tt: s.dma_start(out=ybuf[:], in_=YL3[:, tt, :]))
            p.dma("sync", lambda s, tt=tt: s.dma_start(out=gbuf[:], in_=G3[:, tt, :]))
            p.bar()
            for grp in range(2):
                for i in range(32):
                    h = grp * 32 + i
                    p.op("tensor", lambda t, tt=tt, h=h, i=i: t.matmul(
                        PS8[i // 8][:, (i % 8) * 64:(i % 8) * 64 + 64],
                        cft[:, tt * 128:(tt + 1) * 128], sicbf[:, h * 64:(h + 1) * 64],
                        start=True, stop=True))
                p.bar()
                for i in range(32):
                    h = grp * 32 + i
                    p.op("vector", lambda v, tt=tt, h=h, i=i, c=c: v.tensor_scalar(
                        scolb[:, i:i + 1], sdo3[:, tt, h:h + 1], ecb3[:, c, h:h + 1],
                        None, op0=ALU.mult))
                p.bar()
                for i in range(32):
                    h = grp * 32 + i
                    p.op("vector", lambda v, tt=tt, h=h, i=i: v.scalar_tensor_tensor(
                        ybuf[:, h * 64:(h + 1) * 64],
                        PS8[i // 8][:, (i % 8) * 64:(i % 8) * 64 + 64],
                        scolb[:, i:i + 1], ybuf[:, h * 64:(h + 1) * 64],
                        op0=ALU.mult, op1=ALU.add))
                p.bar()
            p.op("scalar", lambda s: s.activation(sqscr[:], ybuf[:], AF.Square,
                                                  accum_out=ssq[:]))
            p.bar()
            p.op("scalar", lambda s: s.activation(rstd[:], ssq[:], AF.Ln,
                                                  scale=1.0 / 4096, bias=epsc[:, 0:1]))
            p.bar()
            p.op("scalar", lambda s: s.activation(rstd[:], rstd[:], AF.Exp, scale=-0.5))
            p.bar()
            p.op("vector", lambda v: v.tensor_scalar(ynb[:], ybuf[:], rstd[:, 0:1], None, op0=ALU.mult))
            p.op("scalar", lambda s: s.activation(sgb[:], gbuf[:], AF.Sigmoid))
            p.bar()
            p.op("vector", lambda v: v.tensor_tensor(ynb[:], ynb[:], nwb[:], op=ALU.mult))
            p.bar()
            p.op("vector", lambda v: v.tensor_tensor(sgb[:], sgb[:], gbuf[:], op=ALU.mult))
            p.bar()
            p.op("vector", lambda v: v.tensor_tensor(ygb[:], ynb[:], sgb[:], op=ALU.mult))
            p.bar()
            for grp in range(4):
                for i in range(8):
                    fc = grp * 8 + i
                    p.op("tensor", lambda t, fc=fc, i=i: t.transpose(
                        PS8[4 + i // 2][:, (i % 2) * 64:(i % 2) * 64 + 64].bitcast(BF16),
                        ygb[:, fc * 128:(fc + 1) * 128], idb[:]))
                p.bar()
                for i in range(8):
                    fc = grp * 8 + i
                    p.op("vector", lambda v, fc=fc, i=i: v.tensor_copy(
                        ygslab[:, fc * 128:(fc + 1) * 128],
                        PS8[4 + i // 2][:, (i % 2) * 64:(i % 2) * 64 + 64].bitcast(BF16)))
                p.bar()
            p.dma("sync", lambda s, tt=tt: s.dma_start(
                out=OYGF3[:, :, tt * 128:(tt + 1) * 128],
                in_=ygslab[:].rearrange("p (a b) -> p a b", b=128)))
            p.bar()
        _cleanup(cms)
        _fin(p, ctx)
    return nc


def stage_st2b(inputs, r1, r2a, mods):
    identf = np.eye(128, dtype=np.float32)
    identbf = np.eye(128).astype(BF)
    oh4 = np.zeros((4, 4 * 128), np.float32)
    for c in range(4):
        oh4[c, c * 128:(c + 1) * 128] = 1.0
    oh8 = np.zeros((8, 8 * 128), np.float32)
    for j in range(8):
        oh8[j, j * 128:(j + 1) * 128] = 1.0
    soutall = np.concatenate([np.asarray(r2a[j]["sout"]) for j in range(8)], axis=0)
    alastp = np.concatenate([np.asarray(r2a[j]["alast"]).T for j in range(8)], axis=1)
    nw = np.asarray(inputs["L0_attn_norm_weight"]).reshape(1, 4096)
    maps = []
    for core in range(NCORES):
        b, q = divmod(core, 4)
        selm = np.zeros((8, 8), np.float32)
        selv = np.zeros((8, 1), np.float32)
        for j in range(4 * b, 4 * b + q):
            selv[j] = 1.0
            for m in range(j + 1, 4 * b + q):
                selm[m, j] = 1.0
        maps.append(dict(
            yloc=np.asarray(r2a[core]["yloc"]), sout_g=soutall,
            alasto=np.asarray(r2a[core]["alast"]).T.copy(), alastp=alastp,
            selm=selm, selv=selv, sdo=np.asarray(r2a[core]["sdo"]),
            cfeat=np.asarray(r1[core]["cfeat"]), gate=np.asarray(r1[core]["gate"]),
            nw=nw, identf=identf, identbf=identbf,
            oh4=oh4.astype(BF), oh8=oh8.astype(BF),
        ))
    return maps



def build_st2c():
    nc = _mknc()
    YGF = nc.dram_tensor("ygf", [128, 32 * 1024], BF16, kind="ExternalInput").ap()
    X = nc.dram_tensor("xmod", [128, HKC * 1024], F32, kind="ExternalInput").ap()
    W = nc.dram_tensor("w_out", [128, 32 * 2048], F32, kind="ExternalInput").ap()
    GA = nc.dram_tensor("ga0", [128, HKC], F32, kind="ExternalInput").ap()
    OX = nc.dram_tensor("x1", [128, HKC * 1024], F32, kind="ExternalOutput").ap()

    W3 = W.rearrange("p (c m) -> p c m", m=2048)
    OX3 = OX.rearrange("p (c t) -> p c t", t=1024)

    with ExitStack() as ctx:
        cms = {}
        def sb(n, s, d=F32, side=None):
            n = n + "_s"
            cm = nc.sbuf_tensor(n, s, d, side=side)
            h = cm.__enter__(); cms[n] = cm; return h
        def ps(n, s, d=F32):
            n = n + "_s"
            cm = nc.psum_tensor(n, s, d)
            h = cm.__enter__(); cms[n] = cm; return h

        ygf = sb("ygf", [128, 32 * 1024], BF16)
        yg3 = ygf[:].rearrange("p (c t) -> p c t", t=1024)
        x_sb = sb("x", [128, HKC * 1024])
        x3 = x_sb[:].rearrange("p (c t) -> p c t", t=1024)
        ga = sb("ga", [128, HKC])
        wsl = sb("wsl", [128, 2 * 32 * 256], BF16)
        wspar = [wsl[:, par * 32 * 256:(par + 1) * 32 * 256]
                 .rearrange("p (c m) -> p c m", m=256) for par in range(2)]
        ob = sb("ob", [128, 2 * 1024])
        ob3 = ob[:].rearrange("p (c t) -> p c t", t=1024)
        PS8 = [ps(f"PS{i}", [128, 512]) for i in range(8)]

        def ld2c(ms, par):
            p.dma("gpsimd", lambda g, ms=ms, par=par: g.dma_start(
                out=wsl[:, par * 32 * 256:(par + 1) * 32 * 256],
                in_=W3[:, :, ms * 256:(ms + 1) * 256]))

        p = _mkp(nc)
        p.dma("sync", lambda s: s.dma_start(out=ygf[:], in_=YGF))
        p.dma("sync", lambda s: s.dma_start(out=x_sb[:], in_=X))
        p.dma("sync", lambda s: s.dma_start(out=ga[:], in_=GA))
        ld2c(0, 0)
        p.bar()
        for ms in range(8):
            par = ms % 2
            if ms + 1 < 8:
                ld2c(ms + 1, (ms + 1) % 2)
            if ms > 0:
                p.dma("sync", lambda s, ms=ms: s.dma_start(
                    out=OX3[:, (ms - 1) * 2:ms * 2, :], in_=ob3[:, :, :]))
            for mc in range(2):
                for tt in range(8):
                    pu = PS8[mc * 2 + tt // 4][:, (tt % 4) * 128:(tt % 4) * 128 + 128]
                    for kc in range(32):
                        p.op("tensor", lambda t, mc=mc, tt=tt, kc=kc, pu=pu, par=par: t.matmul(
                            pu, wspar[par][:, kc, mc * 128:(mc + 1) * 128],
                            yg3[:, kc, tt * 128:(tt + 1) * 128],
                            start=(kc == 0), stop=(kc == 31)))
            p.bar()
            for mc in range(2):
                kcg = ms * 2 + mc
                for tt in range(8):
                    pu = PS8[mc * 2 + tt // 4][:, (tt % 4) * 128:(tt % 4) * 128 + 128]
                    p.op("vector", lambda v, mc=mc, tt=tt, kcg=kcg, pu=pu:
                         v.scalar_tensor_tensor(ob3[:, mc, tt * 128:(tt + 1) * 128],
                                                pu, ga[:, kcg:kcg + 1],
                                                x3[:, kcg, tt * 128:(tt + 1) * 128],
                                                op0=ALU.mult, op1=ALU.add))
            p.bar()
        p.dma("sync", lambda s: s.dma_start(
            out=OX3[:, 14:16, :], in_=ob3[:, :, :]))
        p.bar()
        _cleanup(cms)
        _fin(p, ctx)
    return nc


def stage_st2c(inputs, r2b, mods):
    W = km(np.asarray(inputs["L0_attn_out_proj_w"]))
    x = np.asarray(inputs["L0_attn_hidden_states"])
    maps = []
    for core in range(NCORES):
        b, q = divmod(core, 4)
        t0 = q * TPC
        xf = np.ascontiguousarray(x[b, t0:t0 + TPC].T)
        maps.append(dict(
            ygf=np.asarray(r2b[core]["ygf"]), xmod=km(xf), w_out=W,
            ga0=mods[0][b]["ga"],
        ))
    return maps



def build_st3(debug=False):
    nc = _mknc()
    X = nc.dram_tensor("x1", [128, HKC * 1024], F32, kind="ExternalInput").ap()
    CB = nc.dram_tensor("cb0", [128, HKC], F32, kind="ExternalInput").ap()
    SB_ = nc.dram_tensor("sb0", [128, HKC], F32, kind="ExternalInput").ap()
    GB = nc.dram_tensor("gb0", [128, HKC], F32, kind="ExternalInput").ap()
    RW = nc.dram_tensor("routw", [128, HKC * 8], F32, kind="ExternalInput").ap()
    WG = nc.dram_tensor("wg", [128, 8 * HKC * 1024], F32, kind="ExternalInput").ap()
    WU = nc.dram_tensor("wu", [128, 8 * HKC * 1024], F32, kind="ExternalInput").ap()
    WD = nc.dram_tensor("wd", [128, 8 * 8 * 2048], F32, kind="ExternalInput").ap()
    SHG = nc.dram_tensor("shg", [128, HKC * 2048], F32, kind="ExternalInput").ap()
    SHU = nc.dram_tensor("shu", [128, HKC * 2048], F32, kind="ExternalInput").ap()
    SHD = nc.dram_tensor("shd", [128, HKC * 2048], F32, kind="ExternalInput").ap()
    SEG = nc.dram_tensor("seg", [128, HKC], F32, kind="ExternalInput").ap()
    IDF = nc.dram_tensor("identf", [128, 128], F32, kind="ExternalInput").ap()
    OH8F = nc.dram_tensor("oh8f", [8, 8 * 128], F32, kind="ExternalInput").ap()
    OX = nc.dram_tensor("x2", [128, HKC * 1024], F32, kind="ExternalOutput").ap()
    if debug:
        OH2 = nc.dram_tensor("dh2", [128, HKC * 1024], BF16, kind="ExternalOutput").ap()
        OCOMB = nc.dram_tensor("dcomb", [8, 1024], F32, kind="ExternalOutput").ap()
        OYM = nc.dram_tensor("dym", [128, HKC * 1024], F32, kind="ExternalOutput").ap()
        OSG = nc.dram_tensor("dsg", [1, 1024], F32, kind="ExternalOutput").ap()
        OSHB = nc.dram_tensor("dshb", [128, HKC * 1024], BF16, kind="ExternalOutput").ap()

    WG4 = WG.rearrange("p (e c m) -> p e c m", c=HKC, m=1024)
    WU4 = WU.rearrange("p (e c m) -> p e c m", c=HKC, m=1024)
    WD4 = WD.rearrange("p (e c m) -> p e c m", c=8, m=2048)
    SHG3 = SHG.rearrange("p (c m) -> p c m", m=2048)
    SHU3 = SHU.rearrange("p (c m) -> p c m", m=2048)
    SHD3 = SHD.rearrange("p (c m) -> p c m", m=2048)
    OX3 = OX.rearrange("p (c t) -> p c t", t=1024)
    X3d = X.rearrange("p (c t) -> p c t", t=1024)

    with ExitStack() as ctx:
        cms = {}
        def sb(n, s, d=F32, side=None):
            n = n + "_s"
            cm = nc.sbuf_tensor(n, s, d, side=side)
            h = cm.__enter__(); cms[n] = cm; return h
        def ps(n, s, d=F32):
            n = n + "_s"
            cm = nc.psum_tensor(n, s, d)
            h = cm.__enter__(); cms[n] = cm; return h
        def free(*names):
            for n in names:
                cms.pop(n + "_s").__exit__(None, None, None)

        ocb = sb("ocb", [128, 1], BF16)
        ocf = sb("ocf", [128, 1]); orf = sb("orf", [1, 128])
        epsc = sb("epsc", [1, 1])
        cb_c = sb("cb_c", [128, HKC]); sb_c = sb("sb_c", [128, HKC])
        gb_c = sb("gb_c", [128, HKC])
        rw = sb("rw", [128, HKC * 8])
        rw3 = rw[:].rearrange("p (c m) -> p c m", m=8)
        rwb = sb("rwb", [128, HKC * 8], BF16)
        rwb3 = rwb[:].rearrange("p (c m) -> p c m", m=8)
        seg = sb("seg", [128, HKC]); segb = sb("segb", [128, HKC], BF16)
        idf = sb("idf", [128, 128])
        oh8f = sb("oh8f", [8, 8 * 128])
        h2 = sb("h2", [128, HKC * 1024], BF16)
        h23 = h2[:].rearrange("p (c t) -> p c t", t=1024)
        combF = sb("combF", [8, 1024])
        sgrow = sb("sgrow", [1, 1024])
        sgb = sb("sgb", [128, 1024])
        scr1 = sb("scr1", [128, 8]); scr2 = sb("scr2", [128, 8])
        scri = sb("scri", [128, 8], I32); scrf = sb("scrf", [128, 8])
        m1c = sb("m1c", [128, 1]); m2c = sb("m2c", [128, 1])
        p1c = sb("p1c", [128, 1]); p2c = sb("p2c", [128, 1])
        combt = sb("combt", [128, 8])

        p = _mkp(nc)
        for dst, src in [(cb_c, CB), (sb_c, SB_), (gb_c, GB), (rw, RW),
                         (seg, SEG), (idf, IDF), (oh8f, OH8F)]:
            p.dma("sync", lambda s, dst=dst, src=src: s.dma_start(out=dst[:], in_=src))
        p.op("gpsimd", lambda g: g.memset(ocb[:], 1.0))
        p.op("gpsimd", lambda g: g.memset(ocf[:], 1.0))
        p.op("gpsimd", lambda g: g.memset(orf[:], 1.0))
        p.op("gpsimd", lambda g: g.memset(epsc[:], 1e-5))
        p.bar()
        p.op("vector", lambda v: v.tensor_scalar(cb_c[:], cb_c[:], 1.0, None, op0=ALU.add))
        p.op("scalar", lambda s: s.copy(rwb[:], rw[:]))
        p.op("scalar", lambda s: s.copy(segb[:], seg[:]))
        p.bar()
        x_sb = sb("x_sb", [128, HKC * 1024], side="right")
        x3 = x_sb[:].rearrange("p (c t) -> p c t", t=1024)
        p.dma("sync", lambda s: s.dma_start(out=x_sb[:], in_=X))
        p.bar()
        emit_ln_mod(p, lambda n, sh, d=F32: sb(n, sh, d, side="right"), ps, nc,
                    x3, h23, cb_c, sb_c, ocb, ocf, orf, HKC, 1024,
                    eps_ap=epsc[0:1, 0:1], tag="b")
        p.bar()
        free("ln_m2b", "ln_t1b", "ln_MRBb", "ln_RBb", "ln_rowsb", "ln_sqb", "x_sb",
             "ln_rbb", "ln_ssb")
        PS8 = [ps(f"PS{i}", [128, 512]) for i in range(8)]
        ymoe = sb("ymoe", [128, HKC * 1024])
        ym3 = ymoe[:].rearrange("p (c t) -> p c t", t=1024)
        act32 = sb("act32", [128, 16384], BF16, side="right")
        cbc = act32
        cbc3 = act32[:, 0:8192].rearrange("p (a b) -> p a b", b=1024)
        hec3 = act32[:, 8192:16384].rearrange("p (a b) -> p a b", b=1024)
        wbig = sb("wbig", [128, 16384], BF16, side="right")
        p.op("gpsimd", lambda g: g.memset(ymoe[:], 0.0))
        p.bar()

        for tt in range(8):
            for kc in range(HKC):
                p.op("tensor", lambda t, tt=tt, kc=kc: t.matmul(
                    PS8[tt][:, 0:8], h23[:, kc, tt * 128:(tt + 1) * 128],
                    rwb3[:, kc, :], start=(kc == 0), stop=(kc == HKC - 1)))
        p.bar()
        mask2f = sb("mask2f", [128, 8])
        for tt in range(8):
            p.op("vector", lambda v, tt=tt: v.tensor_copy(scr1[:], PS8[tt][:, 0:8]))
            p.bar()
            p.op("vector", lambda v: v.reduce_max(m1c[:], scr1[:], axis=AX.X))
            p.bar()
            p.op("vector", lambda v: v.tensor_scalar(scr2[:], scr1[:], m1c[:, 0:1], None, op0=ALU.subtract))
            p.bar()
            p.op("vector", lambda v: v.tensor_scalar(scri[:], scr2[:], 0.0, None, op0=ALU.is_ge))
            p.bar()
            p.op("vector", lambda v: v.tensor_copy(scrf[:], scri[:]))
            p.bar()
            p.op("vector", lambda v: v.scalar_tensor_tensor(scr2[:], scrf[:], -1e30, scr1[:],
                                                            op0=ALU.mult, op1=ALU.add))
            p.bar()
            p.op("vector", lambda v: v.reduce_max(m2c[:], scr2[:], axis=AX.X))
            p.bar()
            p.op("vector", lambda v: v.tensor_scalar(scr2[:], scr2[:], m2c[:, 0:1], None, op0=ALU.subtract))
            p.bar()
            p.op("vector", lambda v: v.tensor_scalar(scri[:], scr2[:], 0.0, None, op0=ALU.is_ge))
            p.bar()
            p.op("vector", lambda v: v.tensor_copy(mask2f[:], scri[:]))
            p.op("vector", lambda v: v.tensor_tensor(m2c[:], m1c[:], m2c[:], op=ALU.subtract))
            p.bar()
            p.op("scalar", lambda s: s.activation(p1c[:], m2c[:], AF.Sigmoid))
            p.bar()
            p.op("vector", lambda v: v.tensor_scalar(p2c[:], p1c[:], -1.0, 1.0,
                                                     op0=ALU.mult, op1=ALU.add))
            p.op("vector", lambda v: v.tensor_scalar(combt[:], scrf[:], p1c[:, 0:1], None, op0=ALU.mult))
            p.bar()
            p.op("vector", lambda v: v.scalar_tensor_tensor(combt[:], mask2f[:], p2c[:, 0:1],
                                                            combt[:], op0=ALU.mult, op1=ALU.add))
            p.bar()
            p.op("tensor", lambda t, tt=tt: t.transpose(PS8[tt][0:8, 128:256], combt[:],
                                                        idf[:]))
            p.bar()
            p.op("vector", lambda v, tt=tt: v.tensor_copy(
                combF[:, tt * 128:(tt + 1) * 128], PS8[tt][0:8, 128:256]))
            p.bar()
        for e in range(8):
            for f in range(2):
                p.op("tensor", lambda t, e=e, f=f: t.matmul(
                    PS8[e % 4][:, f * 256:(f + 1) * 256] if False else PS8[e % 4][:, 0:512],
                    oh8f[:, e * 128:(e + 1) * 128], combF[:, f * 512:(f + 1) * 512],
                    start=True, stop=True) if f == 0 else t.matmul(
                    PS8[4 + e % 4][:, 0:512],
                    oh8f[:, e * 128:(e + 1) * 128], combF[:, f * 512:(f + 1) * 512],
                    start=True, stop=True))
            if e % 4 == 3:
                p.bar()
                for u in range(e - 3, e + 1):
                    p.op("vector", lambda v, u=u: v.tensor_copy(
                        cbc3[:, u, 0:512], PS8[u % 4][:, 0:512]))
                    p.op("vector", lambda v, u=u: v.tensor_copy(
                        cbc3[:, u, 512:1024], PS8[4 + u % 4][:, 0:512]))
                p.bar()

        wq = [wbig[:, i * 4096:(i + 1) * 4096].rearrange("p (c m) -> p c m", m=256)
              for i in range(4)]
        wqf = [wbig[:, i * 4096:(i + 1) * 4096] for i in range(4)]
        iters = [(e, mq) for e in range(8) for mq in range(4)]
        def ld_exp(i):
            e, mq = iters[i]
            par = i % 2
            p.dma("gpsimd", lambda g, e=e, mq=mq, par=par: g.dma_start(
                out=wqf[par], in_=WG4[:, e, :, mq * 256:(mq + 1) * 256]))
            p.dma("gpsimd", lambda g, e=e, mq=mq, par=par: g.dma_start(
                out=wqf[2 + par], in_=WU4[:, e, :, mq * 256:(mq + 1) * 256]))
        ld_exp(0)
        p.bar()
        for i in range(32):
            e, mq = iters[i]
            par = i % 2
            bs = (i % 2) * 4
            if i + 1 < 32 and mq != 3:
                ld_exp(i + 1)
            for mc in range(2):
                for f in range(2):
                    for kc in range(HKC):
                        p.op("tensor", lambda t, mc=mc, f=f, kc=kc, par=par, bs=bs: t.matmul(
                            PS8[bs + f][:, 0:512], wq[par][:, kc, mc * 128:(mc + 1) * 128],
                            h23[:, kc, f * 512:(f + 1) * 512],
                            start=(kc == 0), stop=(kc == HKC - 1)))
                    for kc in range(HKC):
                        p.op("tensor", lambda t, mc=mc, f=f, kc=kc, par=par, bs=bs: t.matmul(
                            PS8[bs + 2 + f][:, 0:512], wq[2 + par][:, kc, mc * 128:(mc + 1) * 128],
                            h23[:, kc, f * 512:(f + 1) * 512],
                            start=(kc == 0), stop=(kc == HKC - 1)))
                p.bar()
                mg = mq * 2 + mc
                for f in range(2):
                    fs = slice(f * 512, (f + 1) * 512)
                    p.op("scalar", lambda s, mg=mg, f=f, fs=fs, bs=bs: s.activation(
                        hec3[:, mg, fs], PS8[bs + f][:, 0:512], AF.Silu))
                p.bar()
                for f in range(2):
                    fs = slice(f * 512, (f + 1) * 512)
                    p.op("vector", lambda v, mg=mg, f=f, fs=fs, bs=bs: v.tensor_tensor(
                        hec3[:, mg, fs], hec3[:, mg, fs], PS8[bs + 2 + f][:, 0:512], op=ALU.mult))
                p.bar()
                for f in range(2):
                    fs = slice(f * 512, (f + 1) * 512)
                    p.op("vector", lambda v, e=e, mg=mg, fs=fs: v.tensor_tensor(
                        hec3[:, mg, fs], hec3[:, mg, fs], cbc3[:, e, fs], op=ALU.mult))
                p.bar()
            if mq == 3:
                for mh2 in range(2):
                    dwp = wbig[:, mh2 * 8192:(mh2 + 1) * 8192]
                    dw3 = dwp.rearrange("p (c m) -> p c m", m=1024)
                    p.dma("gpsimd", lambda g, e=e, mh2=mh2, dwp=dwp: g.dma_start(
                        out=dwp, in_=WD4[:, e, :, mh2 * 1024:(mh2 + 1) * 1024]))
                    p.bar()
                    for mo2 in range(8):
                        mo = mh2 * 8 + mo2
                        for f in range(2):
                            pu = PS8[4 + (mo * 2 + f) % 4]
                            for kc in range(8):
                                p.op("tensor", lambda t, mo2=mo2, f=f, kc=kc, pu=pu, dw3=dw3: t.matmul(
                                    pu[:], dw3[:, kc, mo2 * 128:(mo2 + 1) * 128],
                                    hec3[:, kc, f * 512:(f + 1) * 512],
                                    start=(kc == 0), stop=(kc == 7)))
                            if (mo * 2 + f) % 4 == 3:
                                p.bar()
                                for u in range(mo * 2 + f - 3, mo * 2 + f + 1):
                                    uo, uf = divmod(u, 2)
                                    p.op("vector", lambda v, uo=uo, uf=uf, u=u: v.tensor_tensor(
                                        ym3[:, uo, uf * 512:(uf + 1) * 512],
                                        ym3[:, uo, uf * 512:(uf + 1) * 512],
                                        PS8[4 + u % 4][:], op=ALU.add))
                                p.bar()
                if i + 1 < 32:
                    ld_exp(i + 1)
                    p.bar()

        if debug:
            p.dma("sync", lambda s: s.dma_start(out=OH2, in_=h2[:]))
            p.dma("sync", lambda s: s.dma_start(out=OCOMB, in_=combF[:]))
            p.dma("sync", lambda s: s.dma_start(out=OYM, in_=ymoe[:]))
            p.bar()
        shb3 = act32[:].rearrange("p (c t) -> p c t", t=1024)
        shgp = [wbig[:, par * 8192:par * 8192 + 4096]
                .rearrange("p (c m) -> p c m", m=256) for par in range(2)]
        shup = [wbig[:, par * 8192 + 4096:(par + 1) * 8192]
                .rearrange("p (c m) -> p c m", m=256) for par in range(2)]

        def ld_sh(ms2, par):
            p.dma("gpsimd", lambda g, ms2=ms2, par=par: g.dma_start(
                out=wbig[:, par * 8192:par * 8192 + 4096],
                in_=SHG3[:, :, ms2 * 256:(ms2 + 1) * 256]))
            p.dma("gpsimd", lambda g, ms2=ms2, par=par: g.dma_start(
                out=wbig[:, par * 8192 + 4096:(par + 1) * 8192],
                in_=SHU3[:, :, ms2 * 256:(ms2 + 1) * 256]))

        ld_sh(0, 0)
        p.bar()
        for ms2 in range(8):
            par = ms2 % 2
            if ms2 + 1 < 8:
                ld_sh(ms2 + 1, (ms2 + 1) % 2)
            for mc in range(2):
                mg = ms2 * 2 + mc
                for f in range(2):
                    for kc in range(HKC):
                        p.op("tensor", lambda t, mc=mc, f=f, kc=kc, par=par: t.matmul(
                            PS8[f][:, 0:512], shgp[par][:, kc, mc * 128:(mc + 1) * 128],
                            h23[:, kc, f * 512:(f + 1) * 512],
                            start=(kc == 0), stop=(kc == HKC - 1)))
                    for kc in range(HKC):
                        p.op("tensor", lambda t, mc=mc, f=f, kc=kc, par=par: t.matmul(
                            PS8[2 + f][:, 0:512], shup[par][:, kc, mc * 128:(mc + 1) * 128],
                            h23[:, kc, f * 512:(f + 1) * 512],
                            start=(kc == 0), stop=(kc == HKC - 1)))
                p.bar()
                for f in range(2):
                    fs = slice(f * 512, (f + 1) * 512)
                    p.op("scalar", lambda s, mg=mg, f=f, fs=fs: s.activation(
                        shb3[:, mg, fs], PS8[f][:, 0:512], AF.Silu))
                p.bar()
                for f in range(2):
                    fs = slice(f * 512, (f + 1) * 512)
                    p.op("vector", lambda v, mg=mg, f=f, fs=fs: v.tensor_tensor(
                        shb3[:, mg, fs], shb3[:, mg, fs], PS8[2 + f][:, 0:512], op=ALU.mult))
                p.bar()
        for tt in range(8):
            for kc in range(HKC):
                p.op("tensor", lambda t, tt=tt, kc=kc: t.matmul(
                    PS8[tt % 4][:, 0:1], h23[:, kc, tt * 128:(tt + 1) * 128],
                    segb[:, kc:kc + 1], start=(kc == 0), stop=(kc == HKC - 1)))
            if tt % 4 == 3:
                p.bar()
                for u in range(tt - 3, tt + 1):
                    p.op("scalar", lambda s, u=u: s.activation(
                        m1c[:], PS8[u % 4][:, 0:1], AF.Sigmoid))
                    p.bar()
                    p.op("tensor", lambda t, u=u: t.transpose(
                        PS8[4][0:1, 0:128], m1c[:], idf[:]))
                    p.bar()
                    p.op("vector", lambda v, u=u: v.tensor_copy(
                        sgrow[:, u * 128:(u + 1) * 128], PS8[4][0:1, 0:128]))
                    p.bar()
        for f in range(2):
            p.op("tensor", lambda t, f=f: t.matmul(
                PS8[f][:, 0:512], orf[0:1, :], sgrow[0:1, f * 512:(f + 1) * 512],
                start=True, stop=True))
        p.bar()
        for f in range(2):
            p.op("vector", lambda v, f=f: v.tensor_copy(
                sgb[:, f * 512:(f + 1) * 512], PS8[f][:, 0:512]))
        p.bar()
        if debug:
            p.dma("sync", lambda s: s.dma_start(out=OSG, in_=sgrow[:]))
            p.dma("sync", lambda s: s.dma_start(out=OSHB, in_=act32[:]))
            p.bar()
        wsdp = [wbig[:, par * 4096:(par + 1) * 4096]
                .rearrange("p (c m) -> p c m", m=256) for par in range(2)]
        xbuf = wbig[:, 8192:16384].bitcast(F32)
        scrt = sb("scrt", [128, 512], side="right")
        scrt2 = sb("scrt2", [128, 512], side="right")

        def ld_shd(ms2, par):
            p.dma("gpsimd", lambda g, ms2=ms2, par=par: g.dma_start(
                out=wbig[:, par * 4096:(par + 1) * 4096],
                in_=SHD3[:, :, ms2 * 256:(ms2 + 1) * 256]))

        ld_shd(0, 0)
        p.bar()
        for ms2 in range(8):
            par = ms2 % 2
            if ms2 + 1 < 8:
                ld_shd(ms2 + 1, (ms2 + 1) % 2)
            for mc in range(2):
                mo = ms2 * 2 + mc
                for f in range(2):
                    pu = PS8[4 + f]
                    for kc in range(HKC):
                        p.op("tensor", lambda t, mc=mc, f=f, kc=kc, pu=pu, par=par: t.matmul(
                            pu[:], wsdp[par][:, kc, mc * 128:(mc + 1) * 128],
                            shb3[:, kc, f * 512:(f + 1) * 512],
                            start=(kc == 0), stop=(kc == HKC - 1)))
                p.bar()
                for f in range(2):
                    fs = slice(f * 512, (f + 1) * 512)
                    p.op("vector", lambda v, f=f, fs=fs: v.tensor_tensor(
                        scrt[:] if f == 0 else scrt2[:], PS8[4 + f][:], sgb[:, fs], op=ALU.mult))
                p.bar()
                for f in range(2):
                    fs = slice(f * 512, (f + 1) * 512)
                    p.op("vector", lambda v, mo=mo, f=f, fs=fs: v.tensor_tensor(
                        ym3[:, mo, fs], ym3[:, mo, fs], scrt[:] if f == 0 else scrt2[:], op=ALU.add))
                p.bar()
        p.dma("sync", lambda s: s.dma_start(out=xbuf[:, 0:1024], in_=X3d[:, 0, :]))
        p.bar()
        for kc in range(HKC):
            if kc + 1 < HKC:
                p.dma("sync", lambda s, kc=kc: s.dma_start(
                    out=xbuf[:, (kc + 1) % 2 * 1024:((kc + 1) % 2 + 1) * 1024],
                    in_=X3d[:, kc + 1, :]))
            if kc > 0:
                p.dma("scalar", lambda s, kc=kc: s.dma_start(
                    out=OX3[:, kc - 1, :], in_=ym3[:, kc - 1, :]))
            p.op("vector", lambda v, kc=kc: v.scalar_tensor_tensor(
                ym3[:, kc, :], ym3[:, kc, :], gb_c[:, kc:kc + 1],
                xbuf[:, kc % 2 * 1024:(kc % 2 + 1) * 1024], op0=ALU.mult, op1=ALU.add))
            p.bar()
        p.dma("sync", lambda s: s.dma_start(out=OX3[:, HKC - 1, :], in_=ym3[:, HKC - 1, :]))
        p.bar()
        _cleanup(cms)
        _fin(p, ctx)
    return nc


def stage_st3(inputs, r2c, mods):
    wg = np.asarray(inputs["L0_ffn_w_gate"])
    wu = np.asarray(inputs["L0_ffn_w_up"])
    wd = np.asarray(inputs["L0_ffn_w_down"])
    WG = np.concatenate([km(wg[e].T) for e in range(8)], axis=1)
    WU = np.concatenate([km(wu[e].T) for e in range(8)], axis=1)
    WD = np.concatenate([km(wd[e].T) for e in range(8)], axis=1)
    SHG = km(np.asarray(inputs["L0_ffn_sh_gate"]).T)
    SHU = km(np.asarray(inputs["L0_ffn_sh_up"]).T)
    SHD = km(np.asarray(inputs["L0_ffn_sh_down"]).T)
    RW = km(np.asarray(inputs["L0_ffn_gate_w"]).T)
    SEG = km(np.asarray(inputs["L0_ffn_se_gate_w"]).T)
    identf = np.eye(128, dtype=np.float32)
    oh8f = np.zeros((8, 8 * 128), np.float32)
    for j in range(8):
        oh8f[j, j * 128:(j + 1) * 128] = 1.0
    maps = []
    for core in range(NCORES):
        b, q = divmod(core, 4)
        maps.append(dict(
            x1=np.asarray(r2c[core]["x1"]),
            cb0=mods[0][b]["cb"], sb0=mods[0][b]["sb"], gb0=mods[0][b]["gb"],
            routw=RW, wg=WG, wu=WU, wd=WD, shg=SHG, shu=SHU, shd=SHD, seg=SEG,
            identf=identf, oh8f=oh8f,
        ))
    return maps



def build_st4a():
    nc = _mknc()
    X = nc.dram_tensor("x2", [128, HKC * 1024], F32, kind="ExternalInput").ap()
    CA = nc.dram_tensor("ca1", [128, HKC], F32, kind="ExternalInput").ap()
    SA = nc.dram_tensor("sa1", [128, HKC], F32, kind="ExternalInput").ap()
    ANW = nc.dram_tensor("anw", [128, HKC], F32, kind="ExternalInput").ap()
    QW = nc.dram_tensor("qw", [128, HKC * 2048], F32, kind="ExternalInput").ap()
    KW = nc.dram_tensor("kw", [128, HKC * 2048], F32, kind="ExternalInput").ap()
    VW = nc.dram_tensor("vw", [128, HKC * 4096], F32, kind="ExternalInput").ap()
    GW = nc.dram_tensor("gw", [128, HKC * 4096], F32, kind="ExternalInput").ap()
    COS = nc.dram_tensor("cosx", [128, 1024], F32, kind="ExternalInput").ap()
    SIN = nc.dram_tensor("sinx", [128, 1024], F32, kind="ExternalInput").ap()
    CDQ = nc.dram_tensor("cdq", [128, 8 * 64], F32, kind="ExternalInput").ap()
    KD = nc.dram_tensor("kdc", [128, 8 * 64], F32, kind="ExternalInput").ap()
    CDG = nc.dram_tensor("cdg", [128, 8 * 1024], BF16, kind="ExternalInput").ap()
    CD64 = nc.dram_tensor("cd64", [128, 8], F32, kind="ExternalInput").ap()
    DMT = nc.dram_tensor("dmaskT", [64, 8 * 64], F32, kind="ExternalInput").ap()
    IDB = nc.dram_tensor("identbf", [128, 128], BF16, kind="ExternalInput").ap()
    OH = nc.dram_tensor("hres", [128, HKC * 1024], F32, kind="ExternalOutput").ap()
    OOP = nc.dram_tensor("opart", [128, 8 * 4096], BF16, kind="ExternalOutput").ap()
    OQG = nc.dram_tensor("qg", [128, HKC * 1024], BF16, kind="ExternalOutput").ap()
    OG = nc.dram_tensor("gtok", [128, 8 * 4096], BF16, kind="ExternalOutput").ap()
    OSR = nc.dram_tensor("soutr", [128, 2 * 8 * 512], BF16, kind="ExternalOutput").ap()

    QW3 = QW.rearrange("p (c m) -> p c m", m=2048)
    KW3 = KW.rearrange("p (c m) -> p c m", m=2048)
    VW3 = VW.rearrange("p (c m) -> p c m", m=4096)
    GW3 = GW.rearrange("p (c m) -> p c m", m=4096)
    OG3 = OG.rearrange("p (a b) -> p a b", b=4096)
    OOP3 = OOP.rearrange("p (a b) -> p a b", b=4096)

    with ExitStack() as ctx:
        cms = {}
        def sb(n, s, d=F32, side=None):
            n = n + "_s"
            cm = nc.sbuf_tensor(n, s, d, side=side)
            h = cm.__enter__(); cms[n] = cm; return h
        def ps(n, s, d=F32):
            n = n + "_s"
            cm = nc.psum_tensor(n, s, d)
            h = cm.__enter__(); cms[n] = cm; return h
        def free(*names):
            for n in names:
                cms.pop(n + "_s").__exit__(None, None, None)

        ocb = sb("ocb", [128, 1], BF16)
        ocf = sb("ocf", [128, 1]); orf = sb("orf", [1, 128])
        epsc = sb("epsc", [1, 1]); epsc2 = sb("eps2", [1, 1])
        ca_c = sb("ca_c", [128, HKC]); sa_c = sb("sa_c", [128, HKC])
        anw_c = sb("anw_c", [128, HKC])
        cosb = sb("cosb", [128, 1024]); sinb = sb("sinb", [128, 1024])
        cdq = sb("cdq", [128, 8 * 64]); kdc = sb("kdc", [128, 8 * 64])
        cdg = sb("cdg", [128, 8 * 1024], BF16)
        cd64 = sb("cd64", [128, 8])
        dmt = sb("dmt", [64, 8 * 64])
        idb = sb("idb", [128, 128], BF16)
        hh = sb("hh", [128, HKC * 1024], BF16, side="right")
        hh3 = hh[:].rearrange("p (c t) -> p c t", t=1024)

        x_sb = sb("x_sb", [128, HKC * 1024], side="right")
        x3 = x_sb[:].rearrange("p (c t) -> p c t", t=1024)
        hres = x_sb
        hr3 = x3

        p = _mkp(nc)
        for dst, src in [(ca_c, CA), (sa_c, SA), (anw_c, ANW), (cosb, COS),
                         (sinb, SIN), (cdq, CDQ), (kdc, KD), (cd64, CD64),
                         (dmt, DMT), (idb, IDB), (cdg, CDG)]:
            p.dma("sync", lambda s, dst=dst, src=src: s.dma_start(out=dst[:], in_=src))
        p.dma("sync", lambda s: s.dma_start(out=x_sb[:], in_=X))
        p.op("gpsimd", lambda g: g.memset(ocb[:], 1.0))
        p.op("gpsimd", lambda g: g.memset(ocf[:], 1.0))
        p.op("gpsimd", lambda g: g.memset(orf[:], 1.0))
        p.op("gpsimd", lambda g: g.memset(epsc[:], 1e-5))
        p.op("gpsimd", lambda g: g.memset(epsc2[:], 1e-6))
        p.bar()
        p.op("vector", lambda v: v.tensor_scalar(ca_c[:], ca_c[:], 1.0, None, op0=ALU.add))
        p.bar()
        emit_ln_mod(p, lambda n, sh, d=F32: sb(n, sh, d, side="right"), ps, nc,
                    x3, hr3, ca_c, sa_c, ocb, ocf, orf, HKC, 1024,
                    eps_ap=epsc[0:1, 0:1], tag="a4")
        p.bar()
        free("ln_m2a4", "ln_t1a4", "ln_MRBa4", "ln_RBa4", "ln_rowsa4", "ln_sqa4",
             "ln_rba4", "ln_ssa4")
        p.dma("sync", lambda s: s.dma_start(out=OH, in_=hres[:]))
        sqb = sb("sqb", [128, HKC * 1024], BF16, side="right")
        sq3 = sqb[:].rearrange("p (c t) -> p c t", t=1024)
        rrow = sb("rrow", [1, 1024])
        RBr = sb("RBr", [128, 1024])
        SSr = ps("SSr", [1, 512])
        RBp = ps("RBp", [128, 512])
        p.op("scalar", lambda s: s.activation(sqb[:], hres[:], AF.Square))
        p.bar()
        for f in range(2):
            for kc in range(HKC):
                p.op("tensor", lambda t, f=f, kc=kc: t.matmul(
                    SSr[:, 0:512], ocb[:, 0:1], sq3[:, kc, f * 512:(f + 1) * 512],
                    start=(kc == 0), stop=(kc == HKC - 1)))
            p.bar()
            p.op("scalar", lambda s, f=f: s.activation(
                rrow[:, f * 512:(f + 1) * 512], SSr[:, 0:512], AF.Ln,
                scale=1.0 / 2048, bias=epsc2[0:1, 0:1]))
            p.bar()
        p.op("scalar", lambda s: s.activation(rrow[:], rrow[:], AF.Exp, scale=-0.5))
        p.bar()
        for f in range(2):
            p.op("tensor", lambda t, f=f: t.matmul(
                RBp[:, 0:512], orf[0:1, :], rrow[0:1, f * 512:(f + 1) * 512],
                start=True, stop=True))
            p.bar()
            p.op("vector", lambda v, f=f: v.tensor_copy(
                RBr[:, f * 512:(f + 1) * 512], RBp[:, 0:512]))
            p.bar()
        for kc in range(HKC):
            p.op("vector", lambda v, kc=kc: v.scalar_tensor_tensor(
                hh3[:, kc, :], hr3[:, kc, :], anw_c[:, kc:kc + 1], RBr[:],
                op0=ALU.mult, op1=ALU.mult))
        p.bar()
        free("RBr", "rrow", "sqb", "x_sb", "RBp", "SSr")

        qb = sb("qb", [128, HKC * 1024], BF16)
        qb3 = qb[:].rearrange("p (c t) -> p c t", t=1024)
        kb = sb("kb", [128, HKC * 1024], BF16)
        kb3 = kb[:].rearrange("p (c t) -> p c t", t=1024)
        vtok = sb("vtok", [128, 8 * 4096], BF16)
        vt3 = vtok[:].rearrange("p (a b) -> p a b", b=4096)
        PS8 = [ps(f"PS{i}", [128, 512]) for i in range(8)]
        wsl = sb("wsl", [128, HKC * 256], BF16, side="right")
        qtmp = sb("qtmp", [128, 2 * 1024], side="right")
        qt3 = qtmp[:].rearrange("p (a b) -> p a b", b=1024)
        wlh = [wsl[:, par * HKC * 128:(par + 1) * HKC * 128]
               .rearrange("p (c m) -> p c m", m=128) for par in range(2)]

        def ld4a(W3s, mcg, par):
            p.dma("gpsimd", lambda g, W3s=W3s, mcg=mcg, par=par: g.dma_start(
                out=wsl[:, par * HKC * 128:(par + 1) * HKC * 128],
                in_=W3s[:, :, mcg * 128:(mcg + 1) * 128]))

        for which, W3s, dst3 in [(0, QW3, qb3), (1, KW3, kb3)]:
            ld4a(W3s, 0, 0)
            p.bar()
            for hd in range(8):
                for mc in range(2):
                    mcg = hd * 2 + mc
                    par = mcg % 2
                    if mcg + 1 < 16:
                        ld4a(W3s, mcg + 1, (mcg + 1) % 2)
                    for f in range(2):
                        pu = PS8[mc * 2 + f]
                        for kc in range(HKC):
                            p.op("tensor", lambda t, mc=mc, f=f, kc=kc, pu=pu, par=par: t.matmul(
                                pu[:], wlh[par][:, kc, :],
                                hh3[:, kc, f * 512:(f + 1) * 512],
                                start=(kc == 0), stop=(kc == HKC - 1)))
                    p.bar()
                for mc in range(2):
                    for f in range(2):
                        p.op("vector", lambda v, mc=mc, f=f: v.tensor_copy(
                            qt3[:, mc, f * 512:(f + 1) * 512], PS8[mc * 2 + f][:]))
                p.bar()
                o1, o2 = dst3[:, 2 * hd, :], dst3[:, 2 * hd + 1, :]
                scl = 0.0625 if which == 0 else 1.0
                p.op("vector", lambda v, o1=o1: v.tensor_tensor(o1, qt3[:, 0, :], cosb[:], op=ALU.mult))
                p.op("vector", lambda v, o2=o2: v.tensor_tensor(o2, qt3[:, 1, :], cosb[:], op=ALU.mult))
                p.bar()
                p.op("vector", lambda v: v.tensor_tensor(qt3[:, 0, :], qt3[:, 0, :], sinb[:], op=ALU.mult))
                p.op("vector", lambda v: v.tensor_tensor(qt3[:, 1, :], qt3[:, 1, :], sinb[:], op=ALU.mult))
                p.bar()
                p.op("vector", lambda v, o1=o1: v.tensor_tensor(o1, o1, qt3[:, 1, :], op=ALU.subtract))
                p.op("vector", lambda v, o2=o2: v.tensor_tensor(o2, o2, qt3[:, 0, :], op=ALU.add))
                p.bar()
                if which == 0:
                    p.op("vector", lambda v, o1=o1: v.tensor_scalar(o1, o1, scl, None, op0=ALU.mult))
                    p.op("vector", lambda v, o2=o2: v.tensor_scalar(o2, o2, scl, None, op0=ALU.mult))
                    p.bar()
        for which, W3s in [(0, VW3), (1, GW3)]:
            ld4a(W3s, 0, 0)
            p.bar()
            for fs in range(32):
                par = fs % 2
                if fs + 1 < 32:
                    ld4a(W3s, fs + 1, (fs + 1) % 2)
                for tt in range(8):
                    for kc in range(HKC):
                        p.op("tensor", lambda t, tt=tt, kc=kc, par=par: t.matmul(
                            PS8[tt][:, 0:128], hh3[:, kc, tt * 128:(tt + 1) * 128],
                            wlh[par][:, kc, :], start=(kc == 0), stop=(kc == HKC - 1)))
                p.bar()
                if which == 0:
                    for tt in range(8):
                        p.op("vector", lambda v, tt=tt, fs=fs: v.tensor_copy(
                            vt3[:, tt, fs * 128:(fs + 1) * 128], PS8[tt][:, 0:128]))
                    p.bar()
                else:
                    gsl = qtmp[:].bitcast(BF16)
                    for tt in range(8):
                        p.op("vector", lambda v, tt=tt, fs=fs: v.tensor_copy(
                            gsl[:, (fs % 2) * 1024 + tt * 128:(fs % 2) * 1024 + (tt + 1) * 128],
                            PS8[tt][:, 0:128]))
                    p.bar()
                    p.dma("sync", lambda s, fs=fs: s.dma_start(
                        out=OG3[:, :, fs * 128:(fs + 1) * 128],
                        in_=gsl[:, (fs % 2) * 1024:(fs % 2) * 1024 + 1024]
                        .rearrange("p (a b) -> p a b", b=128)))
        p.bar()
        qg3 = qtmp[:].bitcast(BF16)[:, 0:2048].rearrange("p (a b) -> p a b", b=1024)
        OQG3 = OQG.rearrange("p (c t) -> p c t", t=1024)
        for hd in range(8):
            for kc2 in range(2):
                p.op("vector", lambda v, hd=hd, kc2=kc2: v.tensor_tensor(
                    qg3[:, kc2, :], qb3[:, 2 * hd + kc2, :],
                    cdg[:, hd * 1024:(hd + 1) * 1024], op=ALU.mult))
            p.bar()
            p.dma("sync", lambda s, hd=hd: s.dma_start(
                out=OQG3[:, 2 * hd:2 * hd + 2, :], in_=qg3[:, :, :]))
            p.bar()
        free("qtmp", "wsl", "hh")

        sloc = sb("sloc", [128, 2 * 8 * 512], BF16)
        sl4 = sloc[:].rearrange("p (k h v) -> p k h v", k=2, h=8)
        vstg = sb("vstg", [64, 8 * 512], BF16)
        vs3 = vstg[:].rearrange("p (h v) -> p h v", v=512)
        kstg = sb("kstg", [64, 16 * 128], BF16)
        ks3 = kstg[:].rearrange("p (a b) -> p a b", b=128)
        qcd = sb("qcd", [128, 16 * 64], BF16)
        qc3 = qcd[:].rearrange("p (a b) -> p a b", b=64)
        kkd = sb("kkd", [128, 16 * 64], BF16)
        kk3 = kkd[:].rearrange("p (a b) -> p a b", b=64)
        attns = sb("attns", [64, 8 * 64], BF16)
        at3 = attns[:].rearrange("p (a b) -> p a b", b=64)
        obuf = sb("obuf", [128, 4096], BF16)
        p.op("gpsimd", lambda g: g.memset(sloc[:], 0.0))
        p.bar()
        for c in range(16):
            tt = c // 2
            pb = (c % 2) * 64
            for hd in range(8):
                p.op("vector", lambda v, tt=tt, pb=pb, hd=hd: v.tensor_copy(
                    vs3[:, hd, :], vt3[pb:pb + 64, tt, hd * 512:(hd + 1) * 512]))
                for kc2 in range(2):
                    p.op("vector", lambda v, tt=tt, pb=pb, hd=hd, kc2=kc2: v.tensor_tensor(
                        qc3[:, 2 * hd + kc2, :],
                        qb3[:, 2 * hd + kc2, tt * 128 + pb:tt * 128 + pb + 64],
                        cdq[:, hd * 64:(hd + 1) * 64], op=ALU.mult))
                    p.op("vector", lambda v, tt=tt, pb=pb, hd=hd, kc2=kc2: v.tensor_tensor(
                        kk3[:, 2 * hd + kc2, :],
                        kb3[:, 2 * hd + kc2, tt * 128 + pb:tt * 128 + pb + 64],
                        kdc[:, hd * 64:(hd + 1) * 64], op=ALU.mult))
            p.bar()
            for hd in range(8):
                for kc2 in range(2):
                    p.op("tensor", lambda t, tt=tt, pb=pb, hd=hd, kc2=kc2: t.matmul(
                        PS8[0][0:64, hd * 64:(hd + 1) * 64],
                        kb3[:, 2 * hd + kc2, tt * 128 + pb:tt * 128 + pb + 64],
                        qb3[:, 2 * hd + kc2, tt * 128 + pb:tt * 128 + pb + 64],
                        start=(kc2 == 0), stop=(kc2 == 1)))
                for kc2 in range(2):
                    u = 2 * hd + kc2
                    p.op("tensor", lambda t, u=u: t.transpose(
                        PS8[1 + u // 8][0:64, (u % 8) * 64:(u % 8) * 64 + 64].bitcast(BF16),
                        kk3[:, u, :], idb[:]))
            p.bar()
            for hd in range(8):
                p.op("vector", lambda v, hd=hd: v.tensor_tensor(
                    at3[:, hd, :], PS8[0][0:64, hd * 64:(hd + 1) * 64],
                    dmt[:, hd * 64:(hd + 1) * 64], op=ALU.mult))
            p.op("vector", lambda v: v.tensor_copy(
                kstg[:, 0:1024], PS8[1][0:64, :].bitcast(BF16)))
            p.op("vector", lambda v: v.tensor_copy(
                kstg[:, 1024:2048], PS8[2][0:64, :].bitcast(BF16)))
            p.bar()
            for half in range(2):
                for hh_ in range(4):
                    hd = half * 4 + hh_
                    pu = PS8[2 + hh_]
                    p.op("tensor", lambda t, hd=hd, pu=pu: t.matmul(
                        pu[0:64, 0:512], at3[:, hd, :], vs3[:, hd, :],
                        start=True, stop=False))
                    p.op("tensor", lambda t, hd=hd, pu=pu: t.matmul(
                        pu[0:64, 0:512], qc3[:, 2 * hd, :], sl4[:, 0, hd, :],
                        start=False, stop=False))
                    p.op("tensor", lambda t, hd=hd, pu=pu: t.matmul(
                        pu[0:64, 0:512], qc3[:, 2 * hd + 1, :], sl4[:, 1, hd, :],
                        start=False, stop=True))
                p.bar()
                for hh_ in range(4):
                    hd = half * 4 + hh_
                    p.op("vector", lambda v, pb=pb, hd=hd, hh_=hh_: v.tensor_copy(
                        obuf[pb:pb + 64, hd * 512:(hd + 1) * 512], PS8[2 + hh_][0:64, 0:512]))
                p.bar()
            for grp in range(4):
                for j in range(4):
                    u = grp * 4 + j
                    hd, kc2 = divmod(u, 2)
                    p.op("tensor", lambda t, hd=hd, kc2=kc2, j=j: t.matmul(
                        PS8[2 + j][:, 0:512], ks3[:, 2 * hd + kc2, :], vs3[:, hd, :],
                        start=True, stop=True))
                p.bar()
                for j in range(4):
                    u = grp * 4 + j
                    hd, kc2 = divmod(u, 2)
                    p.op("vector", lambda v, hd=hd, kc2=kc2, j=j: v.scalar_tensor_tensor(
                        sl4[:, kc2, hd, :], sl4[:, kc2, hd, :], cd64[:, hd:hd + 1],
                        PS8[2 + j][:, 0:512], op0=ALU.mult, op1=ALU.add))
                p.bar()
            if c % 2 == 1:
                p.dma("sync", lambda s, tt=tt: s.dma_start(
                    out=OOP3[:, tt, :], in_=obuf[:]))
                p.bar()
        p.dma("sync", lambda s: s.dma_start(out=OSR, in_=sloc[:]))
        _cleanup(cms)
        _fin(p, ctx)
    return nc


def stage_st4a(inputs, r3, mods):
    if r3 is None:
        r3 = [dict(x2=np.zeros((1, 1), np.float32))] * NCORES
        mods = {1: [dict(ca=np.zeros((128, HKC), np.float32),
                         sa=np.zeros((128, HKC), np.float32))] * 2}
    NH, DK = 8, 256
    gamma = 1.0 - np.exp2(-5.0 - np.arange(NH, dtype=np.float64))
    i64 = np.arange(64, dtype=np.float64)
    cdq = np.zeros((128, 8 * 64), np.float32)
    kdc = np.zeros((128, 8 * 64), np.float32)
    cdg = np.zeros((128, 8 * 1024), np.float32)
    dmt = np.zeros((64, 8 * 64), np.float32)
    for h in range(NH):
        cdq[:, h * 64:(h + 1) * 64] = (gamma[h] ** (i64 + 1.0))[None, :]
        kdc[:, h * 64:(h + 1) * 64] = (gamma[h] ** (63.0 - i64))[None, :]
        tloc = np.arange(1024, dtype=np.float64)
        cdg[:, h * 1024:(h + 1) * 1024] = (gamma[h] ** (tloc + 1.0))[None, :]
        diff = i64[None, :] - i64[:, None]
        dmt[:, h * 64:(h + 1) * 64] = np.where(diff >= 0, gamma[h] ** diff, 0.0)
    cd64 = np.broadcast_to((gamma ** 64.0).astype(np.float32), (128, 8)).copy()
    inv_freq = 1.0 / 10000.0 ** (np.arange(0, DK, 2, dtype=np.float64) / DK)
    QW = km(np.asarray(inputs["L1_attn_q_w"]))
    KW = km(np.asarray(inputs["L1_attn_k_w"]))
    VW = km(np.asarray(inputs["L1_attn_v_w"]))
    GW = km(np.asarray(inputs["L1_attn_g_w"]))
    anw = col16(np.asarray(inputs["L1_attn_attn_norm_w"]))
    identbf = np.eye(128).astype(BF)
    maps = []
    for core in range(NCORES):
        b, q = divmod(core, 4)
        tg = q * 1024 + np.arange(1024, dtype=np.float64)
        ang = tg[None, :] * inv_freq[:, None]
        maps.append(dict(
            x2=np.asarray(r3[core]["x2"]),
            ca1=mods[1][b]["ca"], sa1=mods[1][b]["sa"], anw=anw,
            qw=QW, kw=KW, vw=VW, gw=GW,
            cosx=np.cos(ang).astype(np.float32), sinx=np.sin(ang).astype(np.float32),
            cdq=cdq, kdc=kdc, cdg=cdg.astype(BF), cd64=cd64, dmaskT=dmt,
            identbf=identbf,
        ))
    return maps



def build_st4b():
    nc = _mknc()
    OP = nc.dram_tensor("opart", [128, 8 * 4096], BF16, kind="ExternalInput").ap()
    QG = nc.dram_tensor("qg", [128, HKC * 1024], BF16, kind="ExternalInput").ap()
    GT = nc.dram_tensor("gtok", [128, 8 * 4096], BF16, kind="ExternalInput").ap()
    SRA = nc.dram_tensor("soutr_g", [1024, 8192], BF16, kind="ExternalInput").ap()
    WC = nc.dram_tensor("wcols", [128, 64], F32, kind="ExternalInput").ap()
    GNW = nc.dram_tensor("gnw", [1, 512], F32, kind="ExternalInput").ap()
    IDB = nc.dram_tensor("identbf", [128, 128], BF16, kind="ExternalInput").ap()
    OOGF = nc.dram_tensor("ogf", [128, 32 * 1024], BF16, kind="ExternalOutput").ap()

    OP3 = OP.rearrange("p (a b) -> p a b", b=4096)
    GT3 = GT.rearrange("p (a b) -> p a b", b=4096)
    QG3 = QG.rearrange("p (c t) -> p c t", t=1024)
    OOGF3 = OOGF.rearrange("p (a b) -> p a b", b=1024)

    with ExitStack() as ctx:
        cms = {}
        def sb(n, s, d=F32, side=None):
            n = n + "_s"
            cm = nc.sbuf_tensor(n, s, d, side=side)
            h = cm.__enter__(); cms[n] = cm; return h
        def ps(n, s, d=F32):
            n = n + "_s"
            cm = nc.psum_tensor(n, s, d)
            h = cm.__enter__(); cms[n] = cm; return h
        def free(*names):
            for n in names:
                cms.pop(n + "_s").__exit__(None, None, None)

        orf = sb("orf", [1, 128]); eps2 = sb("eps2", [128, 1])
        wc = sb("wc", [128, 64]); gnwr = sb("gnwr", [1, 512])
        idb = sb("idb", [128, 128], BF16)
        qg = sb("qg", [128, HKC * 1024], BF16)
        qg3 = qg[:].rearrange("p (c t) -> p c t", t=1024)
        sicr = sb("sicr", [128, 8192], BF16)
        sic4 = sicr[:].rearrange("p (k h v) -> p k h v", k=2, h=8)
        gnwb = sb("gnwb", [128, 512])
        obuf = sb("obuf", [128, 4096])
        opb = sb("opb", [128, 4096], BF16)
        onb = sb("onb", [128, 4096], BF16)
        gbuf = sb("gbuf", [128, 4096], BF16)
        sgb = sb("sgb", [128, 4096], BF16)
        sqscr = sb("sqscr", [128, 512], BF16)
        ssq8 = sb("ssq8", [128, 8]); rstd8 = sb("rstd8", [128, 8])
        ogsl = sb("ogsl", [128, 32 * 128], BF16)
        PS8 = [ps(f"PS{i}", [128, 512]) for i in range(8)]

        p = _mkp(nc)
        for dst, src in [(wc, WC), (gnwr, GNW), (idb, IDB), (qg, QG)]:
            p.dma("sync", lambda s, dst=dst, src=src: s.dma_start(out=dst[:], in_=src))
        p.op("gpsimd", lambda g: g.memset(orf[:], 1.0))
        p.op("gpsimd", lambda g: g.memset(eps2[:], 1e-6))
        p.bar()
        p.op("tensor", lambda t: t.matmul(PS8[0][:, 0:512], orf[0:1, :], gnwr[0:1, :],
                                          start=True, stop=True))
        p.bar()
        p.op("vector", lambda v: v.tensor_copy(gnwb[:], PS8[0][:, 0:512]))
        p.bar()
        sicp = sb("sicp", [128, 8192], side="right")
        sbj = sb("sbj", [128, 2 * 8192], BF16, side="right")
        sbj3 = sbj[:].rearrange("p (a b) -> p a b", b=8192)
        p.op("gpsimd", lambda g: g.memset(sicp[:], 0.0))
        p.dma("sync", lambda s: s.dma_start(out=sbj3[:, 0, :], in_=SRA[0:128, :]))
        p.bar()
        for j in range(8):
            if j < 7:
                p.dma("sync", lambda s, j=j: s.dma_start(
                    out=sbj3[:, (j + 1) % 2, :], in_=SRA[(j + 1) * 128:(j + 2) * 128, :]))
            for u in range(16):
                p.op("vector", lambda v, j=j, u=u: v.scalar_tensor_tensor(
                    sicp[:, u * 512:(u + 1) * 512],
                    sbj3[:, j % 2, u * 512:(u + 1) * 512],
                    wc[:, j * 8 + u % 8:j * 8 + u % 8 + 1],
                    sicp[:, u * 512:(u + 1) * 512],
                    op0=ALU.mult, op1=ALU.add))
            p.bar()
        p.op("vector", lambda v: v.tensor_copy(sicr[:], sicp[:]))
        p.bar()
        free("sbj", "sicp")

        for tt in range(8):
            p.dma("sync", lambda s, tt=tt: s.dma_start(out=opb[:], in_=OP3[:, tt, :]))
            p.dma("sync", lambda s, tt=tt: s.dma_start(out=gbuf[:], in_=GT3[:, tt, :]))
            p.bar()
            for hd in range(8):
                for kc2 in range(2):
                    p.op("tensor", lambda t, tt=tt, hd=hd, kc2=kc2: t.matmul(
                        PS8[hd][:, 0:512],
                        qg3[:, 2 * hd + kc2, tt * 128:(tt + 1) * 128],
                        sic4[:, kc2, hd, :],
                        start=(kc2 == 0), stop=(kc2 == 1)))
            p.bar()
            for hd in range(8):
                hs = slice(hd * 512, (hd + 1) * 512)
                p.op("vector", lambda v, hd=hd, hs=hs: v.tensor_tensor(
                    obuf[:, hs], PS8[hd][:, 0:512], opb[:, hs], op=ALU.add))
            p.bar()
            for hd in range(8):
                hs = slice(hd * 512, (hd + 1) * 512)
                p.op("scalar", lambda s, hd=hd, hs=hs: s.activation(
                    sqscr[:], obuf[:, hs], AF.Square, accum_out=ssq8[:, hd:hd + 1]))
            p.bar()
            p.op("scalar", lambda s: s.activation(rstd8[:], ssq8[:], AF.Ln,
                                                  scale=1.0 / 512, bias=eps2[:, 0:1]))
            p.bar()
            p.op("scalar", lambda s: s.activation(rstd8[:], rstd8[:], AF.Exp, scale=-0.5))
            p.op("scalar", lambda s2: s2.activation(sgb[:], gbuf[:], AF.Sigmoid))
            p.bar()
            for hd in range(8):
                hs = slice(hd * 512, (hd + 1) * 512)
                p.op("vector", lambda v, hd=hd, hs=hs: v.tensor_scalar(
                    onb[:, hs], obuf[:, hs], rstd8[:, hd:hd + 1], None, op0=ALU.mult))
            p.bar()
            for hd in range(8):
                hs = slice(hd * 512, (hd + 1) * 512)
                p.op("vector", lambda v, hd=hd, hs=hs: v.tensor_tensor(
                    onb[:, hs], onb[:, hs], gnwb[:], op=ALU.mult))
            p.bar()
            p.op("vector", lambda v: v.tensor_tensor(sgb[:], sgb[:], gbuf[:], op=ALU.mult))
            p.bar()
            p.op("vector", lambda v: v.tensor_tensor(onb[:], onb[:], sgb[:], op=ALU.mult))
            p.bar()
            for grp in range(4):
                for i in range(8):
                    fc = grp * 8 + i
                    p.op("tensor", lambda t, fc=fc, i=i: t.transpose(
                        PS8[4 + i // 2][:, (i % 2) * 64:(i % 2) * 64 + 64].bitcast(BF16),
                        onb[:, fc * 128:(fc + 1) * 128], idb[:]))
                p.bar()
                for i in range(8):
                    fc = grp * 8 + i
                    p.op("vector", lambda v, fc=fc, i=i: v.tensor_copy(
                        ogsl[:, fc * 128:(fc + 1) * 128],
                        PS8[4 + i // 2][:, (i % 2) * 64:(i % 2) * 64 + 64].bitcast(BF16)))
                p.bar()
            p.dma("sync", lambda s, tt=tt: s.dma_start(
                out=OOGF3[:, :, tt * 128:(tt + 1) * 128],
                in_=ogsl[:].rearrange("p (a b) -> p a b", b=128)))
            p.bar()
        _cleanup(cms)
        _fin(p, ctx)
    return nc


def stage_st4b(inputs, r4a, mods):
    NH = 8
    gamma = 1.0 - np.exp2(-5.0 - np.arange(NH, dtype=np.float64))
    sra = np.concatenate([np.asarray(r4a[j]["soutr"]) for j in range(8)], axis=0)
    gnw = np.asarray(inputs["L1_attn_gnorm_w"]).reshape(1, 512)
    identbf = np.eye(128).astype(BF)
    maps = []
    for core in range(NCORES):
        b, q = divmod(core, 4)
        wc = np.zeros((8, 8), np.float64)
        for j in range(4 * b, 4 * b + q):
            steps = q - 1 - (j - 4 * b)
            wc[j] = gamma ** (1024.0 * steps)
        wcols = np.zeros((128, 64), np.float32)
        for j in range(8):
            wcols[:, j * 8:(j + 1) * 8] = wc[j][None, :]
        maps.append(dict(
            opart=np.asarray(r4a[core]["opart"]), qg=np.asarray(r4a[core]["qg"]),
            gtok=np.asarray(r4a[core]["gtok"]), soutr_g=sra,
            wcols=wcols, gnw=gnw, identbf=identbf,
        ))
    return maps



def build_st4c():
    nc = _mknc()
    OGF = nc.dram_tensor("ogf", [128, 32 * 1024], BF16, kind="ExternalInput").ap()
    H = nc.dram_tensor("hres", [128, HKC * 1024], F32, kind="ExternalInput").ap()
    X = nc.dram_tensor("x2", [128, HKC * 1024], F32, kind="ExternalInput").ap()
    OW = nc.dram_tensor("ow", [128, 32 * 2048], F32, kind="ExternalInput").ap()
    MNW = nc.dram_tensor("mnw", [128, HKC], F32, kind="ExternalInput").ap()
    GAW = nc.dram_tensor("gatew", [128, HKC * 11264], F32, kind="ExternalInput").ap()
    DWW = nc.dram_tensor("downw", [128, 44 * 2048], F32, kind="ExternalInput").ap()
    GA = nc.dram_tensor("ga1", [128, HKC], F32, kind="ExternalInput").ap()
    OX = nc.dram_tensor("x3", [128, HKC * 1024], F32, kind="ExternalOutput").ap()

    OW3 = OW.rearrange("p (c m) -> p c m", m=2048)
    GAW3 = GAW.rearrange("p (c m) -> p c m", m=11264)
    DWW3 = DWW.rearrange("p (c m) -> p c m", m=2048)
    OX3 = OX.rearrange("p (c t) -> p c t", t=1024)
    X3d = X.rearrange("p (c t) -> p c t", t=1024)
    H3d = H.rearrange("p (c t) -> p c t", t=1024)

    with ExitStack() as ctx:
        cms = {}
        def sb(n, s, d=F32, side=None):
            n = n + "_s"
            cm = nc.sbuf_tensor(n, s, d, side=side)
            h = cm.__enter__(); cms[n] = cm; return h
        def ps(n, s, d=F32):
            n = n + "_s"
            cm = nc.psum_tensor(n, s, d)
            h = cm.__enter__(); cms[n] = cm; return h
        def free(*names):
            for n in names:
                cms.pop(n + "_s").__exit__(None, None, None)

        ocb = sb("ocb", [128, 1], BF16)
        ocf = sb("ocf", [128, 1]); orf = sb("orf", [1, 128])
        eps2 = sb("eps2", [1, 1])
        mnw_c = sb("mnw_c", [128, HKC]); ga_c = sb("ga_c", [128, HKC])
        r2 = sb("r2", [128, HKC * 1024])
        r23 = r2[:].rearrange("p (c t) -> p c t", t=1024)
        h3b = sb("h3b", [128, HKC * 1024], BF16)
        h33 = h3b[:].rearrange("p (c t) -> p c t", t=1024)
        hff = sb("hff", [128, 11 * 1024], BF16)
        hf3 = hff[:].rearrange("p (c t) -> p c t", t=1024)
        PS8 = [ps(f"PS{i}", [128, 512]) for i in range(8)]

        p = _mkp(nc)
        p.op("gpsimd", lambda g: g.memset(ocb[:], 1.0))
        p.op("gpsimd", lambda g: g.memset(ocf[:], 1.0))
        p.op("gpsimd", lambda g: g.memset(orf[:], 1.0))
        p.op("gpsimd", lambda g: g.memset(eps2[:], 1e-6))
        p.dma("sync", lambda s: s.dma_start(out=mnw_c[:], in_=MNW))
        p.dma("sync", lambda s: s.dma_start(out=ga_c[:], in_=GA))
        p.bar()
        ogf = sb("ogf", [128, 32 * 1024], BF16, side="right")
        yg3 = ogf[:].rearrange("p (c t) -> p c t", t=1024)
        wsl = sb("wsl", [128, 32 * 256], BF16, side="right")
        ws3 = wsl[:].rearrange("p (c m) -> p c m", m=256)
        hbuf = sb("hbuf", [128, 2 * 1024], side="right")
        hb3 = hbuf[:].rearrange("p (c t) -> p c t", t=1024)
        p.dma("sync", lambda s: s.dma_start(out=ogf[:], in_=OGF))
        p.bar()
        for ms in range(8):
            p.dma("gpsimd", lambda g, ms=ms: g.dma_start(
                out=wsl[:], in_=OW3[:, :, ms * 256:(ms + 1) * 256]))
            p.dma("sync", lambda s, ms=ms: s.dma_start(
                out=hbuf[:], in_=H3d[:, ms * 2:(ms + 1) * 2, :].rearrange("p c t -> p (c t)")))
            p.bar()
            for mc in range(2):
                for tt in range(8):
                    pu = PS8[mc * 2 + tt // 4][:, (tt % 4) * 128:(tt % 4) * 128 + 128]
                    for kc in range(32):
                        p.op("tensor", lambda t, mc=mc, tt=tt, kc=kc, pu=pu: t.matmul(
                            pu, ws3[:, kc, mc * 128:(mc + 1) * 128],
                            yg3[:, kc, tt * 128:(tt + 1) * 128],
                            start=(kc == 0), stop=(kc == 31)))
            p.bar()
            for mc in range(2):
                for tt in range(8):
                    pu = PS8[mc * 2 + tt // 4][:, (tt % 4) * 128:(tt % 4) * 128 + 128]
                    p.op("vector", lambda v, ms=ms, mc=mc, tt=tt, pu=pu: v.tensor_tensor(
                        r23[:, ms * 2 + mc, tt * 128:(tt + 1) * 128], pu,
                        hb3[:, mc, tt * 128:(tt + 1) * 128], op=ALU.add))
            p.bar()
        free("hbuf", "wsl", "ogf")
        sqb = sb("sqb", [128, HKC * 1024], BF16, side="right")
        sq3 = sqb[:].rearrange("p (c t) -> p c t", t=1024)
        rrow = sb("rrow", [1, 1024], side="right")
        RBr = sb("RBr", [128, 1024], side="right")
        p.op("scalar", lambda s: s.activation(sqb[:], r2[:], AF.Square))
        p.bar()
        for f in range(2):
            for kc in range(HKC):
                p.op("tensor", lambda t, f=f, kc=kc: t.matmul(
                    PS8[0][0:1, 0:512], ocb[:, 0:1], sq3[:, kc, f * 512:(f + 1) * 512],
                    start=(kc == 0), stop=(kc == HKC - 1)))
            p.bar()
            p.op("scalar", lambda s, f=f: s.activation(
                rrow[0:1, f * 512:(f + 1) * 512], PS8[0][0:1, 0:512], AF.Ln,
                scale=1.0 / 2048, bias=eps2[0:1, 0:1]))
            p.bar()
        p.op("scalar", lambda s: s.activation(rrow[:], rrow[:], AF.Exp, scale=-0.5))
        p.bar()
        for f in range(2):
            p.op("tensor", lambda t, f=f: t.matmul(
                PS8[1][:, 0:512], orf[0:1, :], rrow[0:1, f * 512:(f + 1) * 512],
                start=True, stop=True))
            p.bar()
            p.op("vector", lambda v, f=f: v.tensor_copy(
                RBr[:, f * 512:(f + 1) * 512], PS8[1][:, 0:512]))
            p.bar()
        for kc in range(HKC):
            p.op("vector", lambda v, kc=kc: v.scalar_tensor_tensor(
                h33[:, kc, :], r23[:, kc, :], mnw_c[:, kc:kc + 1], RBr[:],
                op0=ALU.mult, op1=ALU.mult))
        p.bar()
        free("RBr", "rrow", "sqb")
        wga = sb("wga", [128, 2 * HKC * 256], BF16, side="right")
        wdw = sb("wdw", [128, 11 * 2048], BF16, side="right")
        wd3 = wdw[:].rearrange("p (c m) -> p c m", m=2048)
        def ld4c(g, par):
            p.dma("gpsimd", lambda g_, g=g, par=par: g_.dma_start(
                out=wga[:, par * HKC * 256:par * HKC * 256 + HKC * 128],
                in_=GAW3[:, :, g * 128:(g + 1) * 128]))
            p.dma("gpsimd", lambda g_, g=g, par=par: g_.dma_start(
                out=wga[:, par * HKC * 256 + HKC * 128:(par + 1) * HKC * 256],
                in_=GAW3[:, :, 5632 + g * 128:5632 + (g + 1) * 128]))
        ld4c(0, 0)
        p.bar()
        for slab in range(4):
            for ic in range(11):
                g = slab * 11 + ic
                par = g % 2
                bs = par * 4
                if ic < 10:
                    ld4c(g + 1, (g + 1) % 2)
                wg4 = wga[:, par * HKC * 256:(par + 1) * HKC * 256].rearrange(
                    "p (u c m) -> p u c m", u=2, m=128)
                for f in range(2):
                    for kc in range(HKC):
                        p.op("tensor", lambda t, f=f, kc=kc, wg4=wg4, bs=bs: t.matmul(
                            PS8[bs + f][:, 0:512], wg4[:, 0, kc, :],
                            h33[:, kc, f * 512:(f + 1) * 512],
                            start=(kc == 0), stop=(kc == HKC - 1)))
                    for kc in range(HKC):
                        p.op("tensor", lambda t, f=f, kc=kc, wg4=wg4, bs=bs: t.matmul(
                            PS8[bs + 2 + f][:, 0:512], wg4[:, 1, kc, :],
                            h33[:, kc, f * 512:(f + 1) * 512],
                            start=(kc == 0), stop=(kc == HKC - 1)))
                p.bar()
                for f in range(2):
                    fs = slice(f * 512, (f + 1) * 512)
                    p.op("scalar", lambda s, ic=ic, f=f, fs=fs, bs=bs: s.activation(
                        hf3[:, ic, fs], PS8[bs + f][:, 0:512], AF.Silu))
                p.bar()
                for f in range(2):
                    fs = slice(f * 512, (f + 1) * 512)
                    p.op("vector", lambda v, ic=ic, f=f, fs=fs, bs=bs: v.tensor_tensor(
                        hf3[:, ic, fs], hf3[:, ic, fs], PS8[bs + 2 + f][:, 0:512], op=ALU.mult))
                p.bar()
            p.dma("gpsimd", lambda g_, slab=slab: g_.dma_start(
                out=wdw[:], in_=DWW3[:, slab * 11:(slab + 1) * 11, :].rearrange("p c m -> p (c m)")))
            p.bar()
            for mo in range(HKC):
                for f in range(2):
                    pu = PS8[4 + (mo * 2 + f) % 4]
                    for kc in range(11):
                        p.op("tensor", lambda t, mo=mo, f=f, kc=kc, pu=pu: t.matmul(
                            pu[:], wd3[:, kc, mo * 128:(mo + 1) * 128],
                            hf3[:, kc, f * 512:(f + 1) * 512],
                            start=(kc == 0), stop=(kc == 10)))
                    if (mo * 2 + f) % 4 == 3:
                        p.bar()
                        if mo == HKC - 1 and f == 1 and slab < 3:
                            ns = (slab + 1) * 11
                            ld4c(ns, ns % 2)
                        for u in range(mo * 2 + f - 3, mo * 2 + f + 1):
                            uo, uf = divmod(u, 2)
                            p.op("vector", lambda v, uo=uo, uf=uf, u=u: v.tensor_tensor(
                                r23[:, uo, uf * 512:(uf + 1) * 512],
                                r23[:, uo, uf * 512:(uf + 1) * 512],
                                PS8[4 + u % 4][:], op=ALU.add))
                        p.bar()
        xbuf = sb("xbuf", [128, 2 * 1024], side="right")
        for kc in range(HKC):
            p.dma("sync", lambda s, kc=kc: s.dma_start(
                out=xbuf[:, kc % 2 * 1024:(kc % 2 + 1) * 1024], in_=X3d[:, kc, :]))
            p.bar()
            p.op("vector", lambda v, kc=kc: v.scalar_tensor_tensor(
                r23[:, kc, :], r23[:, kc, :], ga_c[:, kc:kc + 1],
                xbuf[:, kc % 2 * 1024:(kc % 2 + 1) * 1024], op0=ALU.mult, op1=ALU.add))
            p.bar()
            p.dma("sync", lambda s, kc=kc: s.dma_start(out=OX3[:, kc, :], in_=r23[:, kc, :]))
        p.bar()
        _cleanup(cms)
        _fin(p, ctx)
    return nc


def stage_st4c(inputs, r3, r4a, r4b, mods):
    OW = km(np.asarray(inputs["L1_attn_o_w"]))
    GAW = km(np.asarray(inputs["L1_attn_gate_w"]))
    DWW = km(np.asarray(inputs["L1_attn_down_w"]))
    mnw = col16(np.asarray(inputs["L1_attn_mlp_norm_w"]))
    maps = []
    for core in range(NCORES):
        b, q = divmod(core, 4)
        maps.append(dict(
            ogf=np.asarray(r4b[core]["ogf"]), hres=np.asarray(r4a[core]["hres"]),
            x2=np.asarray(r3[core]["x2"]), ow=OW, mnw=mnw,
            gatew=GAW, downw=DWW, ga1=mods[1][b]["ga"],
        ))
    return maps



def build_st5a():
    nc = _mknc()
    GUP = nc.dram_tensor("gup", [128, 11 * 256], I32, kind="ExternalInput").ap()
    GSC = nc.dram_tensor("gsc", [128, 11 * 16], F16, kind="ExternalInput").ap()
    DPK = nc.dram_tensor("dpk", [128, 2 * 704], I32, kind="ExternalInput").ap()
    DSC = nc.dram_tensor("dsc", [128, 2 * 44], F16, kind="ExternalInput").ap()
    IDB = nc.dram_tensor("identbf", [128, 128], BF16, kind="ExternalInput").ap()
    OWGp = [nc.dram_tensor(f"wgtc{pp}", [128, 4 * 1408], BF16,
                           kind="ExternalOutput").ap() for pp in range(4)]
    OWDp = [nc.dram_tensor(f"wdtc{pp}", [128, 11 * 256], BF16,
                           kind="ExternalOutput").ap() for pp in range(4)]
    GUP3 = GUP.rearrange("p (c m) -> p c m", m=256)
    DPK3 = DPK.rearrange("p (c m) -> p c m", m=704)

    with ExitStack() as ctx:
        cms = {}
        def sb(n, s, d=F32, side=None):
            n = n + "_s"
            cm = nc.sbuf_tensor(n, s, d, side=side)
            h = cm.__enter__(); cms[n] = cm; return h
        def ps(n, s, d=F32):
            n = n + "_s"
            cm = nc.psum_tensor(n, s, d)
            h = cm.__enter__(); cms[n] = cm; return h

        idb = sb("idb", [128, 128], BF16)
        gup = sb("gup", [128, 11 * 256], I32)
        gu3 = gup[:].rearrange("p (c m) -> p c m", m=256)
        gsch = sb("gsch", [128, 11 * 16], F16)
        gsc = sb("gsc", [128, 11 * 16])
        gs3 = gsc[:].rearrange("p (c m) -> p c m", m=16)
        dpk = sb("dpk", [128, 2 * 704], I32)
        dp3 = dpk[:].rearrange("p (c m) -> p c m", m=704)
        dsch = sb("dsch", [128, 2 * 44], F16)
        dsc = sb("dsc", [128, 2 * 44])
        ds3 = dsc[:].rearrange("p (c m) -> p c m", m=44)
        W = 704
        c128 = sb("c128", [128, W], I32)
        m4 = sb("m4", [128, 8 * W], I32)
        m43 = m4[:].rearrange("p (a b) -> p a b", b=W)
        bits = sb("bits", [128, 8 * W], I32)
        bt3 = bits[:].rearrange("p (a b) -> p a b", b=W)
        sgn = sb("sgn", [128, 8 * W], I32)
        sg3 = sgn[:].rearrange("p (a b) -> p a b", b=W)
        fsc = sb("fsc", [128, 8 * W])
        fs3 = fsc[:].rearrange("p (a b) -> p a b", b=W)
        ind = sb("ind", [128, 8 * W])
        in3 = ind[:].rearrange("p (a b) -> p a b", b=W)
        v1b = sb("v1b", [128, 8 * W])
        v13 = v1b[:].rearrange("p (a b) -> p a b", b=W)
        wnat = sb("wnat", [128, 5632])
        wbf = sb("wbf", [128, 5632], BF16)
        slab = sb("slab", [128, 44 * 128], BF16)
        PS8 = [ps(f"PS{i}", [128, 512]) for i in range(8)]

        p = _mkp(nc)
        p.dma("sync", lambda s: s.dma_start(out=idb[:], in_=IDB))
        p.dma("sync", lambda s: s.dma_start(out=gup[:], in_=GUP))
        p.dma("sync", lambda s: s.dma_start(out=gsch[:], in_=GSC))
        p.dma("sync", lambda s: s.dma_start(out=dpk[:], in_=DPK))
        p.dma("sync", lambda s: s.dma_start(out=dsch[:], in_=DSC))
        p.op("gpsimd", lambda g: g.memset(c128[:], 1056964608))
        p.bar()
        p.op("vector", lambda v: v.tensor_copy(gsc[:], gsch[:]))
        p.op("vector", lambda v: v.tensor_copy(dsc[:], dsch[:]))
        p.bar()

        def dequant(pk_ap, width, nh, scol_fn, ngrp, out3, nchunks_t):
            wn = wnat[:].rearrange("(p) (a) -> p a") if False else wnat
            for j in range(8):
                p.op("vector", lambda v, j=j: v.tensor_scalar(
                    m43[:, j, 0:width], pk_ap, 4 * j, 7,
                    op0=ALU.logical_shift_right, op1=ALU.bitwise_and))
                p.op("vector", lambda v, j=j: v.tensor_scalar(
                    sg3[:, j, 0:width], pk_ap, 4 * j + 3, 1,
                    op0=ALU.logical_shift_right, op1=ALU.bitwise_and))
            p.bar()
            for j in range(8):
                p.op("vector", lambda v, j=j: v.tensor_scalar(
                    bt3[:, j, 0:width], m43[:, j, 0:width], 22, None,
                    op0=ALU.logical_shift_left))
                p.op("vector", lambda v, j=j: v.tensor_scalar(
                    sg3[:, j, 0:width], sg3[:, j, 0:width], -2, 1,
                    op0=ALU.mult, op1=ALU.add))
                p.op("vector", lambda v, j=j: v.tensor_scalar(
                    m43[:, j, 0:width], m43[:, j, 0:width], 2, None, op0=ALU.is_lt))
            p.bar()
            for j in range(8):
                p.op("vector", lambda v, j=j: v.tensor_tensor(
                    bt3[:, j, 0:width], bt3[:, j, 0:width], c128[:, 0:width], op=ALU.add))
            p.bar()
            for j in range(8):
                p.op("vector", lambda v, j=j: v.tensor_copy(
                    fs3[:, j, 0:width], sg3[:, j, 0:width]))
                p.op("vector", lambda v, j=j: v.tensor_copy(
                    in3[:, j, 0:width], m43[:, j, 0:width]))
            p.bar()
            for j in range(8):
                p.op("vector", lambda v, j=j: v.scalar_tensor_tensor(
                    v13[:, j, 0:width], in3[:, j, 0:width], 1.0,
                    bt3[:, j, 0:width].bitcast(F32), op0=ALU.add, op1=ALU.mult))
            p.bar()
            for j in range(8):
                p.op("vector", lambda v, j=j: v.tensor_tensor(
                    v13[:, j, 0:width], v13[:, j, 0:width], in3[:, j, 0:width],
                    op=ALU.subtract))
            p.bar()
            import os as _os
            if _os.environ.get("ST5A_NOSTRIDE"):
                for j in range(8):
                    p.op("vector", lambda v, j=j, width=width: v.tensor_tensor(
                        wnat[:, j * width:(j + 1) * width], v13[:, j, 0:width],
                        fs3[:, j, 0:width], op=ALU.mult))
            else:
                for j in range(8):
                    p.op("vector", lambda v, j=j, width=width: v.tensor_tensor(
                        wnat[:, j:width * 8:8], v13[:, j, 0:width], fs3[:, j, 0:width],
                        op=ALU.mult))
            p.bar()
            for g in range(ngrp):
                p.op("vector", lambda v, g=g: v.tensor_scalar(
                    wbf[:, g * 128:(g + 1) * 128], wnat[:, g * 128:(g + 1) * 128],
                    scol_fn(g), None, op0=ALU.mult))
            p.bar()
            for grp in range((ngrp + 7) // 8):
                for i in range(min(8, ngrp - grp * 8)):
                    hc = grp * 8 + i
                    p.op("tensor", lambda t, hc=hc, i=i: t.transpose(
                        PS8[4 + i // 2][:, (i % 2) * 64:(i % 2) * 64 + 64].bitcast(BF16),
                        wbf[:, hc * 128:(hc + 1) * 128], idb[:]))
                p.bar()
                for i in range(min(8, ngrp - grp * 8)):
                    hc = grp * 8 + i
                    p.op("vector", lambda v, hc=hc, i=i: v.tensor_copy(
                        slab[:, hc * 128:(hc + 1) * 128],
                        PS8[4 + i // 2][:, (i % 2) * 64:(i % 2) * 64 + 64].bitcast(BF16)))
                p.bar()

        for oc in range(11):
            dequant(gu3[:, oc, :], 256, 2048, lambda g, oc=oc: gs3[:, oc, g:g + 1], 16,
                    None, 16)
            for pp in range(4):
                p.dma("sync", lambda s, oc=oc, pp=pp: s.dma_start(
                    out=OWGp[pp].rearrange("p (c m) -> p c m", m=1408)[:, :, oc * 128:(oc + 1) * 128],
                    in_=slab[:, pp * 512:(pp + 1) * 512].rearrange("p (a b) -> p a b", b=128)))
            p.bar()
        for hc2 in range(2):
            dequant(dp3[:, hc2, :], 704, 5632, lambda g, hc2=hc2: ds3[:, hc2, g:g + 1], 44,
                    None, 44)
            for pp in range(4):
                p.dma("sync", lambda s, hc2=hc2, pp=pp: s.dma_start(
                    out=OWDp[pp].rearrange("p (c m) -> p c m", m=256)[:, :, hc2 * 128:(hc2 + 1) * 128],
                    in_=slab[:, pp * 11 * 128:(pp + 1) * 11 * 128].rearrange("p (a b) -> p a b", b=128)))
            p.bar()
        _cleanup(cms)
        _fin(p, ctx)
    return nc


def stage_st5a(inputs):
    gup = np.asarray(inputs["L1_ffn_gate_up_packed"])
    gsc = np.asarray(inputs["L1_ffn_gate_up_scales"]).T
    dpk = np.asarray(inputs["L1_ffn_down_packed"])
    dsc = np.asarray(inputs["L1_ffn_down_scales"]).T
    identbf = np.eye(128).astype(BF)
    maps = []
    for core in range(NCORES):
        go = slice(core * 1408, (core + 1) * 1408)
        dh = slice(core * 256, (core + 1) * 256)
        maps.append(dict(
            gup=km(gup[go].astype(np.int32)),
            gsc=km(np.ascontiguousarray(gsc[go])),
            dpk=km(dpk[dh].astype(np.int32)),
            dsc=km(np.ascontiguousarray(dsc[dh])),
            identbf=identbf,
        ))
    return maps



def build_st5b():
    nc = _mknc()
    X = nc.dram_tensor("x3", [128, HKC * 1024], F32, kind="ExternalInput").ap()
    CB = nc.dram_tensor("cb1", [128, HKC], F32, kind="ExternalInput").ap()
    SB_ = nc.dram_tensor("sb1", [128, HKC], F32, kind="ExternalInput").ap()
    GB = nc.dram_tensor("gb1", [128, HKC], F32, kind="ExternalInput").ap()
    WGTp = [nc.dram_tensor(f"wgt_g{pp}", [1024, 4 * 1408], BF16,
                           kind="ExternalInput").ap() for pp in range(4)]
    WDTp = [nc.dram_tensor(f"wdt_g{pp}", [1024, 11 * 256], BF16,
                           kind="ExternalInput").ap() for pp in range(4)]
    OX = nc.dram_tensor("xout", [128, HKC * 1024], F32, kind="ExternalOutput").ap()
    OX3 = OX.rearrange("p (c t) -> p c t", t=1024)
    X3d = X.rearrange("p (c t) -> p c t", t=1024)

    with ExitStack() as ctx:
        cms = {}
        def sb(n, s, d=F32, side=None):
            n = n + "_s"
            cm = nc.sbuf_tensor(n, s, d, side=side)
            h = cm.__enter__(); cms[n] = cm; return h
        def ps(n, s, d=F32):
            n = n + "_s"
            cm = nc.psum_tensor(n, s, d)
            h = cm.__enter__(); cms[n] = cm; return h
        def free(*names):
            for n in names:
                cms.pop(n + "_s").__exit__(None, None, None)

        ocb = sb("ocb", [128, 1], BF16)
        ocf = sb("ocf", [128, 1]); orf = sb("orf", [1, 128])
        epsc = sb("epsc", [1, 1]); eps2 = sb("eps2", [1, 1])
        cb_c = sb("cb_c", [128, HKC]); sb_c = sb("sb_c", [128, HKC])
        gb_c = sb("gb_c", [128, HKC])
        h5 = sb("h5", [128, HKC * 1024], BF16)
        h53 = h5[:].rearrange("p (c t) -> p c t", t=1024)

        p = _mkp(nc)
        p.dma("sync", lambda s: s.dma_start(out=cb_c[:], in_=CB))
        p.dma("sync", lambda s: s.dma_start(out=sb_c[:], in_=SB_))
        p.dma("sync", lambda s: s.dma_start(out=gb_c[:], in_=GB))
        p.op("gpsimd", lambda g: g.memset(ocb[:], 1.0))
        p.op("gpsimd", lambda g: g.memset(ocf[:], 1.0))
        p.op("gpsimd", lambda g: g.memset(orf[:], 1.0))
        p.op("gpsimd", lambda g: g.memset(epsc[:], 1e-5))
        p.op("gpsimd", lambda g: g.memset(eps2[:], 1e-6))
        p.bar()
        p.op("vector", lambda v: v.tensor_scalar(cb_c[:], cb_c[:], 1.0, None, op0=ALU.add))
        p.bar()
        x_sb = sb("x_sb", [128, HKC * 1024], side="right")
        x3 = x_sb[:].rearrange("p (c t) -> p c t", t=1024)
        p.dma("sync", lambda s: s.dma_start(out=x_sb[:], in_=X))
        p.bar()
        emit_ln_mod(p, lambda n, sh, d=F32: sb(n, sh, d, side="right"), ps, nc,
                    x3, h53, cb_c, sb_c, ocb, ocf, orf, HKC, 1024,
                    eps_ap=epsc[0:1, 0:1], tag="b5")
        p.bar()
        free("ln_m2b5", "ln_t1b5", "ln_MRBb5", "ln_RBb5", "ln_rowsb5", "ln_sqb5",
             "x_sb", "ln_rbb5", "ln_ssb5")
        PS8 = [ps(f"PS{i}", [128, 512]) for i in range(8)]
        y5 = sb("y5", [128, HKC * 1024])
        y53 = y5[:].rearrange("p (c t) -> p c t", t=1024)
        hff = sb("hff", [128, 11 * 1024], BF16)
        hf3 = hff[:].rearrange("p (c t) -> p c t", t=1024)
        wga = sb("wga", [128, 2 * HKC * 256], BF16, side="right")
        wdw = sb("wdw", [128, 11 * 2048], BF16, side="right")
        wd3 = wdw[:].rearrange("p (c m) -> p c m", m=2048)
        p.op("gpsimd", lambda g: g.memset(y5[:], 0.0))
        p.bar()
        wgab = [wga[:, 0:HKC * 256], wga[:, HKC * 256:HKC * 512]] if False else None
        def ld5(g, par):
            for u, gg in [(0, g), (1, 44 + g)]:
                rk, oc = divmod(gg, 11)
                for pp in range(4):
                    p.dma("sync", lambda s, u=u, rk=rk, oc=oc, pp=pp, par=par: s.dma_start(
                        out=wga[:, par * 4096 + u * HKC * 128 + pp * 512:
                                par * 4096 + u * HKC * 128 + (pp + 1) * 512],
                        in_=WGTp[pp][rk * 128:(rk + 1) * 128, :]
                        .rearrange("p (c m) -> p c m", m=1408)[:, :, oc * 128:(oc + 1) * 128]))
        ld5(0, 0)
        p.bar()
        for slab in range(4):
            for ic in range(11):
                g = slab * 11 + ic
                par = g % 2
                bs = par * 4
                if ic < 10:
                    ld5(g + 1, (g + 1) % 2)
                wg4p = wga[:, par * 4096:(par + 1) * 4096].rearrange(
                    "p (u c m) -> p u c m", u=2, m=128)
                for f in range(2):
                    for kc in range(HKC):
                        p.op("tensor", lambda t, f=f, kc=kc, wg4p=wg4p, bs=bs: t.matmul(
                            PS8[bs + f][:, 0:512], wg4p[:, 0, kc, :],
                            h53[:, kc, f * 512:(f + 1) * 512],
                            start=(kc == 0), stop=(kc == HKC - 1)))
                    for kc in range(HKC):
                        p.op("tensor", lambda t, f=f, kc=kc, wg4p=wg4p, bs=bs: t.matmul(
                            PS8[bs + 2 + f][:, 0:512], wg4p[:, 1, kc, :],
                            h53[:, kc, f * 512:(f + 1) * 512],
                            start=(kc == 0), stop=(kc == HKC - 1)))
                p.bar()
                for f in range(2):
                    fs = slice(f * 512, (f + 1) * 512)
                    p.op("scalar", lambda s, ic=ic, fs=fs, f=f, bs=bs: s.activation(
                        hf3[:, ic, fs], PS8[bs + f][:, 0:512], AF.Silu))
                p.bar()
                for f in range(2):
                    fs = slice(f * 512, (f + 1) * 512)
                    p.op("vector", lambda v, ic=ic, fs=fs, f=f, bs=bs: v.tensor_tensor(
                        hf3[:, ic, fs], hf3[:, ic, fs], PS8[bs + 2 + f][:, 0:512], op=ALU.mult))
                p.bar()
            for rk in range(8):
                p.dma("sync", lambda s, slab=slab, rk=rk: s.dma_start(
                    out=wd3[:, :, rk * 256:(rk + 1) * 256],
                    in_=WDTp[slab][rk * 128:(rk + 1) * 128, :]
                    .rearrange("p (c m) -> p c m", m=256)))
            p.bar()
            for mo in range(HKC):
                for f in range(2):
                    pu = PS8[4 + (mo * 2 + f) % 4]
                    for kc in range(11):
                        p.op("tensor", lambda t, mo=mo, f=f, kc=kc, pu=pu: t.matmul(
                            pu[:], wd3[:, kc, mo * 128:(mo + 1) * 128],
                            hf3[:, kc, f * 512:(f + 1) * 512],
                            start=(kc == 0), stop=(kc == 10)))
                    if (mo * 2 + f) % 4 == 3:
                        p.bar()
                        if mo == HKC - 1 and f == 1 and slab < 3:
                            ld5((slab + 1) * 11, ((slab + 1) * 11) % 2)
                        for u in range(mo * 2 + f - 3, mo * 2 + f + 1):
                            uo, uf = divmod(u, 2)
                            p.op("vector", lambda v, uo=uo, uf=uf, u=u: v.tensor_tensor(
                                y53[:, uo, uf * 512:(uf + 1) * 512],
                                y53[:, uo, uf * 512:(uf + 1) * 512],
                                PS8[4 + u % 4][:], op=ALU.add))
                        p.bar()
        free("wdw", "wga")
        sqb = sb("sqb", [128, HKC * 1024], BF16, side="right")
        sq3 = sqb[:].rearrange("p (c t) -> p c t", t=1024)
        rrow = sb("rrow", [1, 1024], side="right")
        RBr = sb("RBr", [128, 1024], side="right")
        xbuf = sb("xbuf", [128, 2 * 1024], side="right")
        p.op("scalar", lambda s: s.activation(sqb[:], y5[:], AF.Square))
        p.bar()
        for f in range(2):
            for kc in range(HKC):
                p.op("tensor", lambda t, f=f, kc=kc: t.matmul(
                    PS8[0][0:1, 0:512], ocb[:, 0:1], sq3[:, kc, f * 512:(f + 1) * 512],
                    start=(kc == 0), stop=(kc == HKC - 1)))
            p.bar()
            p.op("scalar", lambda s, f=f: s.activation(
                rrow[0:1, f * 512:(f + 1) * 512], PS8[0][0:1, 0:512], AF.Ln,
                scale=1.0 / 2048, bias=eps2[0:1, 0:1]))
            p.bar()
        p.op("scalar", lambda s: s.activation(rrow[:], rrow[:], AF.Exp, scale=-0.5))
        p.bar()
        for f in range(2):
            p.op("tensor", lambda t, f=f: t.matmul(
                PS8[1][:, 0:512], orf[0:1, :], rrow[0:1, f * 512:(f + 1) * 512],
                start=True, stop=True))
            p.bar()
            p.op("vector", lambda v, f=f: v.tensor_copy(
                RBr[:, f * 512:(f + 1) * 512], PS8[1][:, 0:512]))
            p.bar()
        for kc in range(HKC):
            p.dma("sync", lambda s, kc=kc: s.dma_start(
                out=xbuf[:, kc % 2 * 1024:(kc % 2 + 1) * 1024], in_=X3d[:, kc, :]))
            p.bar()
            p.op("vector", lambda v, kc=kc: v.tensor_tensor(
                y53[:, kc, :], y53[:, kc, :], RBr[:], op=ALU.mult))
            p.bar()
            p.op("vector", lambda v, kc=kc: v.scalar_tensor_tensor(
                y53[:, kc, :], y53[:, kc, :], gb_c[:, kc:kc + 1],
                xbuf[:, kc % 2 * 1024:(kc % 2 + 1) * 1024], op0=ALU.mult, op1=ALU.add))
            p.bar()
            p.dma("sync", lambda s, kc=kc: s.dma_start(out=OX3[:, kc, :], in_=y53[:, kc, :]))
        p.bar()
        _cleanup(cms)
        _fin(p, ctx)
    return nc


def stage_st5b(inputs, r4c, r5a, mods):
    d = {}
    for pp in range(4):
        d[f"wgt_g{pp}"] = np.concatenate(
            [np.asarray(r5a[c][f"wgtc{pp}"]) for c in range(8)], axis=0)
        d[f"wdt_g{pp}"] = np.concatenate(
            [np.asarray(r5a[c][f"wdtc{pp}"]) for c in range(8)], axis=0)
    maps = []
    for core in range(NCORES):
        b, q = divmod(core, 4)
        maps.append(dict(
            x3=np.asarray(r4c[core]["x3"]),
            cb1=mods[1][b]["cb"], sb1=mods[1][b]["sb"], gb1=mods[1][b]["gb"],
            **d,
        ))
    return maps


RG8 = [list(range(8))]


def _cc_allgather(p, in_h, out_h):
    p.bar()
    p.op("gpsimd", lambda g, in_h=in_h, out_h=out_h: g.collective_compute(
        "AllGather", ALU.bypass, replica_groups=RG8,
        ins=[in_h.ap().opt()], outs=[out_h.ap().opt()]))
    p.bar()


def _emit_glue_mods():
    px, p = _FUSE.nc_proxy, _FUSE.p
    px.stage = "glmod"
    nc = px
    OH16 = nc.dram_tensor("oh16", [16, 8], F32, kind="ExternalInput").ap()
    IDF = nc.dram_tensor("identf", [128, 128], F32, kind="ExternalInput").ap()
    m_in = [px.registry["mod0"], px.registry["mod1"]]
    m_g = [nc.dram_tensor(f"mod{l}_g", [16, 1536], F32, addr_space="Shared")
           for l in range(2)]
    mrow_d = nc.dram_tensor("mrow_d", [1, 12288], F32)
    names = [["sa0", "ca0", "ga0", "sb0", "cb0", "gb0"],
             ["sa1", "ca1", "ga1", "sb1", "cb1", "gb1"]]
    outs = [[nc.dram_tensor(n, [128, HKC], F32) for n in names[l]] for l in range(2)]
    with ExitStack() as ctx:
        sbuf = lambda n, sh, d=F32: ctx.enter_context(nc.sbuf_tensor(n, sh, d))
        psum = lambda n, sh, d=F32: ctx.enter_context(nc.psum_tensor(n, sh, d))
        oh16 = sbuf("oh16", [16, 8])
        idf = sbuf("idf", [128, 128])
        gsb = sbuf("gsb", [16, 1536])
        mrow = sbuf("mrow", [1, 12288])
        msb = sbuf("msb", [96, 128])
        modc = sbuf("modc", [128, 96])
        PSg = psum("PSg", [128, 1536])
        p.bar()
        p.dma("sync", lambda s: s.dma_start(out=oh16[:], in_=OH16))
        p.dma("sync", lambda s: s.dma_start(out=idf[:], in_=IDF))
        for l in range(2):
            _cc_allgather(p, m_in[l], m_g[l])
        p.bar()
        for l in range(2):
            p.dma("sync", lambda s, l=l: s.dma_start(out=gsb[:], in_=m_g[l].ap()))
            p.bar()
            for j in range(8):
                for f in range(3):
                    p.op("tensor", lambda t, j=j, f=f: t.matmul(
                        PSg[0:1, f * 512:(f + 1) * 512], oh16[:, j:j + 1],
                        gsb[:, f * 512:(f + 1) * 512], start=True, stop=True))
                p.bar()
                for f in range(3):
                    p.op("vector", lambda v, j=j, f=f: v.tensor_copy(
                        mrow[:, j * 1536 + f * 512:j * 1536 + (f + 1) * 512],
                        PSg[0:1, f * 512:(f + 1) * 512]))
                p.bar()
            p.dma("sync", lambda s: s.dma_start(out=mrow_d.ap(), in_=mrow[:]))
            p.bar()
            p.dma("sync", lambda s: s.dma_start(
                out=msb[:], in_=mrow_d.ap().rearrange("x (a b) -> (x a) b", b=128)))
            p.bar()
            p.op("tensor", lambda t: t.transpose(PSg[:, 0:96], msb[:], idf[0:96, 0:96]))
            p.bar()
            p.op("vector", lambda v: v.tensor_copy(modc[:], PSg[:, 0:96]))
            p.bar()
            for v_ in range(6):
                p.dma("sync", lambda s, l=l, v_=v_: s.dma_start(
                    out=outs[l][v_].ap(), in_=modc[:, v_ * 16:(v_ + 1) * 16]))
            p.bar()


def _emit_glue_mamba():
    px, p = _FUSE.nc_proxy, _FUSE.p
    px.stage = "glmam"
    nc = px
    sout = px.registry["sout"]
    alast = px.registry["alast"]
    sout_g = nc.dram_tensor("sout_g", [1024, 4096], BF16, addr_space="Shared")
    alast_g = nc.dram_tensor("alast_g", [32, 64], F32, addr_space="Shared")
    alastp = nc.dram_tensor("alastp", [64, 32], F32)
    alasto = nc.dram_tensor("alasto", [64, 4], F32)
    IDF = nc.dram_tensor("identf", [128, 128], F32, kind="ExternalInput").ap()
    with ExitStack() as ctx:
        sbuf = lambda n, sh, d=F32: ctx.enter_context(nc.sbuf_tensor(n, sh, d))
        psum = lambda n, sh, d=F32: ctx.enter_context(nc.psum_tensor(n, sh, d))
        idf = sbuf("idf", [128, 128])
        ag = sbuf("ag", [32, 64])
        al = sbuf("al", [4, 64])
        tp = sbuf("tp", [64, 32])
        to = sbuf("to", [64, 4])
        PSg = psum("PSg", [128, 512])
        p.bar()
        p.dma("sync", lambda s: s.dma_start(out=idf[:], in_=IDF))
        _cc_allgather(p, sout, sout_g)
        _cc_allgather(p, alast, alast_g)
        p.bar()
        p.dma("sync", lambda s: s.dma_start(out=ag[:], in_=alast_g.ap()))
        p.dma("sync", lambda s: s.dma_start(out=al[:], in_=alast.ap()))
        p.bar()
        p.op("tensor", lambda t: t.transpose(PSg[0:64, 0:32], ag[:], idf[0:32, 0:32]))
        p.op("tensor", lambda t: t.transpose(PSg[0:64, 64:68], al[:], idf[0:4, 0:4]))
        p.bar()
        p.op("vector", lambda v: v.tensor_copy(tp[:], PSg[0:64, 0:32]))
        p.op("vector", lambda v: v.tensor_copy(to[:], PSg[0:64, 64:68]))
        p.bar()
        p.dma("sync", lambda s: s.dma_start(out=alastp.ap(), in_=tp[:]))
        p.dma("sync", lambda s: s.dma_start(out=alasto.ap(), in_=to[:]))
        p.bar()


def _emit_glue_ret():
    px, p = _FUSE.nc_proxy, _FUSE.p
    px.stage = "glret"
    nc = px
    soutr = px.registry["soutr"]
    soutr_g = nc.dram_tensor("soutr_g", [1024, 8192], BF16, addr_space="Shared")
    p.bar()
    _cc_allgather(p, soutr, soutr_g)
    p.bar()


def _emit_glue_fp4():
    px, p = _FUSE.nc_proxy, _FUSE.p
    px.stage = "glfp4"
    nc = px
    import os as _os
    ngw = int(_os.environ.get("FP4_GW", "4"))
    ngd = int(_os.environ.get("FP4_GD", "4"))
    p.bar()
    for pp in range(4):
        wgt_gp = nc.dram_tensor(f"wgt_g{pp}", [1024, 4 * 1408], BF16)
        if pp < ngw:
            _cc_allgather(p, px.registry[f"wgtc{pp}"], wgt_gp)
        wdt_gp = nc.dram_tensor(f"wdt_g{pp}", [1024, 11 * 256], BF16)
        if pp < ngd:
            _cc_allgather(p, px.registry[f"wdtc{pp}"], wdt_gp)
    p.bar()


FUSE_STAGES = ["st0", "glmods", "st1", "st2a", "glmam", "st2b", "st2c", "st3",
               "st4a", "glret", "st4b", "st4c", "st5a", "glfp4", "st5b"]


def _emit_glue_rd():
    px, p = _FUSE.nc_proxy, _FUSE.p
    px.stage = "glrd"
    nc = px
    wgt_g = px.registry["wgt_g0"]
    wdt_g = px.registry["wdt_g0"]
    xout = px.registry.get("xout")
    if xout is None:
        xout = nc.dram_tensor("xout", [128, HKC * 1024], F32, kind="ExternalOutput")
    with ExitStack() as ctx:
        t1 = ctx.enter_context(nc.sbuf_tensor("t1", [128, 512], BF16))
        t2 = ctx.enter_context(nc.sbuf_tensor("t2", [128, 512], F32))
        p.bar()
        p.dma("sync", lambda s: s.dma_start(out=t1[:, 0:256],
                                            in_=wgt_g.ap()[300:428, 0:256]))
        p.dma("sync", lambda s: s.dma_start(out=t1[:, 256:512],
                                            in_=wdt_g.ap()[300:428, 0:256]))
        p.bar()
        p.op("vector", lambda v: v.tensor_copy(t2[:], t1[:]))
        p.bar()
        p.dma("sync", lambda s: s.dma_start(out=xout.ap()[:, 0:512], in_=t2[:]))
        p.bar()


def _emit_glue_rd2():
    px, p = _FUSE.nc_proxy, _FUSE.p
    px.stage = "glrd2"
    nc = px
    wdtc0 = px.registry["wdtc0"]
    xout = px.registry.get("xout")
    if xout is None:
        xout = nc.dram_tensor("xout", [128, HKC * 1024], F32, kind="ExternalOutput")
    with ExitStack() as ctx:
        t1 = ctx.enter_context(nc.sbuf_tensor("t1", [128, 512], BF16))
        t2 = ctx.enter_context(nc.sbuf_tensor("t2", [128, 512], F32))
        p.bar()
        p.dma("sync", lambda s: s.dma_start(out=t1[:], in_=wdtc0.ap()[:, 0:512]))
        p.bar()
        p.op("vector", lambda v: v.tensor_copy(t2[:], t1[:]))
        p.bar()
        p.dma("sync", lambda s: s.dma_start(out=xout.ap()[:, 0:512], in_=t2[:]))
        p.bar()


def build_fused(stages=None):
    global _FUSE
    import os
    if stages is None:
        env = os.environ.get("FUSE_STAGES", "")
        stages = env.split(",") if env else FUSE_STAGES
    _FUSE = _Fuse()
    px, p = _FUSE.nc_proxy, _FUSE.p
    emitters = {
        "st0": lambda: build_st0(), "glmods": _emit_glue_mods,
        "st1": lambda: build_st1(), "st2a": lambda: build_st2a(),
        "glmam": _emit_glue_mamba, "st2b": lambda: build_st2b(),
        "st2c": lambda: build_st2c(), "st3": lambda: build_st3(),
        "st4a": lambda: build_st4a(), "glret": _emit_glue_ret,
        "st4b": lambda: build_st4b(), "st4c": lambda: build_st4c(),
        "st5a": lambda: build_st5a(), "glfp4": _emit_glue_fp4,
        "st5b": lambda: build_st5b(), "glrd": _emit_glue_rd,
        "glrd2": _emit_glue_rd2,
    }
    try:
        for st in stages:
            px.stage = st
            emitters[st]()
        with ExitStack() as ctx:
            p.finalize(ctx)
        nc = _FUSE.real_nc
    finally:
        _FUSE = None
    return nc


def fused_in_maps(inputs):
    maps0 = stage_st0(inputs)
    maps1 = stage_st1(inputs, None)
    maps2a = stage_st2a(inputs, None)
    maps5a = stage_st5a(inputs)
    nw = np.asarray(inputs["L0_attn_norm_weight"]).reshape(1, 4096)
    oh8 = np.zeros((8, 8 * 128), np.float32)
    for j in range(8):
        oh8[j, j * 128:(j + 1) * 128] = 1.0
    oh8b = oh8.astype(BF)
    oh4 = np.zeros((4, 4 * 128), np.float32)
    for c in range(4):
        oh4[c, c * 128:(c + 1) * 128] = 1.0
    oh4b = oh4.astype(BF)
    identf = np.eye(128, dtype=np.float32)
    identbf = np.eye(128).astype(BF)
    x = np.asarray(inputs["L0_attn_hidden_states"])
    W_out = km(np.asarray(inputs["L0_attn_out_proj_w"]))
    wg = np.asarray(inputs["L0_ffn_w_gate"]); wu = np.asarray(inputs["L0_ffn_w_up"])
    wd = np.asarray(inputs["L0_ffn_w_down"])
    WG = np.concatenate([km(wg[e].T) for e in range(8)], axis=1)
    WU = np.concatenate([km(wu[e].T) for e in range(8)], axis=1)
    WD = np.concatenate([km(wd[e].T) for e in range(8)], axis=1)
    SHG = km(np.asarray(inputs["L0_ffn_sh_gate"]).T)
    SHU = km(np.asarray(inputs["L0_ffn_sh_up"]).T)
    SHD = km(np.asarray(inputs["L0_ffn_sh_down"]).T)
    RW = km(np.asarray(inputs["L0_ffn_gate_w"]).T)
    SEG = km(np.asarray(inputs["L0_ffn_se_gate_w"]).T)
    m4a = stage_st4a(inputs, None, None)
    gnw = np.asarray(inputs["L1_attn_gnorm_w"]).reshape(1, 512)
    NH = 8
    gamma = 1.0 - np.exp2(-5.0 - np.arange(NH, dtype=np.float64))
    OW = km(np.asarray(inputs["L1_attn_o_w"]))
    GAW = km(np.asarray(inputs["L1_attn_gate_w"]))
    DWW = km(np.asarray(inputs["L1_attn_down_w"]))
    mnw = col16(np.asarray(inputs["L1_attn_mlp_norm_w"]))
    maps = []
    for core in range(NCORES):
        b, q = divmod(core, 4)
        oh16 = np.zeros((16, 8), np.float32)
        for j in range(8):
            oh16[2 * j + b, j] = 1.0
        selm = np.zeros((8, 8), np.float32)
        selv = np.zeros((8, 1), np.float32)
        for j in range(4 * b, 4 * b + q):
            selv[j] = 1.0
            for m_ in range(j + 1, 4 * b + q):
                selm[m_, j] = 1.0
        wc = np.zeros((8, 8), np.float64)
        for j in range(4 * b, 4 * b + q):
            steps = q - 1 - (j - 4 * b)
            wc[j] = gamma ** (1024.0 * steps)
        wcols = np.zeros((128, 64), np.float32)
        for j in range(8):
            wcols[:, j * 8:(j + 1) * 8] = wc[j][None, :]
        t0 = q * TPC
        xf = np.ascontiguousarray(x[b, t0:t0 + TPC].T)
        d = {}
        d.update(maps0[core])
        d.update({k: v for k, v in maps1[core].items()
                  if k not in ("ca0", "sa0")})
        d.update({k: v for k, v in maps2a[core].items()
                  if k not in ("xtok", "bfeat", "cfeat", "btok", "dtp", "adt")})
        d.update(maps5a[core])
        d.update(dict(
            oh16=oh16, selm=selm, selv=selv, nw=nw, oh8=oh8b,
            xmod=km(xf), w_out=W_out,
            routw=RW, wg=WG, wu=WU, wd=WD, shg=SHG, shu=SHU, shd=SHD, seg=SEG,
            oh8f=oh8, wcols=wcols, gnw=gnw,
            ow=OW, mnw=mnw, gatew=GAW, downw=DWW,
        ))
        d.update({k: v for k, v in m4a[core].items()
                  if k not in ("x2", "ca1", "sa1")})
        maps.append(d)
    return maps


_BUILT = {}


def _nc(name):
    if name not in _BUILT:
        _BUILT[name] = globals()["build_" + name]()
    return _BUILT[name]


def _run(name, maps):
    res = run_bass_kernel_spmd(_nc(name), maps, core_ids=list(range(NCORES)))
    return res.results


def kernel(**inputs):
    if "fused" not in _BUILT:
        _BUILT["fused"] = build_fused()
    maps = fused_in_maps(inputs)
    res = run_bass_kernel_spmd(_BUILT["fused"], maps, core_ids=list(range(NCORES)))
    r = res.results
    out = np.zeros((2, 4096, 2048), np.float32)
    for core in range(NCORES):
        b, q = divmod(core, 4)
        t0 = q * TPC
        out[b, t0:t0 + TPC] = (np.asarray(r[core]["xout"]).reshape(128, HKC, TPC)
                               .transpose(1, 0, 2).reshape(HID, TPC).T)
    return out
